# version 02
# baseline (speedup 1.0000x reference)
import numpy as np
from contextlib import ExitStack

import concourse.bass as bass
import concourse.mybir as mybir
from concourse.bass_utils import run_bass_kernel_spmd

F32 = mybir.dt.float32
BF16 = mybir.dt.bfloat16
AF = mybir.ActivationFunctionType
ALU = mybir.AluOpType
AX = mybir.AxisListType
f32 = np.float32

NCORES = 8
D = 1024
KC = 8
HID = 1408
NH_A, NT_A = 640, 5
NH_B, NT_B = 768, 6
NE = 9
SEQ = 2048
NB = 2
T = NB * SEQ
NTILES = T // 128
HALF = 2048
HTILES = HALF // 128
CHUNK = 512
HCHUNKS = HALF // CHUNK


class Sem:
    __slots__ = ("name", "v", "h")

    def __init__(self, name):
        self.name, self.v, self.h = name, 0, None


class Plan:
    def __init__(self):
        self.ops = []

    def emit(self, engine, fn, waits=(), incs=()):
        ws = [(s, v) for (s, v) in waits if v > 0]
        ri = []
        for (s, k) in incs:
            s.v += k
            ri.append((s, k))
        self.ops.append((engine, ws, fn, ri))

    def run(self, engine_name, eng):
        for e, ws, fn, ri in self.ops:
            if e != engine_name:
                continue
            merged = {}
            for (s, v) in ws:
                if s.name in merged:
                    merged[s.name] = (s, max(v, merged[s.name][1]))
                else:
                    merged[s.name] = (s, v)
            for (s, v) in merged.values():
                eng.wait_ge(s.h, v)
            inst = fn(eng)
            for (s, k) in ri:
                inst.then_inc(s.h, k)


def build_nc(phases=("F1", "A", "F2"), dbg=False):
    nc = bass.Bass("TRN2", target_bir_lowering=False, debug=False)
    dbg_d = {}
    if dbg:
        dbg_d["yTh"] = nc.dram_tensor("dbg_yTh", [128, KC * HALF], F32, kind="ExternalOutput").ap()
        dbg_d["mTc"] = nc.dram_tensor("dbg_mTc", [128, 128], F32, kind="ExternalOutput").ap()
        dbg_d["dacc"] = nc.dram_tensor("dbg_dacc", [128, HTILES * D], F32, kind="ExternalOutput").ap()
        dbg_d["ye"] = nc.dram_tensor("dbg_ye", [128, KC * CHUNK], F32, kind="ExternalOutput").ap()
        dbg_d["xs0"] = nc.dram_tensor("dbg_xs0", [HALF, D], F32, kind="ExternalOutput").ap()
        dbg_d["dacc2"] = nc.dram_tensor("dbg_dacc2", [128, HTILES * D], F32, kind="ExternalOutput").ap()

    x_d = nc.dram_tensor("x", [T, D], F32, kind="ExternalInput").ap()
    wqkvT_d = nc.dram_tensor("wqkvT", [D, 192], F32, kind="ExternalInput").ap()
    woutT_d = nc.dram_tensor("woutT", [64, D], F32, kind="ExternalInput").ap()
    gwT_d = nc.dram_tensor("gwT", [D, 8], F32, kind="ExternalInput").ap()
    biasbc_d = nc.dram_tensor("biasbc", [128, 8], F32, kind="ExternalInput").ap()
    upT_d = nc.dram_tensor("upT", [NE, D, HID], F32, kind="ExternalInput").ap()
    dnT_d = nc.dram_tensor("dnT", [NE, HID, D], F32, kind="ExternalInput").ap()
    out_d = nc.dram_tensor("out", [T, D], F32, kind="ExternalOutput").ap()
    xs_d = nc.dram_tensor("xs", [T, D], F32, kind="Internal").ap()

    P = Plan()
    dInit = Sem("dInit")
    dWA = [Sem("dWA0"), Sem("dWA1")]
    dWB = Sem("dWB")
    dXs = [Sem("dX0"), Sem("dX1")]
    dYs = [Sem("dY0"), Sem("dY1")]
    dM = [Sem("dM0"), Sem("dM1")]
    dR = [Sem("dR0"), Sem("dR1")]
    dG = [Sem(f"dG{i}") for i in range(4)]
    dDbg = Sem("dDbg")
    pe, act, vec, gp, bar = Sem("pe"), Sem("act"), Sem("vec"), Sem("gp"), Sem("bar")
    all_sems = [dInit, dWA[0], dWA[1], dWB, dXs[0], dXs[1], dYs[0], dYs[1],
                dM[0], dM[1], dR[0], dR[1]] + dG + [dDbg, pe, act, vec, gp, bar]

    def now(s):
        return (s, s.v)

    with ExitStack() as top:
        sb = lambda n_, sh, dt: top.enter_context(nc.sbuf_tensor(n_, sh, dt))
        ident = sb("ident", [128, 128], F32)
        ones1 = sb("ones1", [1, 128], BF16)
        iota8 = sb("iota8", [128, 8], F32)
        biasb = sb("biasb", [128, 8], F32)
        gw_sb = sb("gw_sb", [128, KC * 8], F32)
        xr = [sb(f"xr{i}", [128, D], F32) for i in range(2)]
        ybuf = [sb(f"ybuf{i}", [128, D], F32) for i in range(2)]
        stt_ = [sb(f"stt{i}", [128, 8], F32) for i in range(2)]
        PBbig = top.enter_context(nc.psum_tensor("PBbig", [128, 2048], F32))
        PB = [PBbig[:, 512 * i:512 * (i + 1)] for i in range(4)] + \
             [top.enter_context(nc.psum_tensor(f"PB{i}", [128, 512], F32)) for i in range(4, 8)]

        P.emit("gpsimd", lambda g: g.iota(ident[:, :], pattern=[[1, 128]], base=0,
                                          channel_multiplier=-1, allow_small_or_imprecise_dtypes=True),
               incs=[(gp, 1)])
        P.emit("gpsimd", lambda g: g.iota(iota8[:, :], pattern=[[1, 8]], base=0,
                                          channel_multiplier=0, allow_small_or_imprecise_dtypes=True),
               incs=[(gp, 1)])
        P.emit("gpsimd", lambda g: g.memset(ones1[:, :], 1.0), incs=[(gp, 1)])
        P.emit("gpsimd", lambda g: g.dma_start(out=biasb[:, :], in_=biasbc_d[:, :]), incs=[(dInit, 16)])
        P.emit("gpsimd", lambda g: g.dma_start(
            out=gw_sb[:, :].rearrange("p (k e) -> p k e", k=KC),
            in_=gwT_d[:, :].rearrange("(k p) e -> p k e", p=128)), incs=[(dInit, 16)])
        init_gp = now(gp)
        init_dW = now(dInit)
        P.emit("vector", lambda v: v.tensor_single_scalar(ident[:, :], ident[:, :], 0.0, ALU.is_equal),
               waits=[init_gp], incs=[(vec, 1)])
        ident_tok = now(vec)

        def engine_barrier():
            target = bar.v + 5
            dma_fence = {"sync": [now(q) for q in dXs + dYs],
                         "gpsimd": [now(dInit), now(dWA[0]), now(dWA[1]), now(dWB), now(dDbg)],
                         "scalar": [now(q) for q in dM + dR + dG]}
            for e in ("tensor", "vector", "scalar", "gpsimd", "sync"):
                P.emit(e, lambda eng: eng.nop(), waits=dma_fence.get(e, []), incs=[(bar, 1)])
            for e in ("tensor", "vector", "scalar", "gpsimd", "sync"):
                P.emit(e, lambda eng: eng.nop(), waits=[(bar, target)])

        def emit_ln_tile(war, src_dram, ti, r):
            st = stt_[r]
            s1, s2 = st[:, 0:1], st[:, 1:2]
            mu, var = st[:, 2:3], st[:, 3:4]
            rstd, nm = st[:, 4:5], st[:, 5:6]
            P.emit("sync", lambda sy: sy.dma_start(out=xr[r][:, :],
                                                   in_=src_dram[ti * 128:(ti + 1) * 128, :]),
                   waits=war.pop(("xr", r), []), incs=[(dXs[r], 16)])
            ld = now(dXs[r])
            stw = war.pop(("st", r), [])
            ybw = war.pop(("ybuf", r), [])
            P.emit("vector", lambda v: v.reduce_sum(s1, xr[r][:, :], axis=AX.X),
                   waits=[ld] + stw, incs=[(vec, 1)])
            s1_tok = now(vec)
            P.emit("scalar", lambda s: s.activation(ybuf[r][:, :], xr[r][:, :], AF.Square,
                                                    accum_out=s2),
                   waits=[ld] + stw + ybw, incs=[(act, 1)])
            s2_tok = now(act)
            P.emit("vector", lambda v: v.tensor_scalar_mul(mu, s1, 1.0 / D),
                   waits=[s1_tok], incs=[(vec, 1)])
            mu_tok = now(vec)
            P.emit("vector", lambda v: v.tensor_scalar_mul(var, s2, 1.0 / D),
                   waits=[s2_tok], incs=[(vec, 1)])
            P.emit("vector", lambda v: v.tensor_mul(s1, mu, mu), waits=[mu_tok], incs=[(vec, 1)])
            P.emit("vector", lambda v: v.tensor_sub(var, var, s1), waits=[now(vec)],
                   incs=[(vec, 1)])
            var_tok = now(vec)
            P.emit("scalar", lambda s: s.activation(rstd, var, AF.Sqrt, bias=f32(1e-5)),
                   waits=[var_tok], incs=[(act, 1)])
            sqr_tok = now(act)
            P.emit("vector", lambda v: v.reciprocal(rstd, rstd), waits=[sqr_tok], incs=[(vec, 1)])
            rstd_tok = now(vec)
            P.emit("vector", lambda v: v.scalar_tensor_tensor(nm, mu, -1.0, rstd, ALU.mult, ALU.mult),
                   waits=[rstd_tok], incs=[(vec, 1)])
            nm_tok = now(vec)
            P.emit("scalar", lambda s: s.activation(ybuf[r][:, :], xr[r][:, :], AF.Identity,
                                                    bias=nm, scale=rstd),
                   waits=[rstd_tok, nm_tok], incs=[(act, 1)])
            y_tok = now(act)
            war[("st", r)] = [y_tok]
            return {"ld": ld, "y": y_tok}

        def ffn_phase(src_dram, dst_dram, tag):
            with ExitStack() as ph:
                psb = lambda n_, sh, dt: ph.enter_context(nc.sbuf_tensor(n_ + tag, sh, dt))
                yTh = psb("yTh", [128, KC * HALF], BF16)
                mmF = psb("mmF", [128, 128], F32)
                mTcF = psb("mTcF", [128, 128], BF16)
                m1r = [psb(f"m1r{i}", [1, CHUNK], BF16) for i in range(2)]
                sc_t = psb("sc_t", [128, 8], F32)
                bb_t = psb("bb_t", [128, 8], F32)
                t1_t = psb("t1_t", [128, 8], F32)
                oh_t = psb("oh_t", [128, 8], F32)
                v1_t = psb("v1_t", [128, 1], F32)
                i1_t = psb("i1_t", [128, 1], F32)
                WupA = [psb(f"WupA{i}", [128, KC * NH_A], BF16) for i in range(2)]
                WdnA = [psb(f"WdnA{i}", [128, NT_A * D], BF16) for i in range(2)]
                WupB = psb("WupB", [128, KC * NH_B], BF16)
                WdnB = psb("WdnB", [128, NT_B * D], BF16)
                ye = [psb(f"ye{i}", [128, KC * CHUNK], BF16) for i in range(2)]
                hh = [psb("hhA", [128, NT_A * CHUNK], BF16), psb("hhB", [128, NT_B * CHUNK], BF16)]
                dacc = psb("dacc", [128, HTILES * D], F32)
                yTf32 = dacc

                U = [PB[0], PB[1]]
                DD = [PB[2], PB[3]]
                MB = PB[4]
                TPy = [PB[5][:, 0:128], PB[6][:, 0:128]]
                G8 = PB[7][:, 0:8]
                TPm = PB[4][0:128, 0:128]
                mbs = ybuf[0][:, 0:CHUNK]

                war = {}

                def load_piece(e, piece, waits_):
                    if piece == 0:
                        n0, nt, wu, wd = 0, NT_A, WupA[e % 2], WdnA[e % 2]
                        qsem = dWA[e % 2]
                    else:
                        n0, nt, wu, wd = NH_A, NT_B, WupB, WdnB
                        qsem = dWB
                    P.emit("gpsimd", lambda g, e=e, n0=n0, nt=nt, wu=wu: g.dma_start(
                        out=wu[:, 0:KC * nt * 128].rearrange("p (k n) -> p k n", k=KC),
                        in_=upT_d[e:e + 1, :, n0:n0 + nt * 128].rearrange("o (k p) n -> p (o k) n", p=128)),
                        waits=waits_, incs=[(qsem, 16)])
                    P.emit("gpsimd", lambda g, e=e, n0=n0, nt=nt, wd=wd: g.dma_start(
                        out=wd[:, 0:nt * D].rearrange("p (k c) -> p k c", k=nt),
                        in_=dnT_d[e:e + 1, n0:n0 + nt * 128, :].rearrange("o (k p) c -> p (o k) c", p=128)),
                        waits=[], incs=[(qsem, 16)])
                    return now(qsem)

                mTh_tok = [None]

                def load_m1(e, ch, slot, waits_):
                    P.emit("scalar", lambda s, e=e, ch=ch, slot=slot: s.dma_start(
                        out=m1r[slot][0:1, :],
                        in_=mTcF[e * 16 + 4 * ch: e * 16 + 4 * ch + 4, 0:128]),
                           waits=waits_ + [mTh_tok[0]], incs=[(dM[slot], 16)])
                    return now(dM[slot])

                for half in range(2):
                    t0 = half * HTILES
                    for i in range(HTILES):
                        r = i % 2
                        ci, jj = i // 4, i % 4
                        toks = emit_ln_tile(war, src_dram, t0 + i, r)
                        evac_toks = []
                        for k in range(KC):
                            tp = (i * KC + k) % 2
                            P.emit("tensor", lambda t, k=k, r=r, tp=tp:
                                   t.transpose(TPy[tp], ybuf[r][:, k * 128:(k + 1) * 128], ident[:, :]),
                                   waits=[toks["y"], ident_tok] + war.pop(("tpy", tp), []),
                                   incs=[(pe, 1)])
                            tp_tok = now(pe)
                            P.emit("scalar", lambda s, k=k, tp=tp:
                                   s.copy(yTf32[:, k * 128:(k + 1) * 128], TPy[tp]),
                                   waits=[tp_tok] + war.pop("yTf32", []) +
                                         (war.pop("dacc", []) if (i == 0 and k == 0) else []),
                                   incs=[(act, 1)])
                            ev = now(act)
                            evac_toks.append(ev)
                            war[("tpy", tp)] = [ev]
                            P.emit("gpsimd", lambda g, k=k, i=i:
                                   g.tensor_copy(yTh[:, k * HALF + i * 128: k * HALF + (i + 1) * 128],
                                                 yTf32[:, k * 128:(k + 1) * 128]),
                                   waits=[ev] + (war.pop("yTh", []) if (i == 0 and k == 0) else []),
                                   incs=[(gp, 1)])
                        war[("ybuf", r)] = [now(pe)]
                        war[("xr", r)] = [toks["y"], now(vec)]
                        g8w = war.pop("g8", [])
                        for k in range(KC):
                            P.emit("tensor", lambda t, k=k:
                                   t.matmul(G8, yTf32[:, k * 128:(k + 1) * 128],
                                            gw_sb[:, k * 8:(k + 1) * 8],
                                            start=(k == 0), stop=(k == KC - 1), skip_group_check=True),
                                   waits=([evac_toks[KC - 1], init_dW] + g8w) if k == 0 else [],
                                   incs=[(pe, 1)] if k == KC - 1 else [])
                        gate_tok = now(pe)
                        war["yTf32"] = [gate_tok, now(gp)]
                        P.emit("scalar", lambda s: s.activation(sc_t[:, :], G8, AF.Sigmoid),
                               waits=[gate_tok], incs=[(act, 1)])
                        sig_tok = now(act)
                        war["g8"] = [sig_tok]
                        rw = war.pop("route", [])
                        P.emit("vector", lambda v: v.tensor_add(bb_t[:, :], sc_t[:, :], biasb[:, :]),
                               waits=[sig_tok, init_dW] + rw, incs=[(vec, 1)])
                        P.emit("vector", lambda v: v.reduce_max(v1_t[:, :], bb_t[:, :], axis=AX.X),
                               waits=[now(vec)], incs=[(vec, 1)])
                        P.emit("vector", lambda v: v.tensor_scalar(t1_t[:, :], bb_t[:, :], v1_t[:, 0:1],
                                                                   -100.0, ALU.is_ge, ALU.mult),
                               waits=[now(vec)], incs=[(vec, 1)])
                        P.emit("vector", lambda v: v.tensor_tensor(t1_t[:, :], t1_t[:, :], iota8[:, :],
                                                                   ALU.add),
                               waits=[init_gp, now(vec)], incs=[(vec, 1)])
                        P.emit("vector", lambda v: v.tensor_reduce(i1_t[:, :], t1_t[:, :], axis=AX.X,
                                                                   op=ALU.min),
                               waits=[now(vec)], incs=[(vec, 1)])
                        P.emit("vector", lambda v: v.tensor_single_scalar(oh_t[:, :], t1_t[:, :],
                                                                          i1_t[:, 0:1], ALU.is_equal),
                               waits=[now(vec)], incs=[(vec, 1)])
                        P.emit("vector", lambda v: v.scalar_tensor_tensor(bb_t[:, :], oh_t[:, :], -1e30,
                                                                          bb_t[:, :], ALU.mult, ALU.add),
                               waits=[now(vec)], incs=[(vec, 1)])
                        P.emit("vector", lambda v: v.reduce_max(v1_t[:, :], bb_t[:, :], axis=AX.X),
                               waits=[now(vec)], incs=[(vec, 1)])
                        P.emit("vector", lambda v: v.tensor_scalar(t1_t[:, :], bb_t[:, :], v1_t[:, 0:1],
                                                                   -100.0, ALU.is_ge, ALU.mult),
                               waits=[now(vec)], incs=[(vec, 1)])
                        P.emit("vector", lambda v: v.tensor_tensor(t1_t[:, :], t1_t[:, :], iota8[:, :],
                                                                   ALU.add),
                               waits=[now(vec)], incs=[(vec, 1)])
                        P.emit("vector", lambda v: v.tensor_reduce(i1_t[:, :], t1_t[:, :], axis=AX.X,
                                                                   op=ALU.min),
                               waits=[now(vec)], incs=[(vec, 1)])
                        P.emit("vector", lambda v: v.tensor_single_scalar(t1_t[:, :], t1_t[:, :],
                                                                          i1_t[:, 0:1], ALU.is_equal),
                               waits=[now(vec)], incs=[(vec, 1)])
                        P.emit("vector", lambda v: v.tensor_add(oh_t[:, :], oh_t[:, :], t1_t[:, :]),
                               waits=[now(vec)], incs=[(vec, 1)])
                        P.emit("vector", lambda v, i=i: v.tensor_mul(mmF[:, i:i + 113:16],
                                                                     sc_t[:, :], oh_t[:, :]),
                               waits=[now(vec)] + (war.pop("mmF", []) if i == 0 else []),
                               incs=[(vec, 1)])
                        m_tok = now(vec)
                        war["route"] = [m_tok]
                        if i == HTILES - 1:
                            P.emit("tensor", lambda t: t.transpose(TPm, mmF[:, :], ident[:, :]),
                                   waits=[m_tok] + war.pop("tpm", []) + war.pop("mb", []),
                                   incs=[(pe, 1)])
                            tpm_tok = now(pe)
                            P.emit("scalar", lambda s: s.copy(mTcF[:, :], TPm),
                                   waits=[tpm_tok] + war.pop("mTc", []),
                                   incs=[(act, 1)])
                            mTh_tok[0] = now(act)
                            war["tpm"] = [mTh_tok[0]]
                            war["mmF"] = [tpm_tok]
                    ycast_all = now(gp)

                    ptokA = {0: load_piece(0, 0, war.pop(("WA", 0), []))}
                    ptokB = load_piece(0, 1, war.pop("WB", []))
                    ptokA[1] = load_piece(1, 0, war.pop(("WA", 1), []))
                    m1toks = {}
                    gctr = 0

                    for e in range(NE):
                        routed = e < 8
                        lastAuse = []
                        lastBuse = []
                        for ch in range(HCHUNKS):
                            yp = ch % 2
                            if routed:
                                g_ = gctr
                                gctr += 1
                                slot = g_ % 2
                                m1toks[g_] = load_m1(e, ch, slot, war.pop(("m1", slot), []))
                                mbw = war.pop("mb", [])
                                if g_ == 0:
                                    mbw = mbw + [mTh_tok[0]]
                                P.emit("tensor", lambda t, slot=slot:
                                       t.matmul(MB[:, :], ones1[0:1, :], m1r[slot][0:1, :],
                                                start=True, stop=True, skip_group_check=True),
                                       waits=[m1toks[g_], init_gp] + mbw, incs=[(pe, 1)])
                                mb_tok = now(pe)
                                war[("m1", slot)] = [mb_tok]
                                P.emit("scalar", lambda s: s.copy(mbs[:, :], MB[:, :]),
                                       waits=[mb_tok] + war.pop("mbs", []), incs=[(act, 1)])
                                mbs_tok = now(act)
                                war["mb"] = [mbs_tok]
                                yew = war.pop(("ye", yp), [])
                                for k in range(KC):
                                    P.emit("gpsimd", lambda g, k=k, ch=ch, yp=yp:
                                           g.tensor_tensor(ye[yp][:, k * CHUNK:(k + 1) * CHUNK],
                                                           yTh[:, k * HALF + ch * CHUNK:
                                                               k * HALF + (ch + 1) * CHUNK],
                                                           mbs[:, :], ALU.mult),
                                           waits=([mbs_tok, ycast_all] + yew) if k == 0 else [],
                                           incs=[(gp, 1)] if k == KC - 1 else [])
                                ye_tok = now(gp)
                                war["mbs"] = [ye_tok]
                            for piece in range(2):
                                if piece == 0:
                                    nt, wu, wd, wtok = NT_A, WupA[e % 2], WdnA[e % 2], ptokA[e]
                                else:
                                    nt, wu, wd, wtok = NT_B, WupB, WdnB, ptokB
                                hb = piece
                                sq_last = None
                                for ntile in range(nt):
                                    ub = ntile % 2
                                    uw = war.pop(("U", ub), [])
                                    for k in range(KC):
                                        if routed:
                                            movap = (lambda k=k, yp=yp:
                                                     ye[yp][:, k * CHUNK:(k + 1) * CHUNK])
                                            mov_tok = ye_tok
                                        else:
                                            movap = (lambda k=k, ch=ch:
                                                     yTh[:, k * HALF + ch * CHUNK:
                                                         k * HALF + (ch + 1) * CHUNK])
                                            mov_tok = ycast_all
                                        P.emit("tensor", lambda t, ntile=ntile, k=k, nt=nt, wu=wu,
                                               ub=ub, movap=movap:
                                               t.matmul(U[ub][:, :],
                                                        wu[:, (k * nt + ntile) * 128:
                                                           (k * nt + ntile + 1) * 128],
                                                        movap(), start=(k == 0), stop=(k == KC - 1),
                                                        skip_group_check=True),
                                               waits=([wtok, mov_tok] + uw) if k == 0 else [],
                                               incs=[(pe, 1)] if k == KC - 1 else [])
                                    u_tok = now(pe)
                                    P.emit("scalar", lambda s, ntile=ntile, ub=ub, hb=hb:
                                           s.activation(hh[hb][:, ntile * CHUNK:(ntile + 1) * CHUNK],
                                                        U[ub][:, :], AF.Relu),
                                           waits=[u_tok] + (war.pop(("hhh", hb), [])
                                                            if ntile == 0 else []),
                                           incs=[(act, 1)])
                                    relu_tok = now(act)
                                    war[("U", ub)] = [relu_tok]
                                    P.emit("gpsimd", lambda g, ntile=ntile, hb=hb:
                                           g.tensor_mul(hh[hb][:, ntile * CHUNK:(ntile + 1) * CHUNK],
                                                        hh[hb][:, ntile * CHUNK:(ntile + 1) * CHUNK],
                                                        hh[hb][:, ntile * CHUNK:(ntile + 1) * CHUNK]),
                                           waits=[relu_tok], incs=[(gp, 1)])
                                    sq_last = now(gp)
                                for tt in range(4):
                                    for cc in range(2):
                                        db = (tt * 2 + cc) % 2
                                        dw_ = war.pop(("D", db), [])
                                        for ntile in range(nt):
                                            P.emit("tensor", lambda t, tt=tt, cc=cc, ntile=ntile,
                                                   nt=nt, wd=wd, hb=hb, db=db:
                                                   t.matmul(DD[db][:, :],
                                                            hh[hb][:, ntile * CHUNK + tt * 128:
                                                                   ntile * CHUNK + (tt + 1) * 128],
                                                            wd[:, ntile * D + cc * 512:
                                                               ntile * D + (cc + 1) * 512],
                                                            start=(ntile == 0), stop=(ntile == nt - 1),
                                                            skip_group_check=True),
                                                   waits=([sq_last] + dw_) if ntile == 0 else [],
                                                   incs=[(pe, 1)] if ntile == nt - 1 else [])
                                        d_tok = now(pe)
                                        dslc = (lambda ch=ch, tt=tt, cc=cc:
                                                dacc[:, (ch * 4 + tt) * D + cc * 512:
                                                     (ch * 4 + tt) * D + (cc + 1) * 512])
                                        dchain = war.pop(("dch", ch, tt, cc), [])
                                        if e == 0 and piece == 0:
                                            P.emit("vector", lambda v, dslc=dslc, db=db:
                                                   v.tensor_copy(dslc(), DD[db][:, :]),
                                                   waits=[d_tok] + ([ycast_all]
                                                                    if (ch == 0 and tt == 0 and cc == 0)
                                                                    else []),
                                                   incs=[(vec, 1)])
                                        else:
                                            P.emit("vector", lambda v, dslc=dslc, db=db:
                                                   v.tensor_tensor(dslc(), dslc(), DD[db][:, :],
                                                                   ALU.add),
                                                   waits=[d_tok] + dchain, incs=[(vec, 1)])
                                        war[("D", db)] = [now(vec)]
                                        war[("dch", ch, tt, cc)] = [now(vec)]
                                if piece == 0:
                                    lastAuse = [now(pe)]
                                else:
                                    lastBuse = [now(pe)]
                                war[("hhh", hb)] = [now(pe)]
                            if routed:
                                war[("ye", yp)] = [now(pe)]
                        war[("WA", e % 2)] = lastAuse
                        war["WB"] = lastBuse
                        if e + 2 < NE:
                            ptokA[e + 2] = load_piece(e + 2, 0, war.pop(("WA", e % 2)))
                        if e + 1 < NE:
                            ptokB = load_piece(e + 1, 1, war.pop("WB"))
                    dacc_done = now(vec)

                    if dbg and half == 0 and tag == "f1":
                        P.emit("gpsimd", lambda g: g.dma_start(out=dbg_d["yTh"][:, :], in_=yTh[:, :]),
                               waits=[dacc_done], incs=[(dDbg, 16)])
                        P.emit("gpsimd", lambda g: g.dma_start(out=dbg_d["mTc"][:, :], in_=mTcF[:, :]),
                               waits=[], incs=[(dDbg, 16)])
                        P.emit("gpsimd", lambda g: g.dma_start(out=dbg_d["dacc"][:, :], in_=dacc[:, :]),
                               waits=[], incs=[(dDbg, 16)])
                        P.emit("gpsimd", lambda g: g.dma_start(out=dbg_d["ye"][:, :], in_=ye[1][:, :]),
                               waits=[], incs=[(dDbg, 16)])

                    for i in range(HTILES):
                        r = i % 2
                        P.emit("sync", lambda sy, i=i, r=r, t0=t0: sy.dma_start(
                            out=xr[r][:, :], in_=src_dram[(t0 + i) * 128:(t0 + i + 1) * 128, :]),
                               waits=[dacc_done] + war.pop(("xr", r), []), incs=[(dXs[r], 16)])
                        ld = now(dXs[r])
                        tap_guards = []
                        if dbg and half == 0 and tag == "f1" and i == 0:
                            P.emit("gpsimd", lambda g: g.dma_start(out=dbg_d["xs0"][0:128, :],
                                                                   in_=xr[r][:, :]),
                                   waits=[ld], incs=[(dDbg, 16)])
                            tap_guards.append(now(dDbg))
                        P.emit("vector", lambda v, i=i, r=r: v.scalar_tensor_tensor(
                            ybuf[r][:, :], dacc[:, i * D:(i + 1) * D], 0.5, xr[r][:, :],
                            ALU.mult, ALU.add),
                               waits=[ld] + war.pop(("ybuf", r), []) +
                                     (war.pop("mbs", []) if i == 0 else []),
                               incs=[(vec, 1)])
                        stt_tok = now(vec)
                        yb_guards = []
                        if dbg and half == 0 and tag == "f1" and i == 0:
                            P.emit("gpsimd", lambda g: g.dma_start(out=dbg_d["xs0"][128:256, :],
                                                                   in_=ybuf[0][:, :]),
                                   waits=[stt_tok], incs=[(dDbg, 16)])
                            yb_guards.append(now(dDbg))
                        war[("xr", r)] = [stt_tok] + tap_guards
                        P.emit("sync", lambda sy, i=i, r=r, t0=t0: sy.dma_start(
                            out=dst_dram[(t0 + i) * 128:(t0 + i + 1) * 128, :], in_=ybuf[r][:, :]),
                               waits=[stt_tok], incs=[(dYs[r], 16)])
                        war[("ybuf", r)] = [now(dYs[r])] + yb_guards
                    if dbg and half == 0 and tag == "f1":
                        P.emit("gpsimd", lambda g: g.dma_start(out=dbg_d["xs0"][:, :],
                                                               in_=dst_dram[0:HALF, :]),
                               waits=[now(dYs[0]), now(dYs[1])], incs=[(dDbg, 16)])
                        P.emit("gpsimd", lambda g: g.dma_start(out=dbg_d["dacc2"][:, :], in_=dacc[:, :]),
                               waits=[now(vec)], incs=[(dDbg, 16)])
                    war["dacc"] = [now(vec)]
                    war["yTh"] = [now(pe), now(vec)]
                    war["mTc"] = [now(dM[0]), now(dM[1])]
            engine_barrier()

        def attn_phase(src_dram, dst_dram):
            with ExitStack() as ph:
                psb = lambda n_, sh, dt: ph.enter_context(nc.sbuf_tensor(n_ + "_at", sh, dt))
                iota512 = psb("iota512", [128, 512], F32)
                wqkv_sb = psb("wqkv", [128, KC * 192], BF16)
                wout_sb = psb("wout", [64, D], BF16)
                zT = [psb(f"zT{i}", [128, KC * CHUNK], BF16) for i in range(2)]
                qkT = psb("qkT", [128, NB * SEQ], BF16)
                v_sb = psb("v_sb", [128, NTILES * 64], BF16)
                vI = psb("vI", [128, 16 * 128], BF16)
                kc_r = psb("kc_r", [1, 2 * SEQ], BF16)
                qc_r = psb("qc_r", [1, 2 * SEQ], BF16)
                Esb = [psb(f"Esb{i}", [128, 4 * 512], BF16) for i in range(2)]
                masks = psb("masks", [128, 4 * 512], BF16)
                ndbuf = psb("ndbuf", [128, SEQ], F32)
                rden = psb("rden", [64, SEQ], F32)
                oT = psb("oT", [64, SEQ], BF16)
                stg = [psb(f"stg{i}", [2, 512], F32) for i in range(4)]

                TPz = [PB[5][:, 0:128], PB[6][:, 0:128]]
                QK = PB[7]
                Vp = PB[4][:, 0:64]
                SS = [PB[0], PB[1], PB[2], PB[3]]
                ND = [PB[5][0:2, :], PB[6][0:2, :]]
                OP = [PB[4], PB[7]]

                war = {}
                P.emit("gpsimd", lambda g: g.dma_start(
                    out=wqkv_sb[:, :].rearrange("p (k f) -> p k f", k=KC),
                    in_=wqkvT_d[:, :].rearrange("(k p) f -> p k f", p=128)), incs=[(dInit, 16)])
                P.emit("gpsimd", lambda g: g.dma_start(out=wout_sb[:, :], in_=woutT_d[:, :]),
                       incs=[(dInit, 16)])
                wqkv_tok = now(dInit)
                wout_tok = now(dInit)
                P.emit("gpsimd", lambda g: g.iota(iota512[:, :], pattern=[[1, 512]], base=0,
                                                  channel_multiplier=-1,
                                                  allow_small_or_imprecise_dtypes=True),
                       incs=[(gp, 1)])
                io5_tok = now(gp)
                for m in range(4):
                    P.emit("vector", lambda v, m=m: v.tensor_single_scalar(
                        masks[:, m * 512:(m + 1) * 512], iota512[:, :], float(128.0 * m), ALU.is_ge),
                           waits=[io5_tok], incs=[(vec, 1)])
                mask_tok = now(vec)

                for ch8 in range(8):
                    zr = ch8 % 2
                    last_evac = None
                    for j in range(4):
                        i = ch8 * 4 + j
                        r = i % 2
                        toks = emit_ln_tile(war, src_dram, i, r)
                        for k in range(KC):
                            tp = (i * KC + k) % 2
                            P.emit("tensor", lambda t, k=k, r=r, tp=tp:
                                   t.transpose(TPz[tp], ybuf[r][:, k * 128:(k + 1) * 128], ident[:, :]),
                                   waits=[toks["y"], ident_tok] + war.pop(("tpz", tp), []),
                                   incs=[(pe, 1)])
                            tp_tok = now(pe)
                            P.emit("scalar", lambda s, k=k, j=j, zr=zr, tp=tp:
                                   s.copy(zT[zr][:, k * CHUNK + j * 128: k * CHUNK + (j + 1) * 128],
                                          TPz[tp]),
                                   waits=[tp_tok] + (war.pop(("zslot", zr), [])
                                                     if (k == 0 and j == 0) else []),
                                   incs=[(act, 1)])
                            war[("tpz", tp)] = [now(act)]
                            last_evac = now(act)
                        war[("ybuf", r)] = [now(pe)]
                        war[("xr", r)] = [toks["y"], now(vec)]
                        vw = war.pop("vp", [])
                        for k in range(KC):
                            P.emit("tensor", lambda t, k=k, zr=zr, j=j:
                                   t.matmul(Vp, zT[zr][:, k * CHUNK + j * 128: k * CHUNK + (j + 1) * 128],
                                            wqkv_sb[:, k * 192 + 128: k * 192 + 192],
                                            start=(k == 0), stop=(k == KC - 1), skip_group_check=True),
                                   waits=([wqkv_tok, last_evac] + vw) if k == 0 else [],
                                   incs=[(pe, 1)] if k == KC - 1 else [])
                        vp_tok = now(pe)
                        P.emit("scalar", lambda s, i=i: s.copy(v_sb[:, i * 64:(i + 1) * 64], Vp),
                               waits=[vp_tok], incs=[(act, 1)])
                        war["vp"] = [now(act)]
                    qkw = war.pop("qk", [])
                    for k in range(KC):
                        P.emit("tensor", lambda t, k=k, zr=zr:
                               t.matmul(QK[:, :], wqkv_sb[:, k * 192: k * 192 + 128],
                                        zT[zr][:, k * CHUNK:(k + 1) * CHUNK],
                                        start=(k == 0), stop=(k == KC - 1), skip_group_check=True),
                               waits=([wqkv_tok, last_evac] + qkw) if k == 0 else [],
                               incs=[(pe, 1)] if k == KC - 1 else [])
                    qk_tok = now(pe)
                    b = ch8 // 4
                    pos = (ch8 % 4) * CHUNK
                    P.emit("scalar", lambda s, b=b, pos=pos: s.activation(
                        qkT[0:64, b * SEQ + pos: b * SEQ + pos + CHUNK], QK[0:64, :], AF.Identity,
                        scale=f32(0.125)),
                           waits=[qk_tok], incs=[(act, 1)])
                    P.emit("scalar", lambda s, b=b, pos=pos: s.copy(
                        qkT[64:128, b * SEQ + pos: b * SEQ + pos + CHUNK], QK[64:128, :]),
                           incs=[(act, 1)])
                    war["qk"] = [now(act)]
                    war[("zslot", zr)] = [now(pe)]
                qkT_tok = now(act)
                vsb_tok = now(act)

                for b in range(NB):
                    P.emit("gpsimd", lambda g: g.memset(vI[:, :], 1.0),
                           waits=[vsb_tok] + war.pop("vI", []), incs=[(gp, 1)])
                    vi0 = now(gp)
                    for jb in range(16):
                        P.emit("vector", lambda v, b=b, jb=jb: v.tensor_copy(
                            vI[:, jb * 128:(jb + 1) * 128].rearrange("p (h two) -> p h two", two=2)[:, :, 0:1],
                            v_sb[:, (b * 16 + jb) * 64:(b * 16 + jb + 1) * 64].rearrange("p (h o) -> p h o", o=1)),
                               waits=[vi0] if jb == 0 else [], incs=[(vec, 1)])
                    vI_tok = now(vec)

                    def load_rows(h, slot, waits_, b=b):
                        P.emit("scalar", lambda s, h=h, b=b, slot=slot: s.dma_start(
                            out=kc_r[0:1, slot * SEQ:(slot + 1) * SEQ],
                            in_=qkT[64 + h:65 + h, b * SEQ:(b + 1) * SEQ]),
                               waits=waits_ + [qkT_tok], incs=[(dR[slot], 16)])
                        P.emit("scalar", lambda s, h=h, b=b, slot=slot: s.dma_start(
                            out=qc_r[0:1, slot * SEQ:(slot + 1) * SEQ],
                            in_=qkT[h:h + 1, b * SEQ:(b + 1) * SEQ]),
                               waits=[], incs=[(dR[slot], 16)])
                        return now(dR[slot])

                    rowtok = {0: load_rows(0, 0, war.pop(("rows", 0), []))}
                    rowtok[1] = load_rows(1, 1, war.pop(("rows", 1), []))
                    for h in range(64):
                        slot = h % 2
                        rtok = rowtok.pop(h)
                        for ic in range(4):
                            ndb = ic % 2
                            ndw = war.pop(("nd", ndb), [])
                            njb = 4 * (ic + 1)
                            for g in range(njb // 4):
                                eb = g % 2
                                ssw = war.pop("SSg", [])
                                for j4 in range(4):
                                    jb = 4 * g + j4
                                    P.emit("tensor", lambda t, ic=ic, jb=jb, slot=slot, j4=j4:
                                           t.matmul(SS[j4][:, :],
                                                    kc_r[0:1, slot * SEQ + jb * 128:
                                                         slot * SEQ + (jb + 1) * 128],
                                                    qc_r[0:1, slot * SEQ + ic * 512:
                                                         slot * SEQ + (ic + 1) * 512],
                                                    start=True, stop=True, skip_group_check=True),
                                           waits=([rtok] + ssw) if j4 == 0 else [],
                                           incs=[(pe, 1)] if j4 == 3 else [])
                                s_tok = now(pe)
                                ebw = war.pop(("Ebuf", eb), [])
                                P.emit("scalar", lambda s, eb=eb: s.activation(
                                    Esb[eb][:, :], PBbig[:, :], AF.Exp, bias=f32(-2.0)),
                                       waits=[s_tok] + ebw, incs=[(act, 1)])
                                e_tok = now(act)
                                war["SSg"] = [e_tok]
                                if g == ic:
                                    P.emit("vector", lambda v, eb=eb: v.tensor_tensor(
                                        Esb[eb][:, :], Esb[eb][:, :], masks[:, :], ALU.mult),
                                           waits=[e_tok, mask_tok], incs=[(vec, 1)])
                                    e_tok = now(vec)
                                for j4 in range(4):
                                    jb = 4 * g + j4
                                    P.emit("tensor", lambda t, h=h, jb=jb, eb=eb, ndb=ndb, njb=njb, j4=j4:
                                           t.matmul(ND[ndb], vI[:, jb * 128 + 2 * h: jb * 128 + 2 * h + 2],
                                                    Esb[eb][:, 512 * j4:512 * (j4 + 1)],
                                                    start=(jb == 0), stop=(jb == njb - 1),
                                                    skip_group_check=True),
                                           waits=([vI_tok, e_tok] + (ndw if jb == 0 else []))
                                                 if j4 == 0 else [],
                                           incs=[(pe, 1)] if j4 == 3 else [])
                                war[("Ebuf", eb)] = [now(pe)]
                            nd_tok = now(pe)
                            sg = (h * 4 + ic) % 4
                            P.emit("vector", lambda v, ndb=ndb, sg=sg: v.tensor_copy(stg[sg][:, :], ND[ndb]),
                                   waits=[nd_tok] + war.pop(("stg", sg), []), incs=[(vec, 1)])
                            st_tok = now(vec)
                            war[("nd", ndb)] = [st_tok]
                            P.emit("scalar", lambda s, h=h, ic=ic, sg=sg: s.dma_start(
                                out=ndbuf[h:h + 1, ic * 512:(ic + 1) * 512], in_=stg[sg][0:1, :]),
                                   waits=[st_tok] + war.pop("hscat", []) +
                                         (war.pop("ndbuf", []) if (h == 0 and ic == 0) else []),
                                   incs=[(dG[sg], 16)])
                            P.emit("scalar", lambda s, h=h, ic=ic, sg=sg: s.dma_start(
                                out=ndbuf[64 + h:65 + h, ic * 512:(ic + 1) * 512], in_=stg[sg][1:2, :]),
                                   waits=[], incs=[(dG[sg], 16)])
                            war[("stg", sg)] = [now(dG[sg])]
                            war["hscat"] = [now(dG[sg])]
                        if h + 2 < 64:
                            rowtok[h + 2] = load_rows(h + 2, slot, [now(pe)])
                    scat_toks = [now(q) for q in dG]
                    P.emit("vector", lambda v: v.reciprocal(rden[:, :], ndbuf[64:128, :]),
                           waits=scat_toks + war.pop("rden", []), incs=[(vec, 1)])
                    P.emit("vector", lambda v: v.tensor_mul(oT[:, :], ndbuf[0:64, :], rden[:, :]),
                           waits=[now(vec)] + war.pop("oT", []), incs=[(vec, 1)])
                    oT_tok = now(vec)
                    war["ndbuf"] = [oT_tok]
                    war["rden"] = [oT_tok]
                    for j in range(16):
                        i = b * 16 + j
                        r = i % 2
                        optoks = []
                        for cc in range(2):
                            opw = war.pop(("OP", cc), [])
                            P.emit("tensor", lambda t, j=j, cc=cc:
                                   t.matmul(OP[cc][:, :], oT[:, j * 128:(j + 1) * 128],
                                            wout_sb[:, cc * 512:(cc + 1) * 512],
                                            start=True, stop=True, skip_group_check=True),
                                   waits=[oT_tok, wout_tok] + opw, incs=[(pe, 1)])
                            optoks.append(now(pe))
                        P.emit("sync", lambda sy, i=i, r=r: sy.dma_start(
                            out=xr[r][:, :], in_=src_dram[i * 128:(i + 1) * 128, :]),
                               waits=war.pop(("xr", r), []), incs=[(dXs[r], 16)])
                        ld = now(dXs[r])
                        ybw = war.pop(("ybuf", r), [])
                        for cc in range(2):
                            P.emit("vector", lambda v, r=r, cc=cc: v.tensor_tensor(
                                ybuf[r][:, cc * 512:(cc + 1) * 512], xr[r][:, cc * 512:(cc + 1) * 512],
                                OP[cc][:, :], ALU.add),
                                   waits=[ld, optoks[cc]] + (ybw if cc == 0 else []),
                                   incs=[(vec, 1)])
                            war[("OP", cc)] = [now(vec)]
                        stt_tok = now(vec)
                        war[("xr", r)] = [stt_tok]
                        P.emit("sync", lambda sy, i=i, r=r: sy.dma_start(
                            out=dst_dram[i * 128:(i + 1) * 128, :], in_=ybuf[r][:, :]),
                               waits=[stt_tok], incs=[(dYs[r], 16)])
                        war[("ybuf", r)] = [now(dYs[r])]
                    war["oT"] = [now(pe)]
                    war["vI"] = [now(pe)]
                    war[("rows", 0)] = [now(pe)]
                    war[("rows", 1)] = [now(pe)]
            engine_barrier()

        if "F1" in phases:
            ffn_phase(x_d, xs_d, "f1")
            a_src = xs_d
        else:
            a_src = x_d
        if "A" in phases:
            attn_phase(a_src, xs_d)
            f2_src = xs_d
        else:
            f2_src = a_src
        if "F2" in phases:
            ffn_phase(f2_src, out_d, "f2")
        else:
            prev = None
            for i in range(NTILES):
                P.emit("sync", lambda sy, i=i: sy.dma_start(out=xr[0][:, :],
                                                            in_=f2_src[i * 128:(i + 1) * 128, :]),
                       waits=[prev] if prev else [], incs=[(dXs[0], 16)])
                ld = now(dXs[0])
                P.emit("sync", lambda sy, i=i: sy.dma_start(out=out_d[i * 128:(i + 1) * 128, :],
                                                            in_=xr[0][:, :]),
                       waits=[ld], incs=[(dYs[0], 16)])
                prev = now(dYs[0])

        P.emit("sync", lambda sy: sy.nop(), waits=[now(dYs[0]), now(dYs[1])])

        with ExitStack() as sctx:
            for s in all_sems:
                s.h = sctx.enter_context(nc.semaphore(s.name))
            block = sctx.enter_context(nc.Block())
            block.tensor(lambda t: P.run("tensor", t))
            block.vector(lambda v: P.run("vector", v))
            block.scalar(lambda s: P.run("scalar", s))
            block.gpsimd(lambda g: P.run("gpsimd", g))
            block.sync(lambda sy: P.run("sync", sy))
    return nc


def kernel(L0_attn_x, L0_attn_w_qkv, L0_attn_w_out, L0_ffn_gate_w, L0_ffn_expert_bias,
           L0_ffn_w_up, L0_ffn_w_down, L0_ffn_shared_w_up, L0_ffn_shared_w_down):
    x = np.ascontiguousarray(np.asarray(L0_attn_x, dtype=np.float32))
    wqkvT = np.ascontiguousarray(np.asarray(L0_attn_w_qkv, np.float32).T)
    woutT = np.ascontiguousarray(np.asarray(L0_attn_w_out, np.float32).T)
    gwT = np.ascontiguousarray(np.asarray(L0_ffn_gate_w, np.float32).T)
    biasbc = np.ascontiguousarray(
        np.tile(np.asarray(L0_ffn_expert_bias, np.float32)[None, :], (128, 1)))
    up = np.asarray(L0_ffn_w_up, np.float32)
    dn = np.asarray(L0_ffn_w_down, np.float32)
    sup = np.asarray(L0_ffn_shared_w_up, np.float32)
    sdn = np.asarray(L0_ffn_shared_w_down, np.float32)
    upT = np.ascontiguousarray(np.concatenate([up.transpose(0, 2, 1), sup.T[None]], axis=0))
    dnT = np.ascontiguousarray(np.concatenate([dn.transpose(0, 2, 1), sdn.T[None]], axis=0))

    nc = build_nc()
    in_maps = []
    for c in range(NCORES):
        xs = np.ascontiguousarray(x[2 * c:2 * c + 2].reshape(T, D))
        in_maps.append(dict(x=xs, wqkvT=wqkvT, woutT=woutT, gwT=gwT, biasbc=biasbc,
                            upT=upT, dnT=dnT))
    res = run_bass_kernel_spmd(nc, in_maps, core_ids=list(range(NCORES)))
    out = np.concatenate([np.asarray(res.results[c]["out"]).reshape(2, SEQ, D)
                          for c in range(NCORES)], axis=0)
    return out.astype(np.float32)


# version 03
# speedup vs baseline: 1.1330x; 1.1330x over previous
import numpy as np
from contextlib import ExitStack

import concourse.bass as bass
import concourse.mybir as mybir
from concourse.bass_utils import run_bass_kernel_spmd

F32 = mybir.dt.float32
BF16 = mybir.dt.bfloat16
AF = mybir.ActivationFunctionType
ALU = mybir.AluOpType
AX = mybir.AxisListType
f32 = np.float32

NCORES = 8
D = 1024
KC = 8
HID = 1408
NH_A, NT_A = 640, 5
NH_B, NT_B = 768, 6
NE = 9
SEQ = 2048
NB = 2
T = NB * SEQ
NTILES = T // 128
HALF = 2048
HTILES = HALF // 128
CHUNK = 512
HCHUNKS = HALF // CHUNK


class Sem:
    __slots__ = ("name", "v", "h")

    def __init__(self, name):
        self.name, self.v, self.h = name, 0, None


class Plan:
    def __init__(self):
        self.ops = []

    def emit(self, engine, fn, waits=(), incs=()):
        ws = [(s, v) for (s, v) in waits if v > 0]
        ri = []
        for (s, k) in incs:
            s.v += k
            ri.append((s, k))
        self.ops.append((engine, ws, fn, ri))

    def run(self, engine_name, eng):
        for e, ws, fn, ri in self.ops:
            if e != engine_name:
                continue
            merged = {}
            for (s, v) in ws:
                if s.name in merged:
                    merged[s.name] = (s, max(v, merged[s.name][1]))
                else:
                    merged[s.name] = (s, v)
            for (s, v) in merged.values():
                eng.wait_ge(s.h, v)
            inst = fn(eng)
            for (s, k) in ri:
                inst.then_inc(s.h, k)


def build_nc(phases=("F1", "A", "F2"), dbg=False):
    nc = bass.Bass("TRN2", target_bir_lowering=False, debug=False)
    dbg_d = {}
    if dbg:
        dbg_d["yTh"] = nc.dram_tensor("dbg_yTh", [128, KC * HALF], F32, kind="ExternalOutput").ap()
        dbg_d["mTc"] = nc.dram_tensor("dbg_mTc", [128, 128], F32, kind="ExternalOutput").ap()
        dbg_d["dacc"] = nc.dram_tensor("dbg_dacc", [128, HTILES * D], F32, kind="ExternalOutput").ap()
        dbg_d["ye"] = nc.dram_tensor("dbg_ye", [128, KC * CHUNK], F32, kind="ExternalOutput").ap()
        dbg_d["xs0"] = nc.dram_tensor("dbg_xs0", [HALF, D], F32, kind="ExternalOutput").ap()
        dbg_d["dacc2"] = nc.dram_tensor("dbg_dacc2", [128, HTILES * D], F32, kind="ExternalOutput").ap()

    x_d = nc.dram_tensor("x", [T, D], F32, kind="ExternalInput").ap()
    wqkvT_d = nc.dram_tensor("wqkvT", [D, 192], F32, kind="ExternalInput").ap()
    woutT_d = nc.dram_tensor("woutT", [64, D], F32, kind="ExternalInput").ap()
    gwT_d = nc.dram_tensor("gwT", [D, 8], F32, kind="ExternalInput").ap()
    biasbc_d = nc.dram_tensor("biasbc", [128, 8], F32, kind="ExternalInput").ap()
    upT_d = nc.dram_tensor("upT", [NE, D, HID], F32, kind="ExternalInput").ap()
    dnT_d = nc.dram_tensor("dnT", [NE, HID, D], F32, kind="ExternalInput").ap()
    out_d = nc.dram_tensor("out", [T, D], F32, kind="ExternalOutput").ap()
    xs_d = nc.dram_tensor("xs", [T, D], F32, kind="Internal").ap()

    P = Plan()
    dInit = Sem("dInit")
    dWA = [Sem("dWA0"), Sem("dWA1")]
    dWB = Sem("dWB")
    dXs = [Sem("dX0"), Sem("dX1")]
    dYs = [Sem("dY0"), Sem("dY1")]
    dM = [Sem("dM0"), Sem("dM1")]
    dR = [Sem("dR0"), Sem("dR1")]
    dG = [Sem(f"dG{i}") for i in range(4)]
    dDbg = Sem("dDbg")
    pe, act, vec, gp, bar = Sem("pe"), Sem("act"), Sem("vec"), Sem("gp"), Sem("bar")
    all_sems = [dInit, dWA[0], dWA[1], dWB, dXs[0], dXs[1], dYs[0], dYs[1],
                dM[0], dM[1], dR[0], dR[1]] + dG + [dDbg, pe, act, vec, gp, bar]

    def now(s):
        return (s, s.v)

    with ExitStack() as top:
        sb = lambda n_, sh, dt: top.enter_context(nc.sbuf_tensor(n_, sh, dt))
        ident = sb("ident", [128, 128], F32)
        ones1 = sb("ones1", [1, 128], BF16)
        iota8 = sb("iota8", [128, 8], F32)
        biasb = sb("biasb", [128, 8], F32)
        gw_sb = sb("gw_sb", [128, KC * 8], F32)
        xr = [sb(f"xr{i}", [128, D], F32) for i in range(2)]
        ybuf = [sb(f"ybuf{i}", [128, D], F32) for i in range(2)]
        stt_ = [sb(f"stt{i}", [128, 8], F32) for i in range(2)]
        PBbig = top.enter_context(nc.psum_tensor("PBbig", [128, 2048], F32))
        PB = [PBbig[:, 512 * i:512 * (i + 1)] for i in range(4)] + \
             [top.enter_context(nc.psum_tensor(f"PB{i}", [128, 512], F32)) for i in range(4, 8)]

        P.emit("gpsimd", lambda g: g.iota(ident[:, :], pattern=[[1, 128]], base=0,
                                          channel_multiplier=-1, allow_small_or_imprecise_dtypes=True),
               incs=[(gp, 1)])
        P.emit("gpsimd", lambda g: g.iota(iota8[:, :], pattern=[[1, 8]], base=0,
                                          channel_multiplier=0, allow_small_or_imprecise_dtypes=True),
               incs=[(gp, 1)])
        P.emit("gpsimd", lambda g: g.memset(ones1[:, :], 1.0), incs=[(gp, 1)])
        P.emit("gpsimd", lambda g: g.dma_start(out=biasb[:, :], in_=biasbc_d[:, :]), incs=[(dInit, 16)])
        P.emit("gpsimd", lambda g: g.dma_start(
            out=gw_sb[:, :].rearrange("p (k e) -> p k e", k=KC),
            in_=gwT_d[:, :].rearrange("(k p) e -> p k e", p=128)), incs=[(dInit, 16)])
        init_gp = now(gp)
        init_dW = now(dInit)
        P.emit("vector", lambda v: v.tensor_single_scalar(ident[:, :], ident[:, :], 0.0, ALU.is_equal),
               waits=[init_gp], incs=[(vec, 1)])
        ident_tok = now(vec)

        def engine_barrier():
            target = bar.v + 5
            dma_fence = {"sync": [now(q) for q in dXs + dYs],
                         "gpsimd": [now(dInit), now(dWA[0]), now(dWA[1]), now(dWB), now(dDbg)],
                         "scalar": [now(q) for q in dM + dR + dG]}
            for e in ("tensor", "vector", "scalar", "gpsimd", "sync"):
                P.emit(e, lambda eng: eng.nop(), waits=dma_fence.get(e, []), incs=[(bar, 1)])
            for e in ("tensor", "vector", "scalar", "gpsimd", "sync"):
                P.emit(e, lambda eng: eng.nop(), waits=[(bar, target)])

        def emit_ln_tile(war, src_dram, ti, r):
            st = stt_[r]
            s1, s2 = st[:, 0:1], st[:, 1:2]
            mu, var = st[:, 2:3], st[:, 3:4]
            rstd, nm = st[:, 4:5], st[:, 5:6]
            P.emit("sync", lambda sy: sy.dma_start(out=xr[r][:, :],
                                                   in_=src_dram[ti * 128:(ti + 1) * 128, :]),
                   waits=war.pop(("xr", r), []), incs=[(dXs[r], 16)])
            ld = now(dXs[r])
            stw = war.pop(("st", r), [])
            ybw = war.pop(("ybuf", r), [])
            P.emit("vector", lambda v: v.reduce_sum(s1, xr[r][:, :], axis=AX.X),
                   waits=[ld] + stw, incs=[(vec, 1)])
            s1_tok = now(vec)
            P.emit("scalar", lambda s: s.activation(ybuf[r][:, :], xr[r][:, :], AF.Square,
                                                    accum_out=s2),
                   waits=[ld] + stw + ybw, incs=[(act, 1)])
            s2_tok = now(act)
            P.emit("vector", lambda v: v.tensor_scalar_mul(mu, s1, 1.0 / D),
                   waits=[s1_tok], incs=[(vec, 1)])
            mu_tok = now(vec)
            P.emit("vector", lambda v: v.tensor_scalar_mul(var, s2, 1.0 / D),
                   waits=[s2_tok], incs=[(vec, 1)])
            P.emit("vector", lambda v: v.tensor_mul(s1, mu, mu), waits=[mu_tok], incs=[(vec, 1)])
            P.emit("vector", lambda v: v.tensor_sub(var, var, s1), waits=[now(vec)],
                   incs=[(vec, 1)])
            var_tok = now(vec)
            P.emit("scalar", lambda s: s.activation(rstd, var, AF.Sqrt, bias=f32(1e-5)),
                   waits=[var_tok], incs=[(act, 1)])
            sqr_tok = now(act)
            P.emit("vector", lambda v: v.reciprocal(rstd, rstd), waits=[sqr_tok], incs=[(vec, 1)])
            rstd_tok = now(vec)
            P.emit("vector", lambda v: v.scalar_tensor_tensor(nm, mu, -1.0, rstd, ALU.mult, ALU.mult),
                   waits=[rstd_tok], incs=[(vec, 1)])
            nm_tok = now(vec)
            P.emit("scalar", lambda s: s.activation(ybuf[r][:, :], xr[r][:, :], AF.Identity,
                                                    bias=nm, scale=rstd),
                   waits=[rstd_tok, nm_tok], incs=[(act, 1)])
            y_tok = now(act)
            war[("st", r)] = [y_tok]
            return {"ld": ld, "y": y_tok}

        def ffn_phase(src_dram, dst_dram, tag):
            with ExitStack() as ph:
                psb = lambda n_, sh, dt: ph.enter_context(nc.sbuf_tensor(n_ + tag, sh, dt))
                yTh = psb("yTh", [128, KC * HALF], BF16)
                mmF = psb("mmF", [128, 128], F32)
                mTcF = psb("mTcF", [128, 128], BF16)
                m1r = [psb(f"m1r{i}", [1, CHUNK], BF16) for i in range(2)]
                sc_t = psb("sc_t", [128, 8], F32)
                bb_t = psb("bb_t", [128, 8], F32)
                t1_t = psb("t1_t", [128, 8], F32)
                oh_t = psb("oh_t", [128, 8], F32)
                v1_t = psb("v1_t", [128, 1], F32)
                i1_t = psb("i1_t", [128, 1], F32)
                WupA = [psb(f"WupA{i}", [128, KC * NH_A], BF16) for i in range(2)]
                WdnA = [psb(f"WdnA{i}", [128, NT_A * D], BF16) for i in range(2)]
                WupB = psb("WupB", [128, KC * NH_B], BF16)
                WdnB = psb("WdnB", [128, NT_B * D], BF16)
                ye = [psb(f"ye{i}", [128, KC * CHUNK], BF16) for i in range(2)]
                hh = [psb("hhA", [128, NT_A * CHUNK], BF16), psb("hhB", [128, NT_B * CHUNK], BF16)]
                dacc = psb("dacc", [128, HTILES * D], F32)
                yTf32 = dacc

                U = [PB[0], PB[1]]
                DD = [PB[2], PB[3]]
                MB = PB[4]
                TPy = [PB[5][:, 0:128], PB[6][:, 0:128]]
                G8 = PB[7][:, 0:8]
                TPm = PB[4][0:128, 0:128]
                mbs = ybuf[0][:, 0:CHUNK]

                war = {}

                def load_piece(e, piece, waits_):
                    if piece == 0:
                        n0, nt, wu, wd = 0, NT_A, WupA[e % 2], WdnA[e % 2]
                        qsem = dWA[e % 2]
                    else:
                        n0, nt, wu, wd = NH_A, NT_B, WupB, WdnB
                        qsem = dWB
                    P.emit("gpsimd", lambda g, e=e, n0=n0, nt=nt, wu=wu: g.dma_start(
                        out=wu[:, 0:KC * nt * 128].rearrange("p (k n) -> p k n", k=KC),
                        in_=upT_d[e:e + 1, :, n0:n0 + nt * 128].rearrange("o (k p) n -> p (o k) n", p=128)),
                        waits=waits_, incs=[(qsem, 16)])
                    P.emit("gpsimd", lambda g, e=e, n0=n0, nt=nt, wd=wd: g.dma_start(
                        out=wd[:, 0:nt * D].rearrange("p (k c) -> p k c", k=nt),
                        in_=dnT_d[e:e + 1, n0:n0 + nt * 128, :].rearrange("o (k p) c -> p (o k) c", p=128)),
                        waits=[], incs=[(qsem, 16)])
                    return now(qsem)

                mTh_tok = [None]

                def load_m1(e, ch, slot, waits_):
                    P.emit("scalar", lambda s, e=e, ch=ch, slot=slot: s.dma_start(
                        out=m1r[slot][0:1, :],
                        in_=mTcF[e * 16 + 4 * ch: e * 16 + 4 * ch + 4, 0:128]),
                           waits=waits_ + [mTh_tok[0]], incs=[(dM[slot], 16)])
                    return now(dM[slot])

                for half in range(2):
                    t0 = half * HTILES
                    for i in range(HTILES):
                        r = i % 2
                        ci, jj = i // 4, i % 4
                        toks = emit_ln_tile(war, src_dram, t0 + i, r)
                        evac_toks = []
                        for k in range(KC):
                            tp = (i * KC + k) % 2
                            P.emit("tensor", lambda t, k=k, r=r, tp=tp:
                                   t.transpose(TPy[tp], ybuf[r][:, k * 128:(k + 1) * 128], ident[:, :]),
                                   waits=[toks["y"], ident_tok] + war.pop(("tpy", tp), []),
                                   incs=[(pe, 1)])
                            tp_tok = now(pe)
                            P.emit("scalar", lambda s, k=k, tp=tp:
                                   s.copy(yTf32[:, k * 128:(k + 1) * 128], TPy[tp]),
                                   waits=[tp_tok] + war.pop("yTf32", []) +
                                         (war.pop("dacc", []) if (i == 0 and k == 0) else []),
                                   incs=[(act, 1)])
                            ev = now(act)
                            evac_toks.append(ev)
                            war[("tpy", tp)] = [ev]
                            P.emit("gpsimd", lambda g, k=k, i=i:
                                   g.tensor_copy(yTh[:, k * HALF + i * 128: k * HALF + (i + 1) * 128],
                                                 yTf32[:, k * 128:(k + 1) * 128]),
                                   waits=[ev] + (war.pop("yTh", []) if (i == 0 and k == 0) else []),
                                   incs=[(gp, 1)])
                        war[("ybuf", r)] = [now(pe)]
                        war[("xr", r)] = [toks["y"], now(vec)]
                        g8w = war.pop("g8", [])
                        for k in range(KC):
                            P.emit("tensor", lambda t, k=k:
                                   t.matmul(G8, yTf32[:, k * 128:(k + 1) * 128],
                                            gw_sb[:, k * 8:(k + 1) * 8],
                                            start=(k == 0), stop=(k == KC - 1), skip_group_check=True),
                                   waits=([evac_toks[KC - 1], init_dW] + g8w) if k == 0 else [],
                                   incs=[(pe, 1)] if k == KC - 1 else [])
                        gate_tok = now(pe)
                        war["yTf32"] = [gate_tok, now(gp)]
                        P.emit("scalar", lambda s: s.activation(sc_t[:, :], G8, AF.Sigmoid),
                               waits=[gate_tok], incs=[(act, 1)])
                        sig_tok = now(act)
                        war["g8"] = [sig_tok]
                        rw = war.pop("route", [])
                        P.emit("vector", lambda v: v.tensor_add(bb_t[:, :], sc_t[:, :], biasb[:, :]),
                               waits=[sig_tok, init_dW] + rw, incs=[(vec, 1)])
                        P.emit("vector", lambda v: v.reduce_max(v1_t[:, :], bb_t[:, :], axis=AX.X),
                               waits=[now(vec)], incs=[(vec, 1)])
                        P.emit("vector", lambda v: v.tensor_scalar(t1_t[:, :], bb_t[:, :], v1_t[:, 0:1],
                                                                   -100.0, ALU.is_ge, ALU.mult),
                               waits=[now(vec)], incs=[(vec, 1)])
                        P.emit("vector", lambda v: v.tensor_tensor(t1_t[:, :], t1_t[:, :], iota8[:, :],
                                                                   ALU.add),
                               waits=[init_gp, now(vec)], incs=[(vec, 1)])
                        P.emit("vector", lambda v: v.tensor_reduce(i1_t[:, :], t1_t[:, :], axis=AX.X,
                                                                   op=ALU.min),
                               waits=[now(vec)], incs=[(vec, 1)])
                        P.emit("vector", lambda v: v.tensor_single_scalar(oh_t[:, :], t1_t[:, :],
                                                                          i1_t[:, 0:1], ALU.is_equal),
                               waits=[now(vec)], incs=[(vec, 1)])
                        P.emit("vector", lambda v: v.scalar_tensor_tensor(bb_t[:, :], oh_t[:, :], -1e30,
                                                                          bb_t[:, :], ALU.mult, ALU.add),
                               waits=[now(vec)], incs=[(vec, 1)])
                        P.emit("vector", lambda v: v.reduce_max(v1_t[:, :], bb_t[:, :], axis=AX.X),
                               waits=[now(vec)], incs=[(vec, 1)])
                        P.emit("vector", lambda v: v.tensor_scalar(t1_t[:, :], bb_t[:, :], v1_t[:, 0:1],
                                                                   -100.0, ALU.is_ge, ALU.mult),
                               waits=[now(vec)], incs=[(vec, 1)])
                        P.emit("vector", lambda v: v.tensor_tensor(t1_t[:, :], t1_t[:, :], iota8[:, :],
                                                                   ALU.add),
                               waits=[now(vec)], incs=[(vec, 1)])
                        P.emit("vector", lambda v: v.tensor_reduce(i1_t[:, :], t1_t[:, :], axis=AX.X,
                                                                   op=ALU.min),
                               waits=[now(vec)], incs=[(vec, 1)])
                        P.emit("vector", lambda v: v.tensor_single_scalar(t1_t[:, :], t1_t[:, :],
                                                                          i1_t[:, 0:1], ALU.is_equal),
                               waits=[now(vec)], incs=[(vec, 1)])
                        P.emit("vector", lambda v: v.tensor_add(oh_t[:, :], oh_t[:, :], t1_t[:, :]),
                               waits=[now(vec)], incs=[(vec, 1)])
                        P.emit("vector", lambda v, i=i: v.tensor_mul(mmF[:, i:i + 113:16],
                                                                     sc_t[:, :], oh_t[:, :]),
                               waits=[now(vec)] + (war.pop("mmF", []) if i == 0 else []),
                               incs=[(vec, 1)])
                        m_tok = now(vec)
                        war["route"] = [m_tok]
                        if i == HTILES - 1:
                            P.emit("tensor", lambda t: t.transpose(TPm, mmF[:, :], ident[:, :]),
                                   waits=[m_tok] + war.pop("tpm", []) + war.pop("mb", []),
                                   incs=[(pe, 1)])
                            tpm_tok = now(pe)
                            P.emit("scalar", lambda s: s.copy(mTcF[:, :], TPm),
                                   waits=[tpm_tok] + war.pop("mTc", []),
                                   incs=[(act, 1)])
                            mTh_tok[0] = now(act)
                            war["tpm"] = [mTh_tok[0]]
                            war["mmF"] = [tpm_tok]
                    ycast_all = now(gp)

                    ptokA = {0: load_piece(0, 0, war.pop(("WA", 0), []))}
                    ptokB = load_piece(0, 1, war.pop("WB", []))
                    ptokA[1] = load_piece(1, 0, war.pop(("WA", 1), []))
                    m1toks = {}
                    gctr = 0

                    for e in range(NE):
                        routed = e < 8
                        lastAuse = []
                        lastBuse = []
                        for ch in range(HCHUNKS):
                            yp = ch % 2
                            if routed:
                                g_ = gctr
                                gctr += 1
                                slot = g_ % 2
                                m1toks[g_] = load_m1(e, ch, slot, war.pop(("m1", slot), []))
                                mbw = war.pop("mb", [])
                                if g_ == 0:
                                    mbw = mbw + [mTh_tok[0]]
                                P.emit("tensor", lambda t, slot=slot:
                                       t.matmul(MB[:, :], ones1[0:1, :], m1r[slot][0:1, :],
                                                start=True, stop=True, skip_group_check=True),
                                       waits=[m1toks[g_], init_gp] + mbw, incs=[(pe, 1)])
                                mb_tok = now(pe)
                                war[("m1", slot)] = [mb_tok]
                                P.emit("scalar", lambda s: s.copy(mbs[:, :], MB[:, :]),
                                       waits=[mb_tok] + war.pop("mbs", []), incs=[(act, 1)])
                                mbs_tok = now(act)
                                war["mb"] = [mbs_tok]
                                yew = war.pop(("ye", yp), [])
                                for k in range(KC):
                                    P.emit("gpsimd", lambda g, k=k, ch=ch, yp=yp:
                                           g.tensor_tensor(ye[yp][:, k * CHUNK:(k + 1) * CHUNK],
                                                           yTh[:, k * HALF + ch * CHUNK:
                                                               k * HALF + (ch + 1) * CHUNK],
                                                           mbs[:, :], ALU.mult),
                                           waits=([mbs_tok, ycast_all] + yew) if k == 0 else [],
                                           incs=[(gp, 1)] if k == KC - 1 else [])
                                ye_tok = now(gp)
                                war["mbs"] = [ye_tok]
                            for piece in range(2):
                                if piece == 0:
                                    nt, wu, wd, wtok = NT_A, WupA[e % 2], WdnA[e % 2], ptokA[e]
                                else:
                                    nt, wu, wd, wtok = NT_B, WupB, WdnB, ptokB
                                hb = piece
                                sq_last = None
                                for ntile in range(nt):
                                    ub = ntile % 2
                                    uw = war.pop(("U", ub), [])
                                    for k in range(KC):
                                        if routed:
                                            movap = (lambda k=k, yp=yp:
                                                     ye[yp][:, k * CHUNK:(k + 1) * CHUNK])
                                            mov_tok = ye_tok
                                        else:
                                            movap = (lambda k=k, ch=ch:
                                                     yTh[:, k * HALF + ch * CHUNK:
                                                         k * HALF + (ch + 1) * CHUNK])
                                            mov_tok = ycast_all
                                        P.emit("tensor", lambda t, ntile=ntile, k=k, nt=nt, wu=wu,
                                               ub=ub, movap=movap:
                                               t.matmul(U[ub][:, :],
                                                        wu[:, (k * nt + ntile) * 128:
                                                           (k * nt + ntile + 1) * 128],
                                                        movap(), start=(k == 0), stop=(k == KC - 1),
                                                        skip_group_check=True),
                                               waits=([wtok, mov_tok] + uw) if k == 0 else [],
                                               incs=[(pe, 1)] if k == KC - 1 else [])
                                    u_tok = now(pe)
                                    P.emit("scalar", lambda s, ntile=ntile, ub=ub, hb=hb:
                                           s.activation(hh[hb][:, ntile * CHUNK:(ntile + 1) * CHUNK],
                                                        U[ub][:, :], AF.Relu),
                                           waits=[u_tok] + (war.pop(("hhh", hb), [])
                                                            if ntile == 0 else []),
                                           incs=[(act, 1)])
                                    relu_tok = now(act)
                                    war[("U", ub)] = [relu_tok]
                                    P.emit("vector", lambda v, ntile=ntile, hb=hb:
                                           v.tensor_mul(hh[hb][:, ntile * CHUNK:(ntile + 1) * CHUNK],
                                                        hh[hb][:, ntile * CHUNK:(ntile + 1) * CHUNK],
                                                        hh[hb][:, ntile * CHUNK:(ntile + 1) * CHUNK]),
                                           waits=[relu_tok], incs=[(vec, 1)])
                                    sq_last = now(vec)
                                for tt in range(4):
                                    for cc in range(2):
                                        db = (tt * 2 + cc) % 2
                                        dw_ = war.pop(("D", db), [])
                                        for ntile in range(nt):
                                            P.emit("tensor", lambda t, tt=tt, cc=cc, ntile=ntile,
                                                   nt=nt, wd=wd, hb=hb, db=db:
                                                   t.matmul(DD[db][:, :],
                                                            hh[hb][:, ntile * CHUNK + tt * 128:
                                                                   ntile * CHUNK + (tt + 1) * 128],
                                                            wd[:, ntile * D + cc * 512:
                                                               ntile * D + (cc + 1) * 512],
                                                            start=(ntile == 0), stop=(ntile == nt - 1),
                                                            skip_group_check=True),
                                                   waits=([sq_last] + dw_) if ntile == 0 else [],
                                                   incs=[(pe, 1)] if ntile == nt - 1 else [])
                                        d_tok = now(pe)
                                        dslc = (lambda ch=ch, tt=tt, cc=cc:
                                                dacc[:, (ch * 4 + tt) * D + cc * 512:
                                                     (ch * 4 + tt) * D + (cc + 1) * 512])
                                        dchain = war.pop(("dch", ch, tt, cc), [])
                                        if e == 0 and piece == 0:
                                            P.emit("vector", lambda v, dslc=dslc, db=db:
                                                   v.tensor_copy(dslc(), DD[db][:, :]),
                                                   waits=[d_tok] + ([ycast_all]
                                                                    if (ch == 0 and tt == 0 and cc == 0)
                                                                    else []),
                                                   incs=[(vec, 1)])
                                        else:
                                            P.emit("vector", lambda v, dslc=dslc, db=db:
                                                   v.tensor_tensor(dslc(), dslc(), DD[db][:, :],
                                                                   ALU.add),
                                                   waits=[d_tok] + dchain, incs=[(vec, 1)])
                                        war[("D", db)] = [now(vec)]
                                        war[("dch", ch, tt, cc)] = [now(vec)]
                                if piece == 0:
                                    lastAuse = [now(pe)]
                                else:
                                    lastBuse = [now(pe)]
                                war[("hhh", hb)] = [now(pe)]
                            if routed:
                                war[("ye", yp)] = [now(pe)]
                        war[("WA", e % 2)] = lastAuse
                        war["WB"] = lastBuse
                        if e + 2 < NE:
                            ptokA[e + 2] = load_piece(e + 2, 0, war.pop(("WA", e % 2)))
                        if e + 1 < NE:
                            ptokB = load_piece(e + 1, 1, war.pop("WB"))
                    dacc_done = now(vec)

                    if dbg and half == 0 and tag == "f1":
                        P.emit("gpsimd", lambda g: g.dma_start(out=dbg_d["yTh"][:, :], in_=yTh[:, :]),
                               waits=[dacc_done], incs=[(dDbg, 16)])
                        P.emit("gpsimd", lambda g: g.dma_start(out=dbg_d["mTc"][:, :], in_=mTcF[:, :]),
                               waits=[], incs=[(dDbg, 16)])
                        P.emit("gpsimd", lambda g: g.dma_start(out=dbg_d["dacc"][:, :], in_=dacc[:, :]),
                               waits=[], incs=[(dDbg, 16)])
                        P.emit("gpsimd", lambda g: g.dma_start(out=dbg_d["ye"][:, :], in_=ye[1][:, :]),
                               waits=[], incs=[(dDbg, 16)])

                    for i in range(HTILES):
                        r = i % 2
                        P.emit("sync", lambda sy, i=i, r=r, t0=t0: sy.dma_start(
                            out=xr[r][:, :], in_=src_dram[(t0 + i) * 128:(t0 + i + 1) * 128, :]),
                               waits=[dacc_done] + war.pop(("xr", r), []), incs=[(dXs[r], 16)])
                        ld = now(dXs[r])
                        tap_guards = []
                        if dbg and half == 0 and tag == "f1" and i == 0:
                            P.emit("gpsimd", lambda g: g.dma_start(out=dbg_d["xs0"][0:128, :],
                                                                   in_=xr[r][:, :]),
                                   waits=[ld], incs=[(dDbg, 16)])
                            tap_guards.append(now(dDbg))
                        P.emit("vector", lambda v, i=i, r=r: v.scalar_tensor_tensor(
                            ybuf[r][:, :], dacc[:, i * D:(i + 1) * D], 0.5, xr[r][:, :],
                            ALU.mult, ALU.add),
                               waits=[ld] + war.pop(("ybuf", r), []) +
                                     (war.pop("mbs", []) if i == 0 else []),
                               incs=[(vec, 1)])
                        stt_tok = now(vec)
                        yb_guards = []
                        if dbg and half == 0 and tag == "f1" and i == 0:
                            P.emit("gpsimd", lambda g: g.dma_start(out=dbg_d["xs0"][128:256, :],
                                                                   in_=ybuf[0][:, :]),
                                   waits=[stt_tok], incs=[(dDbg, 16)])
                            yb_guards.append(now(dDbg))
                        war[("xr", r)] = [stt_tok] + tap_guards
                        P.emit("sync", lambda sy, i=i, r=r, t0=t0: sy.dma_start(
                            out=dst_dram[(t0 + i) * 128:(t0 + i + 1) * 128, :], in_=ybuf[r][:, :]),
                               waits=[stt_tok], incs=[(dYs[r], 16)])
                        war[("ybuf", r)] = [now(dYs[r])] + yb_guards
                    if dbg and half == 0 and tag == "f1":
                        P.emit("gpsimd", lambda g: g.dma_start(out=dbg_d["xs0"][:, :],
                                                               in_=dst_dram[0:HALF, :]),
                               waits=[now(dYs[0]), now(dYs[1])], incs=[(dDbg, 16)])
                        P.emit("gpsimd", lambda g: g.dma_start(out=dbg_d["dacc2"][:, :], in_=dacc[:, :]),
                               waits=[now(vec)], incs=[(dDbg, 16)])
                    war["dacc"] = [now(vec)]
                    war["yTh"] = [now(pe), now(vec)]
                    war["mTc"] = [now(dM[0]), now(dM[1])]
            engine_barrier()

        def attn_phase(src_dram, dst_dram):
            with ExitStack() as ph:
                psb = lambda n_, sh, dt: ph.enter_context(nc.sbuf_tensor(n_ + "_at", sh, dt))
                iota512 = psb("iota512", [128, 512], F32)
                wqkv_sb = psb("wqkv", [128, KC * 192], BF16)
                wout_sb = psb("wout", [64, D], BF16)
                zT = [psb(f"zT{i}", [128, KC * CHUNK], BF16) for i in range(2)]
                qkT = psb("qkT", [128, NB * SEQ], BF16)
                v_sb = psb("v_sb", [128, NTILES * 64], BF16)
                vI = psb("vI", [128, 16 * 128], BF16)
                kc_r = psb("kc_r", [1, 2 * SEQ], BF16)
                qc_r = psb("qc_r", [1, 2 * SEQ], BF16)
                Esb = [psb(f"Esb{i}", [128, 4 * 512], BF16) for i in range(2)]
                masks = psb("masks", [128, 4 * 512], BF16)
                ndbuf = psb("ndbuf", [128, SEQ], F32)
                rden = psb("rden", [64, SEQ], F32)
                oT = psb("oT", [64, SEQ], BF16)
                stg = [psb(f"stg{i}", [2, 512], F32) for i in range(4)]

                TPz = [PB[5][:, 0:128], PB[6][:, 0:128]]
                QK = PB[7]
                Vp = PB[4][:, 0:64]
                SS = [PB[0], PB[1], PB[2], PB[3]]
                ND = [PB[5][0:2, :], PB[6][0:2, :]]
                OP = [PB[4], PB[7]]

                war = {}
                P.emit("gpsimd", lambda g: g.dma_start(
                    out=wqkv_sb[:, :].rearrange("p (k f) -> p k f", k=KC),
                    in_=wqkvT_d[:, :].rearrange("(k p) f -> p k f", p=128)), incs=[(dInit, 16)])
                P.emit("gpsimd", lambda g: g.dma_start(out=wout_sb[:, :], in_=woutT_d[:, :]),
                       incs=[(dInit, 16)])
                wqkv_tok = now(dInit)
                wout_tok = now(dInit)
                P.emit("gpsimd", lambda g: g.iota(iota512[:, :], pattern=[[1, 512]], base=0,
                                                  channel_multiplier=-1,
                                                  allow_small_or_imprecise_dtypes=True),
                       incs=[(gp, 1)])
                io5_tok = now(gp)
                for m in range(4):
                    P.emit("vector", lambda v, m=m: v.tensor_single_scalar(
                        masks[:, m * 512:(m + 1) * 512], iota512[:, :], float(128.0 * m), ALU.is_ge),
                           waits=[io5_tok], incs=[(vec, 1)])
                mask_tok = now(vec)

                for ch8 in range(8):
                    zr = ch8 % 2
                    last_evac = None
                    for j in range(4):
                        i = ch8 * 4 + j
                        r = i % 2
                        toks = emit_ln_tile(war, src_dram, i, r)
                        for k in range(KC):
                            tp = (i * KC + k) % 2
                            P.emit("tensor", lambda t, k=k, r=r, tp=tp:
                                   t.transpose(TPz[tp], ybuf[r][:, k * 128:(k + 1) * 128], ident[:, :]),
                                   waits=[toks["y"], ident_tok] + war.pop(("tpz", tp), []),
                                   incs=[(pe, 1)])
                            tp_tok = now(pe)
                            P.emit("scalar", lambda s, k=k, j=j, zr=zr, tp=tp:
                                   s.copy(zT[zr][:, k * CHUNK + j * 128: k * CHUNK + (j + 1) * 128],
                                          TPz[tp]),
                                   waits=[tp_tok] + (war.pop(("zslot", zr), [])
                                                     if (k == 0 and j == 0) else []),
                                   incs=[(act, 1)])
                            war[("tpz", tp)] = [now(act)]
                            last_evac = now(act)
                        war[("ybuf", r)] = [now(pe)]
                        war[("xr", r)] = [toks["y"], now(vec)]
                        vw = war.pop("vp", [])
                        for k in range(KC):
                            P.emit("tensor", lambda t, k=k, zr=zr, j=j:
                                   t.matmul(Vp, zT[zr][:, k * CHUNK + j * 128: k * CHUNK + (j + 1) * 128],
                                            wqkv_sb[:, k * 192 + 128: k * 192 + 192],
                                            start=(k == 0), stop=(k == KC - 1), skip_group_check=True),
                                   waits=([wqkv_tok, last_evac] + vw) if k == 0 else [],
                                   incs=[(pe, 1)] if k == KC - 1 else [])
                        vp_tok = now(pe)
                        P.emit("scalar", lambda s, i=i: s.copy(v_sb[:, i * 64:(i + 1) * 64], Vp),
                               waits=[vp_tok], incs=[(act, 1)])
                        war["vp"] = [now(act)]
                    qkw = war.pop("qk", [])
                    for k in range(KC):
                        P.emit("tensor", lambda t, k=k, zr=zr:
                               t.matmul(QK[:, :], wqkv_sb[:, k * 192: k * 192 + 128],
                                        zT[zr][:, k * CHUNK:(k + 1) * CHUNK],
                                        start=(k == 0), stop=(k == KC - 1), skip_group_check=True),
                               waits=([wqkv_tok, last_evac] + qkw) if k == 0 else [],
                               incs=[(pe, 1)] if k == KC - 1 else [])
                    qk_tok = now(pe)
                    b = ch8 // 4
                    pos = (ch8 % 4) * CHUNK
                    P.emit("scalar", lambda s, b=b, pos=pos: s.activation(
                        qkT[0:64, b * SEQ + pos: b * SEQ + pos + CHUNK], QK[0:64, :], AF.Identity,
                        scale=f32(0.125)),
                           waits=[qk_tok], incs=[(act, 1)])
                    P.emit("scalar", lambda s, b=b, pos=pos: s.copy(
                        qkT[64:128, b * SEQ + pos: b * SEQ + pos + CHUNK], QK[64:128, :]),
                           incs=[(act, 1)])
                    war["qk"] = [now(act)]
                    war[("zslot", zr)] = [now(pe)]
                qkT_tok = now(act)
                vsb_tok = now(act)

                for b in range(NB):
                    P.emit("gpsimd", lambda g: g.memset(vI[:, :], 1.0),
                           waits=[vsb_tok] + war.pop("vI", []), incs=[(gp, 1)])
                    vi0 = now(gp)
                    for jb in range(16):
                        P.emit("vector", lambda v, b=b, jb=jb: v.tensor_copy(
                            vI[:, jb * 128:(jb + 1) * 128].rearrange("p (h two) -> p h two", two=2)[:, :, 0:1],
                            v_sb[:, (b * 16 + jb) * 64:(b * 16 + jb + 1) * 64].rearrange("p (h o) -> p h o", o=1)),
                               waits=[vi0] if jb == 0 else [], incs=[(vec, 1)])
                    vI_tok = now(vec)

                    def load_rows(h, slot, waits_, b=b):
                        P.emit("scalar", lambda s, h=h, b=b, slot=slot: s.dma_start(
                            out=kc_r[0:1, slot * SEQ:(slot + 1) * SEQ],
                            in_=qkT[64 + h:65 + h, b * SEQ:(b + 1) * SEQ]),
                               waits=waits_ + [qkT_tok], incs=[(dR[slot], 16)])
                        P.emit("scalar", lambda s, h=h, b=b, slot=slot: s.dma_start(
                            out=qc_r[0:1, slot * SEQ:(slot + 1) * SEQ],
                            in_=qkT[h:h + 1, b * SEQ:(b + 1) * SEQ]),
                               waits=[], incs=[(dR[slot], 16)])
                        return now(dR[slot])

                    rowtok = {0: load_rows(0, 0, war.pop(("rows", 0), []))}
                    rowtok[1] = load_rows(1, 1, war.pop(("rows", 1), []))
                    for h in range(64):
                        slot = h % 2
                        rtok = rowtok.pop(h)
                        for ic in range(4):
                            ndb = ic % 2
                            ndw = war.pop(("nd", ndb), [])
                            njb = 4 * (ic + 1)
                            for g in range(njb // 2):
                                pb = g % 2
                                eb = g % 2
                                ssw = war.pop(("SSg", pb), [])
                                for j2 in range(2):
                                    jb = 2 * g + j2
                                    P.emit("tensor", lambda t, ic=ic, jb=jb, slot=slot, pb=pb, j2=j2:
                                           t.matmul(SS[2 * pb + j2][:, :],
                                                    kc_r[0:1, slot * SEQ + jb * 128:
                                                         slot * SEQ + (jb + 1) * 128],
                                                    qc_r[0:1, slot * SEQ + ic * 512:
                                                         slot * SEQ + (ic + 1) * 512],
                                                    start=True, stop=True, skip_group_check=True),
                                           waits=([rtok] + ssw) if j2 == 0 else [],
                                           incs=[(pe, 1)] if j2 == 1 else [])
                                s_tok = now(pe)
                                ebw = war.pop(("Ebuf", eb), [])
                                P.emit("scalar", lambda s, eb=eb, pb=pb: s.activation(
                                    Esb[eb][:, 0:1024], PBbig[:, 1024 * pb:1024 * (pb + 1)],
                                    AF.Exp, bias=f32(-2.0)),
                                       waits=[s_tok] + ebw, incs=[(act, 1)])
                                e_tok = now(act)
                                war[("SSg", pb)] = [e_tok]
                                if g >= njb // 2 - 2:
                                    mof = (g - (njb // 2 - 2)) * 1024
                                    P.emit("vector", lambda v, eb=eb, mof=mof: v.tensor_tensor(
                                        Esb[eb][:, 0:1024], Esb[eb][:, 0:1024],
                                        masks[:, mof:mof + 1024], ALU.mult),
                                           waits=[e_tok, mask_tok], incs=[(vec, 1)])
                                    e_tok = now(vec)
                                for j2 in range(2):
                                    jb = 2 * g + j2
                                    P.emit("tensor", lambda t, h=h, jb=jb, eb=eb, ndb=ndb, njb=njb, j2=j2:
                                           t.matmul(ND[ndb], vI[:, jb * 128 + 2 * h: jb * 128 + 2 * h + 2],
                                                    Esb[eb][:, 512 * j2:512 * (j2 + 1)],
                                                    start=(jb == 0), stop=(jb == njb - 1),
                                                    skip_group_check=True),
                                           waits=([vI_tok, e_tok] + (ndw if jb == 0 else []))
                                                 if j2 == 0 else [],
                                           incs=[(pe, 1)] if j2 == 1 else [])
                                war[("Ebuf", eb)] = [now(pe)]
                            nd_tok = now(pe)
                            sg = (h * 4 + ic) % 4
                            P.emit("vector", lambda v, ndb=ndb, sg=sg: v.tensor_copy(stg[sg][:, :], ND[ndb]),
                                   waits=[nd_tok] + war.pop(("stg", sg), []), incs=[(vec, 1)])
                            st_tok = now(vec)
                            war[("nd", ndb)] = [st_tok]
                            P.emit("scalar", lambda s, h=h, ic=ic, sg=sg: s.dma_start(
                                out=ndbuf[h:h + 1, ic * 512:(ic + 1) * 512], in_=stg[sg][0:1, :]),
                                   waits=[st_tok] + war.pop("hscat", []) +
                                         (war.pop("ndbuf", []) if (h == 0 and ic == 0) else []),
                                   incs=[(dG[sg], 16)])
                            P.emit("scalar", lambda s, h=h, ic=ic, sg=sg: s.dma_start(
                                out=ndbuf[64 + h:65 + h, ic * 512:(ic + 1) * 512], in_=stg[sg][1:2, :]),
                                   waits=[], incs=[(dG[sg], 16)])
                            war[("stg", sg)] = [now(dG[sg])]
                            war["hscat"] = [now(dG[sg])]
                        if h + 2 < 64:
                            rowtok[h + 2] = load_rows(h + 2, slot, [now(pe)])
                    scat_toks = [now(q) for q in dG]
                    P.emit("vector", lambda v: v.reciprocal(rden[:, :], ndbuf[64:128, :]),
                           waits=scat_toks + war.pop("rden", []), incs=[(vec, 1)])
                    P.emit("vector", lambda v: v.tensor_mul(oT[:, :], ndbuf[0:64, :], rden[:, :]),
                           waits=[now(vec)] + war.pop("oT", []), incs=[(vec, 1)])
                    oT_tok = now(vec)
                    war["ndbuf"] = [oT_tok]
                    war["rden"] = [oT_tok]
                    for j in range(16):
                        i = b * 16 + j
                        r = i % 2
                        optoks = []
                        for cc in range(2):
                            opw = war.pop(("OP", cc), [])
                            P.emit("tensor", lambda t, j=j, cc=cc:
                                   t.matmul(OP[cc][:, :], oT[:, j * 128:(j + 1) * 128],
                                            wout_sb[:, cc * 512:(cc + 1) * 512],
                                            start=True, stop=True, skip_group_check=True),
                                   waits=[oT_tok, wout_tok] + opw, incs=[(pe, 1)])
                            optoks.append(now(pe))
                        P.emit("sync", lambda sy, i=i, r=r: sy.dma_start(
                            out=xr[r][:, :], in_=src_dram[i * 128:(i + 1) * 128, :]),
                               waits=war.pop(("xr", r), []), incs=[(dXs[r], 16)])
                        ld = now(dXs[r])
                        ybw = war.pop(("ybuf", r), [])
                        for cc in range(2):
                            P.emit("vector", lambda v, r=r, cc=cc: v.tensor_tensor(
                                ybuf[r][:, cc * 512:(cc + 1) * 512], xr[r][:, cc * 512:(cc + 1) * 512],
                                OP[cc][:, :], ALU.add),
                                   waits=[ld, optoks[cc]] + (ybw if cc == 0 else []),
                                   incs=[(vec, 1)])
                            war[("OP", cc)] = [now(vec)]
                        stt_tok = now(vec)
                        war[("xr", r)] = [stt_tok]
                        P.emit("sync", lambda sy, i=i, r=r: sy.dma_start(
                            out=dst_dram[i * 128:(i + 1) * 128, :], in_=ybuf[r][:, :]),
                               waits=[stt_tok], incs=[(dYs[r], 16)])
                        war[("ybuf", r)] = [now(dYs[r])]
                    war["oT"] = [now(pe)]
                    war["vI"] = [now(pe)]
                    war[("rows", 0)] = [now(pe)]
                    war[("rows", 1)] = [now(pe)]
            engine_barrier()

        if "F1" in phases:
            ffn_phase(x_d, xs_d, "f1")
            a_src = xs_d
        else:
            a_src = x_d
        if "A" in phases:
            attn_phase(a_src, xs_d)
            f2_src = xs_d
        else:
            f2_src = a_src
        if "F2" in phases:
            ffn_phase(f2_src, out_d, "f2")
        else:
            prev = None
            for i in range(NTILES):
                P.emit("sync", lambda sy, i=i: sy.dma_start(out=xr[0][:, :],
                                                            in_=f2_src[i * 128:(i + 1) * 128, :]),
                       waits=[prev] if prev else [], incs=[(dXs[0], 16)])
                ld = now(dXs[0])
                P.emit("sync", lambda sy, i=i: sy.dma_start(out=out_d[i * 128:(i + 1) * 128, :],
                                                            in_=xr[0][:, :]),
                       waits=[ld], incs=[(dYs[0], 16)])
                prev = now(dYs[0])

        P.emit("sync", lambda sy: sy.nop(), waits=[now(dYs[0]), now(dYs[1])])

        with ExitStack() as sctx:
            for s in all_sems:
                s.h = sctx.enter_context(nc.semaphore(s.name))
            block = sctx.enter_context(nc.Block())
            block.tensor(lambda t: P.run("tensor", t))
            block.vector(lambda v: P.run("vector", v))
            block.scalar(lambda s: P.run("scalar", s))
            block.gpsimd(lambda g: P.run("gpsimd", g))
            block.sync(lambda sy: P.run("sync", sy))
    return nc


def kernel(L0_attn_x, L0_attn_w_qkv, L0_attn_w_out, L0_ffn_gate_w, L0_ffn_expert_bias,
           L0_ffn_w_up, L0_ffn_w_down, L0_ffn_shared_w_up, L0_ffn_shared_w_down):
    x = np.ascontiguousarray(np.asarray(L0_attn_x, dtype=np.float32))
    wqkvT = np.ascontiguousarray(np.asarray(L0_attn_w_qkv, np.float32).T)
    woutT = np.ascontiguousarray(np.asarray(L0_attn_w_out, np.float32).T)
    gwT = np.ascontiguousarray(np.asarray(L0_ffn_gate_w, np.float32).T)
    biasbc = np.ascontiguousarray(
        np.tile(np.asarray(L0_ffn_expert_bias, np.float32)[None, :], (128, 1)))
    up = np.asarray(L0_ffn_w_up, np.float32)
    dn = np.asarray(L0_ffn_w_down, np.float32)
    sup = np.asarray(L0_ffn_shared_w_up, np.float32)
    sdn = np.asarray(L0_ffn_shared_w_down, np.float32)
    upT = np.ascontiguousarray(np.concatenate([up.transpose(0, 2, 1), sup.T[None]], axis=0))
    dnT = np.ascontiguousarray(np.concatenate([dn.transpose(0, 2, 1), sdn.T[None]], axis=0))

    nc = build_nc()
    in_maps = []
    for c in range(NCORES):
        xs = np.ascontiguousarray(x[2 * c:2 * c + 2].reshape(T, D))
        in_maps.append(dict(x=xs, wqkvT=wqkvT, woutT=woutT, gwT=gwT, biasbc=biasbc,
                            upT=upT, dnT=dnT))
    res = run_bass_kernel_spmd(nc, in_maps, core_ids=list(range(NCORES)))
    out = np.concatenate([np.asarray(res.results[c]["out"]).reshape(2, SEQ, D)
                          for c in range(NCORES)], axis=0)
    return out.astype(np.float32)


# version 04
# speedup vs baseline: 1.4236x; 1.2565x over previous
import numpy as np
from contextlib import ExitStack

import concourse.bass as bass
import concourse.mybir as mybir
from concourse.bass_utils import run_bass_kernel_spmd

F32 = mybir.dt.float32
BF16 = mybir.dt.bfloat16
AF = mybir.ActivationFunctionType
ALU = mybir.AluOpType
AX = mybir.AxisListType
f32 = np.float32

NCORES = 8
D = 1024
KC = 8
HID = 1408
NH_A, NT_A = 640, 5
NH_B, NT_B = 768, 6
NE = 9
SEQ = 2048
NB = 2
T = NB * SEQ
NTILES = T // 128
HALF = 2048
HTILES = HALF // 128
CHUNK = 512
HCHUNKS = HALF // CHUNK


class Sem:
    __slots__ = ("name", "v", "h")

    def __init__(self, name):
        self.name, self.v, self.h = name, 0, None


class Plan:
    def __init__(self):
        self.ops = []

    def emit(self, engine, fn, waits=(), incs=()):
        ws = [(s, v) for (s, v) in waits if v > 0]
        ri = []
        for (s, k) in incs:
            s.v += k
            ri.append((s, k))
        self.ops.append((engine, ws, fn, ri))

    def run(self, engine_name, eng):
        for e, ws, fn, ri in self.ops:
            if e != engine_name:
                continue
            merged = {}
            for (s, v) in ws:
                if s.name in merged:
                    merged[s.name] = (s, max(v, merged[s.name][1]))
                else:
                    merged[s.name] = (s, v)
            for (s, v) in merged.values():
                eng.wait_ge(s.h, v)
            inst = fn(eng)
            for (s, k) in ri:
                inst.then_inc(s.h, k)


def build_nc(phases=("F1", "A", "F2"), dbg=False):
    nc = bass.Bass("TRN2", target_bir_lowering=False, debug=False)
    dbg_d = {}
    if dbg:
        dbg_d["yTh"] = nc.dram_tensor("dbg_yTh", [128, KC * HALF], F32, kind="ExternalOutput").ap()
        dbg_d["mTc"] = nc.dram_tensor("dbg_mTc", [128, 128], F32, kind="ExternalOutput").ap()
        dbg_d["dacc"] = nc.dram_tensor("dbg_dacc", [128, HTILES * D], F32, kind="ExternalOutput").ap()
        dbg_d["ye"] = nc.dram_tensor("dbg_ye", [128, KC * CHUNK], F32, kind="ExternalOutput").ap()
        dbg_d["xs0"] = nc.dram_tensor("dbg_xs0", [HALF, D], F32, kind="ExternalOutput").ap()
        dbg_d["dacc2"] = nc.dram_tensor("dbg_dacc2", [128, HTILES * D], F32, kind="ExternalOutput").ap()

    x_d = nc.dram_tensor("x", [T, D], F32, kind="ExternalInput").ap()
    wqkvT_d = nc.dram_tensor("wqkvT", [D, 192], F32, kind="ExternalInput").ap()
    woutT_d = nc.dram_tensor("woutT", [64, D], F32, kind="ExternalInput").ap()
    gwT_d = nc.dram_tensor("gwT", [D, 8], F32, kind="ExternalInput").ap()
    biasbc_d = nc.dram_tensor("biasbc", [128, 8], F32, kind="ExternalInput").ap()
    upT_d = nc.dram_tensor("upT", [NE, D, HID], F32, kind="ExternalInput").ap()
    dnT_d = nc.dram_tensor("dnT", [NE, HID, D], F32, kind="ExternalInput").ap()
    out_d = nc.dram_tensor("out", [T, D], F32, kind="ExternalOutput").ap()
    xs_d = nc.dram_tensor("xs", [T, D], F32, kind="Internal").ap()

    P = Plan()
    dInit = Sem("dInit")
    dWA = [Sem("dWA0"), Sem("dWA1")]
    dWB = Sem("dWB")
    dXs = [Sem("dX0"), Sem("dX1")]
    dYs = [Sem("dY0"), Sem("dY1")]
    dM = [Sem("dM0"), Sem("dM1")]
    dR = [Sem("dR0"), Sem("dR1")]
    dG = [Sem(f"dG{i}") for i in range(4)]
    dDbg = Sem("dDbg")
    pe, act, vec, gp, bar = Sem("pe"), Sem("act"), Sem("vec"), Sem("gp"), Sem("bar")
    all_sems = [dInit, dWA[0], dWA[1], dWB, dXs[0], dXs[1], dYs[0], dYs[1],
                dM[0], dM[1], dR[0], dR[1]] + dG + [dDbg, pe, act, vec, gp, bar]

    def now(s):
        return (s, s.v)

    with ExitStack() as top:
        sb = lambda n_, sh, dt: top.enter_context(nc.sbuf_tensor(n_, sh, dt))
        ident = sb("ident", [128, 128], F32)
        ones1 = sb("ones1", [1, 128], BF16)
        iota8 = sb("iota8", [128, 8], F32)
        biasb = sb("biasb", [128, 8], F32)
        gw_sb = sb("gw_sb", [128, KC * 8], F32)
        xr = [sb(f"xr{i}", [128, D], F32) for i in range(2)]
        ybuf = [sb(f"ybuf{i}", [128, D], F32) for i in range(2)]
        stt_ = [sb(f"stt{i}", [128, 8], F32) for i in range(2)]
        PBbig = top.enter_context(nc.psum_tensor("PBbig", [128, 2048], F32))
        PB = [PBbig[:, 512 * i:512 * (i + 1)] for i in range(4)] + \
             [top.enter_context(nc.psum_tensor(f"PB{i}", [128, 512], F32)) for i in range(4, 8)]

        P.emit("gpsimd", lambda g: g.iota(ident[:, :], pattern=[[1, 128]], base=0,
                                          channel_multiplier=-1, allow_small_or_imprecise_dtypes=True),
               incs=[(gp, 1)])
        P.emit("gpsimd", lambda g: g.iota(iota8[:, :], pattern=[[1, 8]], base=0,
                                          channel_multiplier=0, allow_small_or_imprecise_dtypes=True),
               incs=[(gp, 1)])
        P.emit("gpsimd", lambda g: g.memset(ones1[:, :], 1.0), incs=[(gp, 1)])
        P.emit("gpsimd", lambda g: g.dma_start(out=biasb[:, :], in_=biasbc_d[:, :]), incs=[(dInit, 16)])
        P.emit("gpsimd", lambda g: g.dma_start(
            out=gw_sb[:, :].rearrange("p (k e) -> p k e", k=KC),
            in_=gwT_d[:, :].rearrange("(k p) e -> p k e", p=128)), incs=[(dInit, 16)])
        init_gp = now(gp)
        init_dW = now(dInit)
        P.emit("vector", lambda v: v.tensor_single_scalar(ident[:, :], ident[:, :], 0.0, ALU.is_equal),
               waits=[init_gp], incs=[(vec, 1)])
        ident_tok = now(vec)

        def engine_barrier():
            target = bar.v + 5
            dma_fence = {"sync": [now(q) for q in dXs + dYs],
                         "gpsimd": [now(dInit), now(dWA[0]), now(dWA[1]), now(dWB), now(dDbg)],
                         "scalar": [now(q) for q in dM + dR + dG]}
            for e in ("tensor", "vector", "scalar", "gpsimd", "sync"):
                P.emit(e, lambda eng: eng.nop(), waits=dma_fence.get(e, []), incs=[(bar, 1)])
            for e in ("tensor", "vector", "scalar", "gpsimd", "sync"):
                P.emit(e, lambda eng: eng.nop(), waits=[(bar, target)])

        def emit_ln_tile(war, src_dram, ti, r):
            st = stt_[r]
            s1, s2 = st[:, 0:1], st[:, 1:2]
            mu, var = st[:, 2:3], st[:, 3:4]
            rstd, nm = st[:, 4:5], st[:, 5:6]
            P.emit("sync", lambda sy: sy.dma_start(out=xr[r][:, :],
                                                   in_=src_dram[ti * 128:(ti + 1) * 128, :]),
                   waits=war.pop(("xr", r), []), incs=[(dXs[r], 16)])
            ld = now(dXs[r])
            stw = war.pop(("st", r), [])
            ybw = war.pop(("ybuf", r), [])
            P.emit("vector", lambda v: v.reduce_sum(s1, xr[r][:, :], axis=AX.X),
                   waits=[ld] + stw, incs=[(vec, 1)])
            s1_tok = now(vec)
            P.emit("scalar", lambda s: s.activation(ybuf[r][:, :], xr[r][:, :], AF.Square,
                                                    accum_out=s2),
                   waits=[ld] + stw + ybw, incs=[(act, 1)])
            s2_tok = now(act)
            P.emit("vector", lambda v: v.tensor_scalar_mul(mu, s1, 1.0 / D),
                   waits=[s1_tok], incs=[(vec, 1)])
            mu_tok = now(vec)
            P.emit("vector", lambda v: v.tensor_scalar_mul(var, s2, 1.0 / D),
                   waits=[s2_tok], incs=[(vec, 1)])
            P.emit("vector", lambda v: v.tensor_mul(s1, mu, mu), waits=[mu_tok], incs=[(vec, 1)])
            P.emit("vector", lambda v: v.tensor_sub(var, var, s1), waits=[now(vec)],
                   incs=[(vec, 1)])
            var_tok = now(vec)
            P.emit("scalar", lambda s: s.activation(rstd, var, AF.Sqrt, bias=f32(1e-5)),
                   waits=[var_tok], incs=[(act, 1)])
            sqr_tok = now(act)
            P.emit("vector", lambda v: v.reciprocal(rstd, rstd), waits=[sqr_tok], incs=[(vec, 1)])
            rstd_tok = now(vec)
            P.emit("vector", lambda v: v.scalar_tensor_tensor(nm, mu, -1.0, rstd, ALU.mult, ALU.mult),
                   waits=[rstd_tok], incs=[(vec, 1)])
            nm_tok = now(vec)
            P.emit("scalar", lambda s: s.activation(ybuf[r][:, :], xr[r][:, :], AF.Identity,
                                                    bias=nm, scale=rstd),
                   waits=[rstd_tok, nm_tok], incs=[(act, 1)])
            y_tok = now(act)
            war[("st", r)] = [y_tok]
            return {"ld": ld, "y": y_tok}

        def ffn_phase(src_dram, dst_dram, tag):
            with ExitStack() as ph:
                psb = lambda n_, sh, dt: ph.enter_context(nc.sbuf_tensor(n_ + tag, sh, dt))
                yTh = psb("yTh", [128, KC * HALF], BF16)
                mmF = psb("mmF", [128, 128], F32)
                mTcF = psb("mTcF", [128, 128], BF16)
                m1r = [psb(f"m1r{i}", [1, CHUNK], BF16) for i in range(2)]
                sc_t = psb("sc_t", [128, 8], F32)
                bb_t = psb("bb_t", [128, 8], F32)
                t1_t = psb("t1_t", [128, 8], F32)
                oh_t = psb("oh_t", [128, 8], F32)
                v1_t = psb("v1_t", [128, 1], F32)
                i1_t = psb("i1_t", [128, 1], F32)
                WupA = [psb(f"WupA{i}", [128, KC * NH_A], BF16) for i in range(2)]
                WdnA = [psb(f"WdnA{i}", [128, NT_A * D], BF16) for i in range(2)]
                WupB = psb("WupB", [128, KC * NH_B], BF16)
                WdnB = psb("WdnB", [128, NT_B * D], BF16)
                ye = [psb(f"ye{i}", [128, KC * CHUNK], BF16) for i in range(2)]
                hh = [psb("hhA", [128, NT_A * CHUNK], BF16), psb("hhB", [128, NT_B * CHUNK], BF16)]
                dacc = psb("dacc", [128, HTILES * D], F32)
                yTf32 = dacc

                U = [PB[0], PB[1]]
                DD = [PB[2], PB[3]]
                MB = PB[4]
                TPy = [PB[5][:, 0:128], PB[6][:, 0:128]]
                G8 = PB[7][:, 0:8]
                TPm = PB[4][0:128, 0:128]

                war = {}

                def load_piece(e, piece, waits_):
                    if piece == 0:
                        n0, nt, wu, wd = 0, NT_A, WupA[e % 2], WdnA[e % 2]
                        qsem = dWA[e % 2]
                    else:
                        n0, nt, wu, wd = NH_A, NT_B, WupB, WdnB
                        qsem = dWB
                    P.emit("gpsimd", lambda g, e=e, n0=n0, nt=nt, wu=wu: g.dma_start(
                        out=wu[:, 0:KC * nt * 128].rearrange("p (k n) -> p k n", k=KC),
                        in_=upT_d[e:e + 1, :, n0:n0 + nt * 128].rearrange("o (k p) n -> p (o k) n", p=128)),
                        waits=waits_, incs=[(qsem, 16)])
                    P.emit("gpsimd", lambda g, e=e, n0=n0, nt=nt, wd=wd: g.dma_start(
                        out=wd[:, 0:nt * D].rearrange("p (k c) -> p k c", k=nt),
                        in_=dnT_d[e:e + 1, n0:n0 + nt * 128, :].rearrange("o (k p) c -> p (o k) c", p=128)),
                        waits=[], incs=[(qsem, 16)])
                    return now(qsem)

                mTh_tok = [None]

                def load_m1(e, ch, slot, waits_):
                    P.emit("scalar", lambda s, e=e, ch=ch, slot=slot: s.dma_start(
                        out=m1r[slot][0:1, :],
                        in_=mTcF[e * 16 + 4 * ch: e * 16 + 4 * ch + 4, 0:128]),
                           waits=waits_ + [mTh_tok[0]], incs=[(dM[slot], 16)])
                    return now(dM[slot])

                for half in range(2):
                    t0 = half * HTILES
                    for i in range(HTILES):
                        r = i % 2
                        ci, jj = i // 4, i % 4
                        toks = emit_ln_tile(war, src_dram, t0 + i, r)
                        evac_toks = []
                        for k in range(KC):
                            tp = (i * KC + k) % 2
                            P.emit("tensor", lambda t, k=k, r=r, tp=tp:
                                   t.transpose(TPy[tp], ybuf[r][:, k * 128:(k + 1) * 128], ident[:, :]),
                                   waits=[toks["y"], ident_tok] + war.pop(("tpy", tp), []),
                                   incs=[(pe, 1)])
                            tp_tok = now(pe)
                            P.emit("scalar", lambda s, k=k, tp=tp:
                                   s.copy(yTf32[:, k * 128:(k + 1) * 128], TPy[tp]),
                                   waits=[tp_tok] + war.pop("yTf32", []) +
                                         (war.pop("dacc", []) if (i == 0 and k == 0) else []),
                                   incs=[(act, 1)])
                            ev = now(act)
                            evac_toks.append(ev)
                            war[("tpy", tp)] = [ev]
                            P.emit("gpsimd", lambda g, k=k, i=i:
                                   g.tensor_copy(yTh[:, k * HALF + i * 128: k * HALF + (i + 1) * 128],
                                                 yTf32[:, k * 128:(k + 1) * 128]),
                                   waits=[ev] + (war.pop("yTh", []) if (i == 0 and k == 0) else []),
                                   incs=[(gp, 1)])
                        war[("ybuf", r)] = [now(pe)]
                        war[("xr", r)] = [toks["y"], now(vec)]
                        g8w = war.pop("g8", [])
                        for k in range(KC):
                            P.emit("tensor", lambda t, k=k:
                                   t.matmul(G8, yTf32[:, k * 128:(k + 1) * 128],
                                            gw_sb[:, k * 8:(k + 1) * 8],
                                            start=(k == 0), stop=(k == KC - 1), skip_group_check=True),
                                   waits=([evac_toks[KC - 1], init_dW] + g8w) if k == 0 else [],
                                   incs=[(pe, 1)] if k == KC - 1 else [])
                        gate_tok = now(pe)
                        war["yTf32"] = [gate_tok, now(gp)]
                        P.emit("scalar", lambda s: s.activation(sc_t[:, :], G8, AF.Sigmoid),
                               waits=[gate_tok], incs=[(act, 1)])
                        sig_tok = now(act)
                        war["g8"] = [sig_tok]
                        rw = war.pop("route", [])
                        P.emit("vector", lambda v: v.tensor_add(bb_t[:, :], sc_t[:, :], biasb[:, :]),
                               waits=[sig_tok, init_dW] + rw, incs=[(vec, 1)])
                        P.emit("vector", lambda v: v.reduce_max(v1_t[:, :], bb_t[:, :], axis=AX.X),
                               waits=[now(vec)], incs=[(vec, 1)])
                        P.emit("vector", lambda v: v.tensor_scalar(t1_t[:, :], bb_t[:, :], v1_t[:, 0:1],
                                                                   -100.0, ALU.is_ge, ALU.mult),
                               waits=[now(vec)], incs=[(vec, 1)])
                        P.emit("vector", lambda v: v.tensor_tensor(t1_t[:, :], t1_t[:, :], iota8[:, :],
                                                                   ALU.add),
                               waits=[init_gp, now(vec)], incs=[(vec, 1)])
                        P.emit("vector", lambda v: v.tensor_reduce(i1_t[:, :], t1_t[:, :], axis=AX.X,
                                                                   op=ALU.min),
                               waits=[now(vec)], incs=[(vec, 1)])
                        P.emit("vector", lambda v: v.tensor_single_scalar(oh_t[:, :], t1_t[:, :],
                                                                          i1_t[:, 0:1], ALU.is_equal),
                               waits=[now(vec)], incs=[(vec, 1)])
                        P.emit("vector", lambda v: v.scalar_tensor_tensor(bb_t[:, :], oh_t[:, :], -1e30,
                                                                          bb_t[:, :], ALU.mult, ALU.add),
                               waits=[now(vec)], incs=[(vec, 1)])
                        P.emit("vector", lambda v: v.reduce_max(v1_t[:, :], bb_t[:, :], axis=AX.X),
                               waits=[now(vec)], incs=[(vec, 1)])
                        P.emit("vector", lambda v: v.tensor_scalar(t1_t[:, :], bb_t[:, :], v1_t[:, 0:1],
                                                                   -100.0, ALU.is_ge, ALU.mult),
                               waits=[now(vec)], incs=[(vec, 1)])
                        P.emit("vector", lambda v: v.tensor_tensor(t1_t[:, :], t1_t[:, :], iota8[:, :],
                                                                   ALU.add),
                               waits=[now(vec)], incs=[(vec, 1)])
                        P.emit("vector", lambda v: v.tensor_reduce(i1_t[:, :], t1_t[:, :], axis=AX.X,
                                                                   op=ALU.min),
                               waits=[now(vec)], incs=[(vec, 1)])
                        P.emit("vector", lambda v: v.tensor_single_scalar(t1_t[:, :], t1_t[:, :],
                                                                          i1_t[:, 0:1], ALU.is_equal),
                               waits=[now(vec)], incs=[(vec, 1)])
                        P.emit("vector", lambda v: v.tensor_add(oh_t[:, :], oh_t[:, :], t1_t[:, :]),
                               waits=[now(vec)], incs=[(vec, 1)])
                        P.emit("vector", lambda v, i=i: v.tensor_mul(mmF[:, i:i + 113:16],
                                                                     sc_t[:, :], oh_t[:, :]),
                               waits=[now(vec)] + (war.pop("mmF", []) if i == 0 else []),
                               incs=[(vec, 1)])
                        m_tok = now(vec)
                        war["route"] = [m_tok]
                        if i == HTILES - 1:
                            P.emit("tensor", lambda t: t.transpose(TPm, mmF[:, :], ident[:, :]),
                                   waits=[m_tok] + war.pop("tpm", []) + war.pop("mb", []),
                                   incs=[(pe, 1)])
                            tpm_tok = now(pe)
                            P.emit("scalar", lambda s: s.copy(mTcF[:, :], TPm),
                                   waits=[tpm_tok] + war.pop("mTc", []),
                                   incs=[(act, 1)])
                            mTh_tok[0] = now(act)
                            war["tpm"] = [mTh_tok[0]]
                            war["mmF"] = [tpm_tok]
                    ycast_all = now(gp)

                    ptokA = {0: load_piece(0, 0, war.pop(("WA", 0), []))}
                    ptokB = load_piece(0, 1, war.pop("WB", []))
                    ptokA[1] = load_piece(1, 0, war.pop(("WA", 1), []))
                    m1toks = {}
                    gctr = 0

                    for e in range(NE):
                        routed = e < 8
                        lastAuse = []
                        lastBuse = []
                        for ch in range(HCHUNKS):
                            yp = ch % 2
                            if routed:
                                g_ = gctr
                                gctr += 1
                                slot = g_ % 2
                                m1toks[g_] = load_m1(e, ch, slot, war.pop(("m1", slot), []))
                                mbw = war.pop("mb", [])
                                if g_ == 0:
                                    mbw = mbw + [mTh_tok[0]]
                                P.emit("tensor", lambda t, slot=slot:
                                       t.matmul(MB[:, :], ones1[0:1, :], m1r[slot][0:1, :],
                                                start=True, stop=True, skip_group_check=True),
                                       waits=[m1toks[g_], init_gp] + mbw, incs=[(pe, 1)])
                                mb_tok = now(pe)
                                war[("m1", slot)] = [mb_tok]
                                yew = war.pop(("ye", yp), [])
                                for k in range(KC):
                                    P.emit("vector", lambda v, k=k, ch=ch, yp=yp:
                                           v.tensor_tensor(ye[yp][:, k * CHUNK:(k + 1) * CHUNK],
                                                           yTh[:, k * HALF + ch * CHUNK:
                                                               k * HALF + (ch + 1) * CHUNK],
                                                           MB[:, :], ALU.mult),
                                           waits=([mb_tok, ycast_all] + yew) if k == 0 else [],
                                           incs=[(vec, 1)] if k == KC - 1 else [])
                                ye_tok = now(vec)
                                war["mb"] = [ye_tok]
                            for piece in range(2):
                                if piece == 0:
                                    nt, wu, wd, wtok = NT_A, WupA[e % 2], WdnA[e % 2], ptokA[e]
                                else:
                                    nt, wu, wd, wtok = NT_B, WupB, WdnB, ptokB
                                hb = piece
                                sq_last = None
                                for ntile in range(nt):
                                    ub = ntile % 2
                                    uw = war.pop(("U", ub), [])
                                    for k in range(KC):
                                        if routed:
                                            movap = (lambda k=k, yp=yp:
                                                     ye[yp][:, k * CHUNK:(k + 1) * CHUNK])
                                            mov_tok = ye_tok
                                        else:
                                            movap = (lambda k=k, ch=ch:
                                                     yTh[:, k * HALF + ch * CHUNK:
                                                         k * HALF + (ch + 1) * CHUNK])
                                            mov_tok = ycast_all
                                        P.emit("tensor", lambda t, ntile=ntile, k=k, nt=nt, wu=wu,
                                               ub=ub, movap=movap:
                                               t.matmul(U[ub][:, :],
                                                        wu[:, (k * nt + ntile) * 128:
                                                           (k * nt + ntile + 1) * 128],
                                                        movap(), start=(k == 0), stop=(k == KC - 1),
                                                        skip_group_check=True),
                                               waits=([wtok, mov_tok] + uw) if k == 0 else [],
                                               incs=[(pe, 1)] if k == KC - 1 else [])
                                    u_tok = now(pe)
                                    P.emit("scalar", lambda s, ntile=ntile, ub=ub, hb=hb:
                                           s.activation(hh[hb][:, ntile * CHUNK:(ntile + 1) * CHUNK],
                                                        U[ub][:, :], AF.Relu),
                                           waits=[u_tok] + (war.pop(("hhh", hb), [])
                                                            if ntile == 0 else []),
                                           incs=[(act, 1)])
                                    relu_tok = now(act)
                                    war[("U", ub)] = [relu_tok]
                                    P.emit("vector", lambda v, ntile=ntile, hb=hb:
                                           v.tensor_mul(hh[hb][:, ntile * CHUNK:(ntile + 1) * CHUNK],
                                                        hh[hb][:, ntile * CHUNK:(ntile + 1) * CHUNK],
                                                        hh[hb][:, ntile * CHUNK:(ntile + 1) * CHUNK]),
                                           waits=[relu_tok], incs=[(vec, 1)])
                                    sq_last = now(vec)
                                for tt in range(4):
                                    for cc in range(2):
                                        db = (tt * 2 + cc) % 2
                                        dw_ = war.pop(("D", db), [])
                                        for ntile in range(nt):
                                            P.emit("tensor", lambda t, tt=tt, cc=cc, ntile=ntile,
                                                   nt=nt, wd=wd, hb=hb, db=db:
                                                   t.matmul(DD[db][:, :],
                                                            hh[hb][:, ntile * CHUNK + tt * 128:
                                                                   ntile * CHUNK + (tt + 1) * 128],
                                                            wd[:, ntile * D + cc * 512:
                                                               ntile * D + (cc + 1) * 512],
                                                            start=(ntile == 0), stop=(ntile == nt - 1),
                                                            skip_group_check=True),
                                                   waits=([sq_last] + dw_) if ntile == 0 else [],
                                                   incs=[(pe, 1)] if ntile == nt - 1 else [])
                                        d_tok = now(pe)
                                        dslc = (lambda ch=ch, tt=tt, cc=cc:
                                                dacc[:, (ch * 4 + tt) * D + cc * 512:
                                                     (ch * 4 + tt) * D + (cc + 1) * 512])
                                        dchain = war.pop(("dch", ch, tt, cc), [])
                                        if e == 0 and piece == 0:
                                            P.emit("vector", lambda v, dslc=dslc, db=db:
                                                   v.tensor_copy(dslc(), DD[db][:, :]),
                                                   waits=[d_tok] + ([ycast_all]
                                                                    if (ch == 0 and tt == 0 and cc == 0)
                                                                    else []),
                                                   incs=[(vec, 1)])
                                        else:
                                            P.emit("vector", lambda v, dslc=dslc, db=db:
                                                   v.tensor_tensor(dslc(), dslc(), DD[db][:, :],
                                                                   ALU.add),
                                                   waits=[d_tok] + dchain, incs=[(vec, 1)])
                                        war[("D", db)] = [now(vec)]
                                        war[("dch", ch, tt, cc)] = [now(vec)]
                                if piece == 0:
                                    lastAuse = [now(pe)]
                                else:
                                    lastBuse = [now(pe)]
                                war[("hhh", hb)] = [now(pe)]
                            if routed:
                                war[("ye", yp)] = [now(pe)]
                        war[("WA", e % 2)] = lastAuse
                        war["WB"] = lastBuse
                        if e + 2 < NE:
                            ptokA[e + 2] = load_piece(e + 2, 0, war.pop(("WA", e % 2)))
                        if e + 1 < NE:
                            ptokB = load_piece(e + 1, 1, war.pop("WB"))
                    dacc_done = now(vec)

                    if dbg and half == 0 and tag == "f1":
                        P.emit("gpsimd", lambda g: g.dma_start(out=dbg_d["yTh"][:, :], in_=yTh[:, :]),
                               waits=[dacc_done], incs=[(dDbg, 16)])
                        P.emit("gpsimd", lambda g: g.dma_start(out=dbg_d["mTc"][:, :], in_=mTcF[:, :]),
                               waits=[], incs=[(dDbg, 16)])
                        P.emit("gpsimd", lambda g: g.dma_start(out=dbg_d["dacc"][:, :], in_=dacc[:, :]),
                               waits=[], incs=[(dDbg, 16)])
                        P.emit("gpsimd", lambda g: g.dma_start(out=dbg_d["ye"][:, :], in_=ye[1][:, :]),
                               waits=[], incs=[(dDbg, 16)])

                    for i in range(HTILES):
                        r = i % 2
                        P.emit("sync", lambda sy, i=i, r=r, t0=t0: sy.dma_start(
                            out=xr[r][:, :], in_=src_dram[(t0 + i) * 128:(t0 + i + 1) * 128, :]),
                               waits=[dacc_done] + war.pop(("xr", r), []), incs=[(dXs[r], 16)])
                        ld = now(dXs[r])
                        tap_guards = []
                        if dbg and half == 0 and tag == "f1" and i == 0:
                            P.emit("gpsimd", lambda g: g.dma_start(out=dbg_d["xs0"][0:128, :],
                                                                   in_=xr[r][:, :]),
                                   waits=[ld], incs=[(dDbg, 16)])
                            tap_guards.append(now(dDbg))
                        P.emit("vector", lambda v, i=i, r=r: v.scalar_tensor_tensor(
                            ybuf[r][:, :], dacc[:, i * D:(i + 1) * D], 0.5, xr[r][:, :],
                            ALU.mult, ALU.add),
                               waits=[ld] + war.pop(("ybuf", r), []), incs=[(vec, 1)])
                        stt_tok = now(vec)
                        yb_guards = []
                        if dbg and half == 0 and tag == "f1" and i == 0:
                            P.emit("gpsimd", lambda g: g.dma_start(out=dbg_d["xs0"][128:256, :],
                                                                   in_=ybuf[0][:, :]),
                                   waits=[stt_tok], incs=[(dDbg, 16)])
                            yb_guards.append(now(dDbg))
                        war[("xr", r)] = [stt_tok] + tap_guards
                        P.emit("sync", lambda sy, i=i, r=r, t0=t0: sy.dma_start(
                            out=dst_dram[(t0 + i) * 128:(t0 + i + 1) * 128, :], in_=ybuf[r][:, :]),
                               waits=[stt_tok], incs=[(dYs[r], 16)])
                        war[("ybuf", r)] = [now(dYs[r])] + yb_guards
                    if dbg and half == 0 and tag == "f1":
                        P.emit("gpsimd", lambda g: g.dma_start(out=dbg_d["xs0"][:, :],
                                                               in_=dst_dram[0:HALF, :]),
                               waits=[now(dYs[0]), now(dYs[1])], incs=[(dDbg, 16)])
                        P.emit("gpsimd", lambda g: g.dma_start(out=dbg_d["dacc2"][:, :], in_=dacc[:, :]),
                               waits=[now(vec)], incs=[(dDbg, 16)])
                    war["dacc"] = [now(vec)]
                    war["yTh"] = [now(pe), now(vec)]
                    war["mTc"] = [now(dM[0]), now(dM[1])]
            engine_barrier()

        def attn_phase(src_dram, dst_dram):
            with ExitStack() as ph:
                psb = lambda n_, sh, dt: ph.enter_context(nc.sbuf_tensor(n_ + "_at", sh, dt))
                iota512 = psb("iota512", [128, 512], F32)
                wqkv_sb = psb("wqkv", [128, KC * 192], BF16)
                wout_sb = psb("wout", [64, D], BF16)
                zT = [psb(f"zT{i}", [128, KC * CHUNK], BF16) for i in range(2)]
                qkT = psb("qkT", [128, NB * SEQ], BF16)
                v_sb = psb("v_sb", [128, NTILES * 64], BF16)
                vI = psb("vI", [128, 16 * 128], BF16)
                kc_r = psb("kc_r", [1, 2 * SEQ], BF16)
                qc_r = psb("qc_r", [1, 2 * SEQ], BF16)
                Esb = [psb(f"Esb{i}", [128, 4 * 512], BF16) for i in range(2)]
                masks = psb("masks", [128, 4 * 512], BF16)
                ndbuf = psb("ndbuf", [128, SEQ], F32)
                rden = psb("rden", [64, SEQ], F32)
                oT = psb("oT", [64, SEQ], BF16)
                stg = [psb(f"stg{i}", [2, 512], F32) for i in range(4)]

                TPz = [PB[5][:, 0:128], PB[6][:, 0:128]]
                QK = PB[7]
                Vp = PB[4][:, 0:64]
                SS = [PB[0], PB[1], PB[2], PB[3]]
                ND = [PB[5][0:2, :], PB[6][0:2, :]]
                OP = [PB[4], PB[7]]

                war = {}
                P.emit("gpsimd", lambda g: g.dma_start(
                    out=wqkv_sb[:, :].rearrange("p (k f) -> p k f", k=KC),
                    in_=wqkvT_d[:, :].rearrange("(k p) f -> p k f", p=128)), incs=[(dInit, 16)])
                P.emit("gpsimd", lambda g: g.dma_start(out=wout_sb[:, :], in_=woutT_d[:, :]),
                       incs=[(dInit, 16)])
                wqkv_tok = now(dInit)
                wout_tok = now(dInit)
                P.emit("gpsimd", lambda g: g.iota(iota512[:, :], pattern=[[1, 512]], base=0,
                                                  channel_multiplier=-1,
                                                  allow_small_or_imprecise_dtypes=True),
                       incs=[(gp, 1)])
                io5_tok = now(gp)
                for m in range(4):
                    P.emit("vector", lambda v, m=m: v.tensor_single_scalar(
                        masks[:, m * 512:(m + 1) * 512], iota512[:, :], float(128.0 * m), ALU.is_ge),
                           waits=[io5_tok], incs=[(vec, 1)])
                mask_tok = now(vec)

                for ch8 in range(8):
                    zr = ch8 % 2
                    last_evac = None
                    for j in range(4):
                        i = ch8 * 4 + j
                        r = i % 2
                        toks = emit_ln_tile(war, src_dram, i, r)
                        for k in range(KC):
                            tp = (i * KC + k) % 2
                            P.emit("tensor", lambda t, k=k, r=r, tp=tp:
                                   t.transpose(TPz[tp], ybuf[r][:, k * 128:(k + 1) * 128], ident[:, :]),
                                   waits=[toks["y"], ident_tok] + war.pop(("tpz", tp), []),
                                   incs=[(pe, 1)])
                            tp_tok = now(pe)
                            P.emit("scalar", lambda s, k=k, j=j, zr=zr, tp=tp:
                                   s.copy(zT[zr][:, k * CHUNK + j * 128: k * CHUNK + (j + 1) * 128],
                                          TPz[tp]),
                                   waits=[tp_tok] + (war.pop(("zslot", zr), [])
                                                     if (k == 0 and j == 0) else []),
                                   incs=[(act, 1)])
                            war[("tpz", tp)] = [now(act)]
                            last_evac = now(act)
                        war[("ybuf", r)] = [now(pe)]
                        war[("xr", r)] = [toks["y"], now(vec)]
                        vw = war.pop("vp", [])
                        for k in range(KC):
                            P.emit("tensor", lambda t, k=k, zr=zr, j=j:
                                   t.matmul(Vp, zT[zr][:, k * CHUNK + j * 128: k * CHUNK + (j + 1) * 128],
                                            wqkv_sb[:, k * 192 + 128: k * 192 + 192],
                                            start=(k == 0), stop=(k == KC - 1), skip_group_check=True),
                                   waits=([wqkv_tok, last_evac] + vw) if k == 0 else [],
                                   incs=[(pe, 1)] if k == KC - 1 else [])
                        vp_tok = now(pe)
                        P.emit("scalar", lambda s, i=i: s.copy(v_sb[:, i * 64:(i + 1) * 64], Vp),
                               waits=[vp_tok], incs=[(act, 1)])
                        war["vp"] = [now(act)]
                    qkw = war.pop("qk", [])
                    for k in range(KC):
                        P.emit("tensor", lambda t, k=k, zr=zr:
                               t.matmul(QK[:, :], wqkv_sb[:, k * 192: k * 192 + 128],
                                        zT[zr][:, k * CHUNK:(k + 1) * CHUNK],
                                        start=(k == 0), stop=(k == KC - 1), skip_group_check=True),
                               waits=([wqkv_tok, last_evac] + qkw) if k == 0 else [],
                               incs=[(pe, 1)] if k == KC - 1 else [])
                    qk_tok = now(pe)
                    b = ch8 // 4
                    pos = (ch8 % 4) * CHUNK
                    P.emit("scalar", lambda s, b=b, pos=pos: s.activation(
                        qkT[0:64, b * SEQ + pos: b * SEQ + pos + CHUNK], QK[0:64, :], AF.Identity,
                        scale=f32(0.125)),
                           waits=[qk_tok], incs=[(act, 1)])
                    P.emit("scalar", lambda s, b=b, pos=pos: s.copy(
                        qkT[64:128, b * SEQ + pos: b * SEQ + pos + CHUNK], QK[64:128, :]),
                           incs=[(act, 1)])
                    war["qk"] = [now(act)]
                    war[("zslot", zr)] = [now(pe)]
                qkT_tok = now(act)
                vsb_tok = now(act)

                for b in range(NB):
                    P.emit("gpsimd", lambda g: g.memset(vI[:, :], 1.0),
                           waits=[vsb_tok] + war.pop("vI", []), incs=[(gp, 1)])
                    vi0 = now(gp)
                    for jb in range(16):
                        P.emit("vector", lambda v, b=b, jb=jb: v.tensor_copy(
                            vI[:, jb * 128:(jb + 1) * 128].rearrange("p (h two) -> p h two", two=2)[:, :, 0:1],
                            v_sb[:, (b * 16 + jb) * 64:(b * 16 + jb + 1) * 64].rearrange("p (h o) -> p h o", o=1)),
                               waits=[vi0] if jb == 0 else [], incs=[(vec, 1)])
                    vI_tok = now(vec)

                    def load_rows(h, slot, waits_, b=b):
                        P.emit("scalar", lambda s, h=h, b=b, slot=slot: s.dma_start(
                            out=kc_r[0:1, slot * SEQ:(slot + 1) * SEQ],
                            in_=qkT[64 + h:65 + h, b * SEQ:(b + 1) * SEQ]),
                               waits=waits_ + [qkT_tok], incs=[(dR[slot], 16)])
                        P.emit("scalar", lambda s, h=h, b=b, slot=slot: s.dma_start(
                            out=qc_r[0:1, slot * SEQ:(slot + 1) * SEQ],
                            in_=qkT[h:h + 1, b * SEQ:(b + 1) * SEQ]),
                               waits=[], incs=[(dR[slot], 16)])
                        return now(dR[slot])

                    rowtok = {0: load_rows(0, 0, war.pop(("rows", 0), []))}
                    rowtok[1] = load_rows(1, 1, war.pop(("rows", 1), []))
                    for h in range(64):
                        slot = h % 2
                        rtok = rowtok.pop(h)
                        for ic in range(4):
                            ndb = ic % 2
                            ndw = war.pop(("nd", ndb), [])
                            njb = 4 * (ic + 1)
                            ngroups = njb // 2

                            def emit_avs(g, e_tok, ndw_):
                                eb = g % 2
                                for j2 in range(2):
                                    jb = 2 * g + j2
                                    P.emit("tensor", lambda t, h=h, jb=jb, eb=eb, ndb=ndb,
                                           njb=njb, j2=j2:
                                           t.matmul(ND[ndb],
                                                    vI[:, jb * 128 + 2 * h: jb * 128 + 2 * h + 2],
                                                    Esb[eb][:, 512 * j2:512 * (j2 + 1)],
                                                    start=(jb == 0), stop=(jb == njb - 1),
                                                    skip_group_check=True),
                                           waits=([vI_tok, e_tok] + ndw_) if j2 == 0 else [],
                                           incs=[(pe, 1)] if j2 == 1 else [])
                                war[("Ebuf", eb)] = [now(pe)]

                            pending = None
                            for g in range(ngroups):
                                pb = g % 2
                                eb = g % 2
                                ssw = war.pop(("SSg", pb), [])
                                for j2 in range(2):
                                    jb = 2 * g + j2
                                    P.emit("tensor", lambda t, ic=ic, jb=jb, slot=slot, pb=pb, j2=j2:
                                           t.matmul(SS[2 * pb + j2][:, :],
                                                    kc_r[0:1, slot * SEQ + jb * 128:
                                                         slot * SEQ + (jb + 1) * 128],
                                                    qc_r[0:1, slot * SEQ + ic * 512:
                                                         slot * SEQ + (ic + 1) * 512],
                                                    start=True, stop=True, skip_group_check=True),
                                           waits=([rtok] + ssw) if j2 == 0 else [],
                                           incs=[(pe, 1)] if j2 == 1 else [])
                                s_tok = now(pe)
                                ebw = war.pop(("Ebuf", eb), [])
                                P.emit("scalar", lambda s, eb=eb, pb=pb: s.activation(
                                    Esb[eb][:, 0:1024], PBbig[:, 1024 * pb:1024 * (pb + 1)],
                                    AF.Exp, bias=f32(-2.0)),
                                       waits=[s_tok] + ebw, incs=[(act, 1)])
                                e_tok = now(act)
                                war[("SSg", pb)] = [e_tok]
                                if g >= ngroups - 2:
                                    mof = (g - (ngroups - 2)) * 1024
                                    P.emit("vector", lambda v, eb=eb, mof=mof: v.tensor_tensor(
                                        Esb[eb][:, 0:1024], Esb[eb][:, 0:1024],
                                        masks[:, mof:mof + 1024], ALU.mult),
                                           waits=[e_tok, mask_tok], incs=[(vec, 1)])
                                    e_tok = now(vec)
                                if pending is not None:
                                    emit_avs(pending[0], pending[1], ndw if pending[0] == 0 else [])
                                pending = (g, e_tok)
                            emit_avs(pending[0], pending[1], ndw if pending[0] == 0 else [])
                            nd_tok = now(pe)
                            sg = (h * 4 + ic) % 4
                            P.emit("vector", lambda v, ndb=ndb, sg=sg: v.tensor_copy(stg[sg][:, :], ND[ndb]),
                                   waits=[nd_tok] + war.pop(("stg", sg), []), incs=[(vec, 1)])
                            st_tok = now(vec)
                            war[("nd", ndb)] = [st_tok]
                            P.emit("scalar", lambda s, h=h, ic=ic, sg=sg: s.dma_start(
                                out=ndbuf[h:h + 1, ic * 512:(ic + 1) * 512], in_=stg[sg][0:1, :]),
                                   waits=[st_tok] + war.pop("hscat", []) +
                                         (war.pop("ndbuf", []) if (h == 0 and ic == 0) else []),
                                   incs=[(dG[sg], 16)])
                            P.emit("scalar", lambda s, h=h, ic=ic, sg=sg: s.dma_start(
                                out=ndbuf[64 + h:65 + h, ic * 512:(ic + 1) * 512], in_=stg[sg][1:2, :]),
                                   waits=[], incs=[(dG[sg], 16)])
                            war[("stg", sg)] = [now(dG[sg])]
                            war["hscat"] = [now(dG[sg])]
                        if h + 2 < 64:
                            rowtok[h + 2] = load_rows(h + 2, slot, [now(pe)])
                    scat_toks = [now(q) for q in dG]
                    P.emit("vector", lambda v: v.reciprocal(rden[:, :], ndbuf[64:128, :]),
                           waits=scat_toks + war.pop("rden", []), incs=[(vec, 1)])
                    P.emit("vector", lambda v: v.tensor_mul(oT[:, :], ndbuf[0:64, :], rden[:, :]),
                           waits=[now(vec)] + war.pop("oT", []), incs=[(vec, 1)])
                    oT_tok = now(vec)
                    war["ndbuf"] = [oT_tok]
                    war["rden"] = [oT_tok]
                    for j in range(16):
                        i = b * 16 + j
                        r = i % 2
                        optoks = []
                        for cc in range(2):
                            opw = war.pop(("OP", cc), [])
                            P.emit("tensor", lambda t, j=j, cc=cc:
                                   t.matmul(OP[cc][:, :], oT[:, j * 128:(j + 1) * 128],
                                            wout_sb[:, cc * 512:(cc + 1) * 512],
                                            start=True, stop=True, skip_group_check=True),
                                   waits=[oT_tok, wout_tok] + opw, incs=[(pe, 1)])
                            optoks.append(now(pe))
                        P.emit("sync", lambda sy, i=i, r=r: sy.dma_start(
                            out=xr[r][:, :], in_=src_dram[i * 128:(i + 1) * 128, :]),
                               waits=war.pop(("xr", r), []), incs=[(dXs[r], 16)])
                        ld = now(dXs[r])
                        ybw = war.pop(("ybuf", r), [])
                        for cc in range(2):
                            P.emit("vector", lambda v, r=r, cc=cc: v.tensor_tensor(
                                ybuf[r][:, cc * 512:(cc + 1) * 512], xr[r][:, cc * 512:(cc + 1) * 512],
                                OP[cc][:, :], ALU.add),
                                   waits=[ld, optoks[cc]] + (ybw if cc == 0 else []),
                                   incs=[(vec, 1)])
                            war[("OP", cc)] = [now(vec)]
                        stt_tok = now(vec)
                        war[("xr", r)] = [stt_tok]
                        P.emit("sync", lambda sy, i=i, r=r: sy.dma_start(
                            out=dst_dram[i * 128:(i + 1) * 128, :], in_=ybuf[r][:, :]),
                               waits=[stt_tok], incs=[(dYs[r], 16)])
                        war[("ybuf", r)] = [now(dYs[r])]
                    war["oT"] = [now(pe)]
                    war["vI"] = [now(pe)]
                    war[("rows", 0)] = [now(pe)]
                    war[("rows", 1)] = [now(pe)]
            engine_barrier()

        if "F1" in phases:
            ffn_phase(x_d, xs_d, "f1")
            a_src = xs_d
        else:
            a_src = x_d
        if "A" in phases:
            attn_phase(a_src, xs_d)
            f2_src = xs_d
        else:
            f2_src = a_src
        if "F2" in phases:
            ffn_phase(f2_src, out_d, "f2")
        else:
            prev = None
            for i in range(NTILES):
                P.emit("sync", lambda sy, i=i: sy.dma_start(out=xr[0][:, :],
                                                            in_=f2_src[i * 128:(i + 1) * 128, :]),
                       waits=[prev] if prev else [], incs=[(dXs[0], 16)])
                ld = now(dXs[0])
                P.emit("sync", lambda sy, i=i: sy.dma_start(out=out_d[i * 128:(i + 1) * 128, :],
                                                            in_=xr[0][:, :]),
                       waits=[ld], incs=[(dYs[0], 16)])
                prev = now(dYs[0])

        P.emit("sync", lambda sy: sy.nop(), waits=[now(dYs[0]), now(dYs[1])])

        with ExitStack() as sctx:
            for s in all_sems:
                s.h = sctx.enter_context(nc.semaphore(s.name))
            block = sctx.enter_context(nc.Block())
            block.tensor(lambda t: P.run("tensor", t))
            block.vector(lambda v: P.run("vector", v))
            block.scalar(lambda s: P.run("scalar", s))
            block.gpsimd(lambda g: P.run("gpsimd", g))
            block.sync(lambda sy: P.run("sync", sy))
    return nc


def kernel(L0_attn_x, L0_attn_w_qkv, L0_attn_w_out, L0_ffn_gate_w, L0_ffn_expert_bias,
           L0_ffn_w_up, L0_ffn_w_down, L0_ffn_shared_w_up, L0_ffn_shared_w_down):
    x = np.ascontiguousarray(np.asarray(L0_attn_x, dtype=np.float32))
    wqkvT = np.ascontiguousarray(np.asarray(L0_attn_w_qkv, np.float32).T)
    woutT = np.ascontiguousarray(np.asarray(L0_attn_w_out, np.float32).T)
    gwT = np.ascontiguousarray(np.asarray(L0_ffn_gate_w, np.float32).T)
    biasbc = np.ascontiguousarray(
        np.tile(np.asarray(L0_ffn_expert_bias, np.float32)[None, :], (128, 1)))
    up = np.asarray(L0_ffn_w_up, np.float32)
    dn = np.asarray(L0_ffn_w_down, np.float32)
    sup = np.asarray(L0_ffn_shared_w_up, np.float32)
    sdn = np.asarray(L0_ffn_shared_w_down, np.float32)
    upT = np.ascontiguousarray(np.concatenate([up.transpose(0, 2, 1), sup.T[None]], axis=0))
    dnT = np.ascontiguousarray(np.concatenate([dn.transpose(0, 2, 1), sdn.T[None]], axis=0))

    nc = build_nc()
    in_maps = []
    for c in range(NCORES):
        xs = np.ascontiguousarray(x[2 * c:2 * c + 2].reshape(T, D))
        in_maps.append(dict(x=xs, wqkvT=wqkvT, woutT=woutT, gwT=gwT, biasbc=biasbc,
                            upT=upT, dnT=dnT))
    res = run_bass_kernel_spmd(nc, in_maps, core_ids=list(range(NCORES)))
    out = np.concatenate([np.asarray(res.results[c]["out"]).reshape(2, SEQ, D)
                          for c in range(NCORES)], axis=0)
    return out.astype(np.float32)


# version 05
# speedup vs baseline: 1.6045x; 1.1271x over previous
import numpy as np
from contextlib import ExitStack

import concourse.bass as bass
import concourse.mybir as mybir
from concourse.bass_utils import run_bass_kernel_spmd

F32 = mybir.dt.float32
BF16 = mybir.dt.bfloat16
AF = mybir.ActivationFunctionType
ALU = mybir.AluOpType
AX = mybir.AxisListType
f32 = np.float32

NCORES = 8
D = 1024
KC = 8
HID = 1408
NH_A, NT_A = 640, 5
NH_B, NT_B = 768, 6
NE = 9
SEQ = 2048
NB = 2
T = NB * SEQ
NTILES = T // 128
HALF = 2048
HTILES = HALF // 128
CHUNK = 512
HCHUNKS = HALF // CHUNK


class Sem:
    __slots__ = ("name", "v", "h")

    def __init__(self, name):
        self.name, self.v, self.h = name, 0, None


class Plan:
    def __init__(self):
        self.ops = []

    def emit(self, engine, fn, waits=(), incs=()):
        ws = [(s, v) for (s, v) in waits if v > 0]
        ri = []
        for (s, k) in incs:
            s.v += k
            ri.append((s, k))
        self.ops.append((engine, ws, fn, ri))

    def run(self, engine_name, eng):
        for e, ws, fn, ri in self.ops:
            if e != engine_name:
                continue
            merged = {}
            for (s, v) in ws:
                if s.name in merged:
                    merged[s.name] = (s, max(v, merged[s.name][1]))
                else:
                    merged[s.name] = (s, v)
            for (s, v) in merged.values():
                eng.wait_ge(s.h, v)
            inst = fn(eng)
            for (s, k) in ri:
                inst.then_inc(s.h, k)


def build_nc(phases=("F1", "A", "F2"), dbg=False):
    nc = bass.Bass("TRN2", target_bir_lowering=False, debug=False)
    dbg_d = {}
    if dbg:
        dbg_d["yTh"] = nc.dram_tensor("dbg_yTh", [128, KC * HALF], F32, kind="ExternalOutput").ap()
        dbg_d["mTc"] = nc.dram_tensor("dbg_mTc", [128, 128], F32, kind="ExternalOutput").ap()
        dbg_d["dacc"] = nc.dram_tensor("dbg_dacc", [128, HTILES * D], F32, kind="ExternalOutput").ap()
        dbg_d["ye"] = nc.dram_tensor("dbg_ye", [128, KC * CHUNK], F32, kind="ExternalOutput").ap()
        dbg_d["xs0"] = nc.dram_tensor("dbg_xs0", [HALF, D], F32, kind="ExternalOutput").ap()
        dbg_d["dacc2"] = nc.dram_tensor("dbg_dacc2", [128, HTILES * D], F32, kind="ExternalOutput").ap()

    x_d = nc.dram_tensor("x", [T, D], F32, kind="ExternalInput").ap()
    wqkvT_d = nc.dram_tensor("wqkvT", [D, 192], F32, kind="ExternalInput").ap()
    woutT_d = nc.dram_tensor("woutT", [64, D], F32, kind="ExternalInput").ap()
    gwT_d = nc.dram_tensor("gwT", [D, 8], F32, kind="ExternalInput").ap()
    biasbc_d = nc.dram_tensor("biasbc", [128, 8], F32, kind="ExternalInput").ap()
    upT_d = nc.dram_tensor("upT", [NE, D, HID], F32, kind="ExternalInput").ap()
    dnT_d = nc.dram_tensor("dnT", [NE, HID, D], F32, kind="ExternalInput").ap()
    out_d = nc.dram_tensor("out", [T, D], F32, kind="ExternalOutput").ap()
    xs_d = nc.dram_tensor("xs", [T, D], F32, kind="Internal").ap()

    P = Plan()
    dInit = Sem("dInit")
    dWA = [Sem("dWA0"), Sem("dWA1")]
    dWB = Sem("dWB")
    dXs = [Sem("dX0"), Sem("dX1")]
    dYs = [Sem("dY0"), Sem("dY1")]
    dM = [Sem("dM0"), Sem("dM1")]
    dR = [Sem("dR0"), Sem("dR1")]
    dG = [Sem(f"dG{i}") for i in range(4)]
    dDbg = Sem("dDbg")
    pe, act, vec, gp, bar = Sem("pe"), Sem("act"), Sem("vec"), Sem("gp"), Sem("bar")
    all_sems = [dInit, dWA[0], dWA[1], dWB, dXs[0], dXs[1], dYs[0], dYs[1],
                dM[0], dM[1], dR[0], dR[1]] + dG + [dDbg, pe, act, vec, gp, bar]

    def now(s):
        return (s, s.v)

    with ExitStack() as top:
        sb = lambda n_, sh, dt: top.enter_context(nc.sbuf_tensor(n_, sh, dt))
        ident = sb("ident", [128, 128], F32)
        ones1 = sb("ones1", [1, 128], BF16)
        iota8 = sb("iota8", [128, 8], F32)
        biasb = sb("biasb", [128, 8], F32)
        gw_sb = sb("gw_sb", [128, KC * 8], F32)
        xr = [sb(f"xr{i}", [128, D], F32) for i in range(2)]
        ybuf = [sb(f"ybuf{i}", [128, D], F32) for i in range(2)]
        stt_ = [sb(f"stt{i}", [128, 8], F32) for i in range(2)]
        PBbig = top.enter_context(nc.psum_tensor("PBbig", [128, 2048], F32))
        PB = [PBbig[:, 512 * i:512 * (i + 1)] for i in range(4)] + \
             [top.enter_context(nc.psum_tensor(f"PB{i}", [128, 512], F32)) for i in range(4, 8)]

        P.emit("gpsimd", lambda g: g.iota(ident[:, :], pattern=[[1, 128]], base=0,
                                          channel_multiplier=-1, allow_small_or_imprecise_dtypes=True),
               incs=[(gp, 1)])
        P.emit("gpsimd", lambda g: g.iota(iota8[:, :], pattern=[[1, 8]], base=0,
                                          channel_multiplier=0, allow_small_or_imprecise_dtypes=True),
               incs=[(gp, 1)])
        P.emit("gpsimd", lambda g: g.memset(ones1[:, :], 1.0), incs=[(gp, 1)])
        P.emit("gpsimd", lambda g: g.dma_start(out=biasb[:, :], in_=biasbc_d[:, :]), incs=[(dInit, 16)])
        P.emit("gpsimd", lambda g: g.dma_start(
            out=gw_sb[:, :].rearrange("p (k e) -> p k e", k=KC),
            in_=gwT_d[:, :].rearrange("(k p) e -> p k e", p=128)), incs=[(dInit, 16)])
        init_gp = now(gp)
        init_dW = now(dInit)
        P.emit("vector", lambda v: v.tensor_single_scalar(ident[:, :], ident[:, :], 0.0, ALU.is_equal),
               waits=[init_gp], incs=[(vec, 1)])
        ident_tok = now(vec)

        def engine_barrier():
            target = bar.v + 5
            dma_fence = {"sync": [now(q) for q in dXs + dYs],
                         "gpsimd": [now(dInit), now(dWA[0]), now(dWA[1]), now(dWB), now(dDbg)],
                         "scalar": [now(q) for q in dM + dR + dG]}
            for e in ("tensor", "vector", "scalar", "gpsimd", "sync"):
                P.emit(e, lambda eng: eng.nop(), waits=dma_fence.get(e, []), incs=[(bar, 1)])
            for e in ("tensor", "vector", "scalar", "gpsimd", "sync"):
                P.emit(e, lambda eng: eng.nop(), waits=[(bar, target)])

        def emit_ln_tile(war, src_dram, ti, r):
            st = stt_[r]
            s1, s2 = st[:, 0:1], st[:, 1:2]
            mu, var = st[:, 2:3], st[:, 3:4]
            rstd, nm = st[:, 4:5], st[:, 5:6]
            P.emit("sync", lambda sy: sy.dma_start(out=xr[r][:, :],
                                                   in_=src_dram[ti * 128:(ti + 1) * 128, :]),
                   waits=war.pop(("xr", r), []), incs=[(dXs[r], 16)])
            ld = now(dXs[r])
            stw = war.pop(("st", r), [])
            ybw = war.pop(("ybuf", r), [])
            P.emit("vector", lambda v: v.reduce_sum(s1, xr[r][:, :], axis=AX.X),
                   waits=[ld] + stw, incs=[(vec, 1)])
            s1_tok = now(vec)
            P.emit("scalar", lambda s: s.activation(ybuf[r][:, :], xr[r][:, :], AF.Square,
                                                    accum_out=s2),
                   waits=[ld] + stw + ybw, incs=[(act, 1)])
            s2_tok = now(act)
            P.emit("vector", lambda v: v.tensor_scalar_mul(mu, s1, 1.0 / D),
                   waits=[s1_tok], incs=[(vec, 1)])
            mu_tok = now(vec)
            P.emit("vector", lambda v: v.tensor_scalar_mul(var, s2, 1.0 / D),
                   waits=[s2_tok], incs=[(vec, 1)])
            P.emit("vector", lambda v: v.tensor_mul(s1, mu, mu), waits=[mu_tok], incs=[(vec, 1)])
            P.emit("vector", lambda v: v.tensor_sub(var, var, s1), waits=[now(vec)],
                   incs=[(vec, 1)])
            var_tok = now(vec)
            P.emit("scalar", lambda s: s.activation(rstd, var, AF.Sqrt, bias=f32(1e-5)),
                   waits=[var_tok], incs=[(act, 1)])
            sqr_tok = now(act)
            P.emit("vector", lambda v: v.reciprocal(rstd, rstd), waits=[sqr_tok], incs=[(vec, 1)])
            rstd_tok = now(vec)
            P.emit("vector", lambda v: v.scalar_tensor_tensor(nm, mu, -1.0, rstd, ALU.mult, ALU.mult),
                   waits=[rstd_tok], incs=[(vec, 1)])
            nm_tok = now(vec)
            P.emit("scalar", lambda s: s.activation(ybuf[r][:, :], xr[r][:, :], AF.Identity,
                                                    bias=nm, scale=rstd),
                   waits=[rstd_tok, nm_tok], incs=[(act, 1)])
            y_tok = now(act)
            war[("st", r)] = [y_tok]
            return {"ld": ld, "y": y_tok}

        def ffn_phase(src_dram, dst_dram, tag):
            with ExitStack() as ph:
                psb = lambda n_, sh, dt: ph.enter_context(nc.sbuf_tensor(n_ + tag, sh, dt))
                yTh = psb("yTh", [128, KC * HALF], BF16)
                mmF = psb("mmF", [128, 128], F32)
                mTcF = psb("mTcF", [128, 128], BF16)
                m1r = [psb(f"m1r{i}", [1, CHUNK], BF16) for i in range(2)]
                sc_t = psb("sc_t", [128, 8], F32)
                bb_t = psb("bb_t", [128, 8], F32)
                t1_t = psb("t1_t", [128, 8], F32)
                oh_t = psb("oh_t", [128, 8], F32)
                v1_t = psb("v1_t", [128, 1], F32)
                i1_t = psb("i1_t", [128, 1], F32)
                WupA = [psb(f"WupA{i}", [128, KC * NH_A], BF16) for i in range(2)]
                WdnA = [psb(f"WdnA{i}", [128, NT_A * D], BF16) for i in range(2)]
                WupB = psb("WupB", [128, KC * NH_B], BF16)
                WdnB = psb("WdnB", [128, NT_B * D], BF16)
                ye = [psb(f"ye{i}", [128, KC * CHUNK], BF16) for i in range(2)]
                hh = [psb("hhA", [128, NT_A * CHUNK], BF16), psb("hhB", [128, NT_B * CHUNK], BF16)]
                dacc = psb("dacc", [128, HTILES * D], F32)
                yTf32 = dacc

                U = [PB[0], PB[1]]
                DD = [PB[2], PB[3]]
                MB = PB[4]
                TPy = [PB[5][:, 0:128], PB[6][:, 0:128]]
                G8 = PB[7][:, 0:8]
                TPm = PB[4][0:128, 0:128]

                war = {}

                def load_piece(e, piece, waits_):
                    if piece == 0:
                        n0, nt, wu, wd = 0, NT_A, WupA[e % 2], WdnA[e % 2]
                        qsem = dWA[e % 2]
                    else:
                        n0, nt, wu, wd = NH_A, NT_B, WupB, WdnB
                        qsem = dWB
                    P.emit("gpsimd", lambda g, e=e, n0=n0, nt=nt, wu=wu: g.dma_start(
                        out=wu[:, 0:KC * nt * 128].rearrange("p (k n) -> p k n", k=KC),
                        in_=upT_d[e:e + 1, :, n0:n0 + nt * 128].rearrange("o (k p) n -> p (o k) n", p=128)),
                        waits=waits_, incs=[(qsem, 16)])
                    P.emit("gpsimd", lambda g, e=e, n0=n0, nt=nt, wd=wd: g.dma_start(
                        out=wd[:, 0:nt * D].rearrange("p (k c) -> p k c", k=nt),
                        in_=dnT_d[e:e + 1, n0:n0 + nt * 128, :].rearrange("o (k p) c -> p (o k) c", p=128)),
                        waits=[], incs=[(qsem, 16)])
                    return now(qsem)

                mTh_tok = [None]

                def load_m1(e, ch, slot, waits_):
                    P.emit("scalar", lambda s, e=e, ch=ch, slot=slot: s.dma_start(
                        out=m1r[slot][0:1, :],
                        in_=mTcF[e * 16 + 4 * ch: e * 16 + 4 * ch + 4, 0:128]),
                           waits=waits_ + [mTh_tok[0]], incs=[(dM[slot], 16)])
                    return now(dM[slot])

                for half in range(2):
                    t0 = half * HTILES
                    for i in range(HTILES):
                        r = i % 2
                        ci, jj = i // 4, i % 4
                        toks = emit_ln_tile(war, src_dram, t0 + i, r)
                        evac_toks = []
                        for k in range(KC):
                            tp = (i * KC + k) % 2
                            P.emit("tensor", lambda t, k=k, r=r, tp=tp:
                                   t.transpose(TPy[tp], ybuf[r][:, k * 128:(k + 1) * 128], ident[:, :]),
                                   waits=[toks["y"], ident_tok] + war.pop(("tpy", tp), []),
                                   incs=[(pe, 1)])
                            tp_tok = now(pe)
                            P.emit("scalar", lambda s, k=k, tp=tp:
                                   s.copy(yTf32[:, k * 128:(k + 1) * 128], TPy[tp]),
                                   waits=[tp_tok] + war.pop("yTf32", []) +
                                         (war.pop("dacc", []) if (i == 0 and k == 0) else []),
                                   incs=[(act, 1)])
                            ev = now(act)
                            evac_toks.append(ev)
                            war[("tpy", tp)] = [ev]
                            P.emit("gpsimd", lambda g, k=k, i=i:
                                   g.tensor_copy(yTh[:, k * HALF + i * 128: k * HALF + (i + 1) * 128],
                                                 yTf32[:, k * 128:(k + 1) * 128]),
                                   waits=[ev] + (war.pop("yTh", []) if (i == 0 and k == 0) else []),
                                   incs=[(gp, 1)])
                        war[("ybuf", r)] = [now(pe)]
                        war[("xr", r)] = [toks["y"], now(vec)]
                        g8w = war.pop("g8", [])
                        for k in range(KC):
                            P.emit("tensor", lambda t, k=k:
                                   t.matmul(G8, yTf32[:, k * 128:(k + 1) * 128],
                                            gw_sb[:, k * 8:(k + 1) * 8],
                                            start=(k == 0), stop=(k == KC - 1), skip_group_check=True),
                                   waits=([evac_toks[KC - 1], init_dW] + g8w) if k == 0 else [],
                                   incs=[(pe, 1)] if k == KC - 1 else [])
                        gate_tok = now(pe)
                        war["yTf32"] = [gate_tok, now(gp)]
                        P.emit("scalar", lambda s: s.activation(sc_t[:, :], G8, AF.Sigmoid),
                               waits=[gate_tok], incs=[(act, 1)])
                        sig_tok = now(act)
                        war["g8"] = [sig_tok]
                        rw = war.pop("route", [])
                        P.emit("vector", lambda v: v.tensor_add(bb_t[:, :], sc_t[:, :], biasb[:, :]),
                               waits=[sig_tok, init_dW] + rw, incs=[(vec, 1)])
                        P.emit("vector", lambda v: v.reduce_max(v1_t[:, :], bb_t[:, :], axis=AX.X),
                               waits=[now(vec)], incs=[(vec, 1)])
                        P.emit("vector", lambda v: v.tensor_scalar(t1_t[:, :], bb_t[:, :], v1_t[:, 0:1],
                                                                   -100.0, ALU.is_ge, ALU.mult),
                               waits=[now(vec)], incs=[(vec, 1)])
                        P.emit("vector", lambda v: v.tensor_tensor(t1_t[:, :], t1_t[:, :], iota8[:, :],
                                                                   ALU.add),
                               waits=[init_gp, now(vec)], incs=[(vec, 1)])
                        P.emit("vector", lambda v: v.tensor_reduce(i1_t[:, :], t1_t[:, :], axis=AX.X,
                                                                   op=ALU.min),
                               waits=[now(vec)], incs=[(vec, 1)])
                        P.emit("vector", lambda v: v.tensor_single_scalar(oh_t[:, :], t1_t[:, :],
                                                                          i1_t[:, 0:1], ALU.is_equal),
                               waits=[now(vec)], incs=[(vec, 1)])
                        P.emit("vector", lambda v: v.scalar_tensor_tensor(bb_t[:, :], oh_t[:, :], -1e30,
                                                                          bb_t[:, :], ALU.mult, ALU.add),
                               waits=[now(vec)], incs=[(vec, 1)])
                        P.emit("vector", lambda v: v.reduce_max(v1_t[:, :], bb_t[:, :], axis=AX.X),
                               waits=[now(vec)], incs=[(vec, 1)])
                        P.emit("vector", lambda v: v.tensor_scalar(t1_t[:, :], bb_t[:, :], v1_t[:, 0:1],
                                                                   -100.0, ALU.is_ge, ALU.mult),
                               waits=[now(vec)], incs=[(vec, 1)])
                        P.emit("vector", lambda v: v.tensor_tensor(t1_t[:, :], t1_t[:, :], iota8[:, :],
                                                                   ALU.add),
                               waits=[now(vec)], incs=[(vec, 1)])
                        P.emit("vector", lambda v: v.tensor_reduce(i1_t[:, :], t1_t[:, :], axis=AX.X,
                                                                   op=ALU.min),
                               waits=[now(vec)], incs=[(vec, 1)])
                        P.emit("vector", lambda v: v.tensor_single_scalar(t1_t[:, :], t1_t[:, :],
                                                                          i1_t[:, 0:1], ALU.is_equal),
                               waits=[now(vec)], incs=[(vec, 1)])
                        P.emit("vector", lambda v: v.tensor_add(oh_t[:, :], oh_t[:, :], t1_t[:, :]),
                               waits=[now(vec)], incs=[(vec, 1)])
                        P.emit("vector", lambda v, i=i: v.tensor_mul(mmF[:, i:i + 113:16],
                                                                     sc_t[:, :], oh_t[:, :]),
                               waits=[now(vec)] + (war.pop("mmF", []) if i == 0 else []),
                               incs=[(vec, 1)])
                        m_tok = now(vec)
                        war["route"] = [m_tok]
                        if i == HTILES - 1:
                            P.emit("tensor", lambda t: t.transpose(TPm, mmF[:, :], ident[:, :]),
                                   waits=[m_tok] + war.pop("tpm", []) + war.pop("mb", []),
                                   incs=[(pe, 1)])
                            tpm_tok = now(pe)
                            P.emit("scalar", lambda s: s.copy(mTcF[:, :], TPm),
                                   waits=[tpm_tok] + war.pop("mTc", []),
                                   incs=[(act, 1)])
                            mTh_tok[0] = now(act)
                            war["tpm"] = [mTh_tok[0]]
                            war["mmF"] = [tpm_tok]
                    ycast_all = now(gp)

                    ptokA = {0: load_piece(0, 0, war.pop(("WA", 0), []))}
                    ptokB = load_piece(0, 1, war.pop("WB", []))
                    ptokA[1] = load_piece(1, 0, war.pop(("WA", 1), []))
                    m1toks = {}
                    gctr = 0

                    ye_toks = {}

                    def emit_mbye(e_, ch_):
                        nonlocal gctr
                        g_ = gctr
                        gctr += 1
                        slot = g_ % 2
                        yp_ = ch_ % 2
                        m1toks[g_] = load_m1(e_, ch_, slot, war.pop(("m1", slot), []))
                        mbw = war.pop("mb", [])
                        if g_ == 0:
                            mbw = mbw + [mTh_tok[0]]
                        P.emit("tensor", lambda t, slot=slot:
                               t.matmul(MB[:, :], ones1[0:1, :], m1r[slot][0:1, :],
                                        start=True, stop=True, skip_group_check=True),
                               waits=[m1toks[g_], init_gp] + mbw, incs=[(pe, 1)])
                        mb_tok = now(pe)
                        war[("m1", slot)] = [mb_tok]
                        yew = war.pop(("ye", yp_), [])
                        for k in range(KC):
                            P.emit("vector", lambda v, k=k, ch_=ch_, yp_=yp_:
                                   v.tensor_tensor(ye[yp_][:, k * CHUNK:(k + 1) * CHUNK],
                                                   yTh[:, k * HALF + ch_ * CHUNK:
                                                       k * HALF + (ch_ + 1) * CHUNK],
                                                   MB[:, :], ALU.mult),
                                   waits=([mb_tok, ycast_all] + yew) if k == 0 else [],
                                   incs=[(vec, 1)] if k == KC - 1 else [])
                        ye_toks[(e_, ch_)] = now(vec)
                        war["mb"] = [now(vec)]

                    emit_mbye(0, 0)
                    for e in range(NE):
                        routed = e < 8
                        lastAuse = []
                        lastBuse = []
                        for ch in range(HCHUNKS):
                            yp = ch % 2
                            if routed:
                                ye_tok = ye_toks.pop((e, ch))
                            for piece in range(2):
                                if piece == 0:
                                    nt, wu, wd, wtok = NT_A, WupA[e % 2], WdnA[e % 2], ptokA[e]
                                else:
                                    nt, wu, wd, wtok = NT_B, WupB, WdnB, ptokB
                                hb = piece
                                sq_last = None
                                for ntile in range(nt):
                                    ub = ntile % 2
                                    uw = war.pop(("U", ub), [])
                                    for k in range(KC):
                                        if routed:
                                            movap = (lambda k=k, yp=yp:
                                                     ye[yp][:, k * CHUNK:(k + 1) * CHUNK])
                                            mov_tok = ye_tok
                                        else:
                                            movap = (lambda k=k, ch=ch:
                                                     yTh[:, k * HALF + ch * CHUNK:
                                                         k * HALF + (ch + 1) * CHUNK])
                                            mov_tok = ycast_all
                                        P.emit("tensor", lambda t, ntile=ntile, k=k, nt=nt, wu=wu,
                                               ub=ub, movap=movap:
                                               t.matmul(U[ub][:, :],
                                                        wu[:, (k * nt + ntile) * 128:
                                                           (k * nt + ntile + 1) * 128],
                                                        movap(), start=(k == 0), stop=(k == KC - 1),
                                                        skip_group_check=True),
                                               waits=([wtok, mov_tok] + uw) if k == 0 else [],
                                               incs=[(pe, 1)] if k == KC - 1 else [])
                                    u_tok = now(pe)
                                    P.emit("scalar", lambda s, ntile=ntile, ub=ub, hb=hb:
                                           s.activation(hh[hb][:, ntile * CHUNK:(ntile + 1) * CHUNK],
                                                        U[ub][:, :], AF.Relu),
                                           waits=[u_tok] + (war.pop(("hhh", hb), [])
                                                            if ntile == 0 else []),
                                           incs=[(act, 1)])
                                    relu_tok = now(act)
                                    war[("U", ub)] = [relu_tok]
                                    P.emit("vector", lambda v, ntile=ntile, hb=hb:
                                           v.tensor_mul(hh[hb][:, ntile * CHUNK:(ntile + 1) * CHUNK],
                                                        hh[hb][:, ntile * CHUNK:(ntile + 1) * CHUNK],
                                                        hh[hb][:, ntile * CHUNK:(ntile + 1) * CHUNK]),
                                           waits=[relu_tok], incs=[(vec, 1)])
                                    sq_last = now(vec)
                                for tt in range(4):
                                    for cc in range(2):
                                        db = (tt * 2 + cc) % 2
                                        dw_ = war.pop(("D", db), [])
                                        for ntile in range(nt):
                                            P.emit("tensor", lambda t, tt=tt, cc=cc, ntile=ntile,
                                                   nt=nt, wd=wd, hb=hb, db=db:
                                                   t.matmul(DD[db][:, :],
                                                            hh[hb][:, ntile * CHUNK + tt * 128:
                                                                   ntile * CHUNK + (tt + 1) * 128],
                                                            wd[:, ntile * D + cc * 512:
                                                               ntile * D + (cc + 1) * 512],
                                                            start=(ntile == 0), stop=(ntile == nt - 1),
                                                            skip_group_check=True),
                                                   waits=([sq_last] + dw_) if ntile == 0 else [],
                                                   incs=[(pe, 1)] if ntile == nt - 1 else [])
                                        d_tok = now(pe)
                                        dslc = (lambda ch=ch, tt=tt, cc=cc:
                                                dacc[:, (ch * 4 + tt) * D + cc * 512:
                                                     (ch * 4 + tt) * D + (cc + 1) * 512])
                                        dchain = war.pop(("dch", ch, tt, cc), [])
                                        if e == 0 and piece == 0:
                                            P.emit("vector", lambda v, dslc=dslc, db=db:
                                                   v.tensor_copy(dslc(), DD[db][:, :]),
                                                   waits=[d_tok] + ([ycast_all]
                                                                    if (ch == 0 and tt == 0 and cc == 0)
                                                                    else []),
                                                   incs=[(vec, 1)])
                                        else:
                                            P.emit("vector", lambda v, dslc=dslc, db=db:
                                                   v.tensor_tensor(dslc(), dslc(), DD[db][:, :],
                                                                   ALU.add),
                                                   waits=[d_tok] + dchain, incs=[(vec, 1)])
                                        war[("D", db)] = [now(vec)]
                                        war[("dch", ch, tt, cc)] = [now(vec)]
                                if piece == 0:
                                    lastAuse = [now(pe)]
                                    if ch < HCHUNKS - 1 and routed:
                                        emit_mbye(e, ch + 1)
                                    elif ch == HCHUNKS - 1 and routed and e + 1 < 8:
                                        emit_mbye(e + 1, 0)
                                else:
                                    lastBuse = [now(pe)]
                                war[("hhh", hb)] = [now(pe)]
                            if routed:
                                war[("ye", yp)] = [now(pe)]
                        war[("WA", e % 2)] = lastAuse
                        war["WB"] = lastBuse
                        if e + 2 < NE:
                            ptokA[e + 2] = load_piece(e + 2, 0, war.pop(("WA", e % 2)))
                        if e + 1 < NE:
                            ptokB = load_piece(e + 1, 1, war.pop("WB"))
                    dacc_done = now(vec)

                    if dbg and half == 0 and tag == "f1":
                        P.emit("gpsimd", lambda g: g.dma_start(out=dbg_d["yTh"][:, :], in_=yTh[:, :]),
                               waits=[dacc_done], incs=[(dDbg, 16)])
                        P.emit("gpsimd", lambda g: g.dma_start(out=dbg_d["mTc"][:, :], in_=mTcF[:, :]),
                               waits=[], incs=[(dDbg, 16)])
                        P.emit("gpsimd", lambda g: g.dma_start(out=dbg_d["dacc"][:, :], in_=dacc[:, :]),
                               waits=[], incs=[(dDbg, 16)])
                        P.emit("gpsimd", lambda g: g.dma_start(out=dbg_d["ye"][:, :], in_=ye[1][:, :]),
                               waits=[], incs=[(dDbg, 16)])

                    for i in range(HTILES):
                        r = i % 2
                        P.emit("sync", lambda sy, i=i, r=r, t0=t0: sy.dma_start(
                            out=xr[r][:, :], in_=src_dram[(t0 + i) * 128:(t0 + i + 1) * 128, :]),
                               waits=[dacc_done] + war.pop(("xr", r), []), incs=[(dXs[r], 16)])
                        ld = now(dXs[r])
                        tap_guards = []
                        if dbg and half == 0 and tag == "f1" and i == 0:
                            P.emit("gpsimd", lambda g: g.dma_start(out=dbg_d["xs0"][0:128, :],
                                                                   in_=xr[r][:, :]),
                                   waits=[ld], incs=[(dDbg, 16)])
                            tap_guards.append(now(dDbg))
                        P.emit("vector", lambda v, i=i, r=r: v.scalar_tensor_tensor(
                            ybuf[r][:, :], dacc[:, i * D:(i + 1) * D], 0.5, xr[r][:, :],
                            ALU.mult, ALU.add),
                               waits=[ld] + war.pop(("ybuf", r), []), incs=[(vec, 1)])
                        stt_tok = now(vec)
                        yb_guards = []
                        if dbg and half == 0 and tag == "f1" and i == 0:
                            P.emit("gpsimd", lambda g: g.dma_start(out=dbg_d["xs0"][128:256, :],
                                                                   in_=ybuf[0][:, :]),
                                   waits=[stt_tok], incs=[(dDbg, 16)])
                            yb_guards.append(now(dDbg))
                        war[("xr", r)] = [stt_tok] + tap_guards
                        P.emit("sync", lambda sy, i=i, r=r, t0=t0: sy.dma_start(
                            out=dst_dram[(t0 + i) * 128:(t0 + i + 1) * 128, :], in_=ybuf[r][:, :]),
                               waits=[stt_tok], incs=[(dYs[r], 16)])
                        war[("ybuf", r)] = [now(dYs[r])] + yb_guards
                    if dbg and half == 0 and tag == "f1":
                        P.emit("gpsimd", lambda g: g.dma_start(out=dbg_d["xs0"][:, :],
                                                               in_=dst_dram[0:HALF, :]),
                               waits=[now(dYs[0]), now(dYs[1])], incs=[(dDbg, 16)])
                        P.emit("gpsimd", lambda g: g.dma_start(out=dbg_d["dacc2"][:, :], in_=dacc[:, :]),
                               waits=[now(vec)], incs=[(dDbg, 16)])
                    war["dacc"] = [now(vec)]
                    war["yTh"] = [now(pe), now(vec)]
                    war["mTc"] = [now(dM[0]), now(dM[1])]
            engine_barrier()

        def attn_phase(src_dram, dst_dram):
            with ExitStack() as ph:
                psb = lambda n_, sh, dt: ph.enter_context(nc.sbuf_tensor(n_ + "_at", sh, dt))
                iota512 = psb("iota512", [128, 512], F32)
                wqkv_sb = psb("wqkv", [128, KC * 192], BF16)
                wout_sb = psb("wout", [64, D], BF16)
                zT = [psb(f"zT{i}", [128, KC * CHUNK], BF16) for i in range(2)]
                qkT = psb("qkT", [128, NB * SEQ], BF16)
                v_sb = psb("v_sb", [128, NTILES * 64], BF16)
                vI = psb("vI", [128, 16 * 128], BF16)
                kc_r = psb("kc_r", [1, 2 * SEQ], BF16)
                qc_r = psb("qc_r", [1, 2 * SEQ], BF16)
                Esb = [psb(f"Esb{i}", [128, 4 * 512], BF16) for i in range(2)]
                masks = psb("masks", [128, 4 * 512], BF16)
                ndbuf = psb("ndbuf", [128, SEQ], F32)
                rden = psb("rden", [64, SEQ], F32)
                oT = psb("oT", [64, SEQ], BF16)
                stg = [psb(f"stg{i}", [2, 512], F32) for i in range(4)]

                TPz = [PB[5][:, 0:128], PB[6][:, 0:128]]
                QK = PB[7]
                Vp = PB[4][:, 0:64]
                SS = [PB[0], PB[1], PB[2], PB[3]]
                ND = [PB[5][0:2, :], PB[6][0:2, :]]
                OP = [PB[4], PB[7]]

                war = {}
                P.emit("gpsimd", lambda g: g.dma_start(
                    out=wqkv_sb[:, :].rearrange("p (k f) -> p k f", k=KC),
                    in_=wqkvT_d[:, :].rearrange("(k p) f -> p k f", p=128)), incs=[(dInit, 16)])
                P.emit("gpsimd", lambda g: g.dma_start(out=wout_sb[:, :], in_=woutT_d[:, :]),
                       incs=[(dInit, 16)])
                wqkv_tok = now(dInit)
                wout_tok = now(dInit)
                P.emit("gpsimd", lambda g: g.iota(iota512[:, :], pattern=[[1, 512]], base=0,
                                                  channel_multiplier=-1,
                                                  allow_small_or_imprecise_dtypes=True),
                       incs=[(gp, 1)])
                io5_tok = now(gp)
                for m in range(4):
                    P.emit("vector", lambda v, m=m: v.tensor_single_scalar(
                        masks[:, m * 512:(m + 1) * 512], iota512[:, :], float(128.0 * m), ALU.is_ge),
                           waits=[io5_tok], incs=[(vec, 1)])
                mask_tok = now(vec)

                for ch8 in range(8):
                    zr = ch8 % 2
                    last_evac = None
                    for j in range(4):
                        i = ch8 * 4 + j
                        r = i % 2
                        toks = emit_ln_tile(war, src_dram, i, r)
                        for k in range(KC):
                            tp = (i * KC + k) % 2
                            P.emit("tensor", lambda t, k=k, r=r, tp=tp:
                                   t.transpose(TPz[tp], ybuf[r][:, k * 128:(k + 1) * 128], ident[:, :]),
                                   waits=[toks["y"], ident_tok] + war.pop(("tpz", tp), []),
                                   incs=[(pe, 1)])
                            tp_tok = now(pe)
                            P.emit("scalar", lambda s, k=k, j=j, zr=zr, tp=tp:
                                   s.copy(zT[zr][:, k * CHUNK + j * 128: k * CHUNK + (j + 1) * 128],
                                          TPz[tp]),
                                   waits=[tp_tok] + (war.pop(("zslot", zr), [])
                                                     if (k == 0 and j == 0) else []),
                                   incs=[(act, 1)])
                            war[("tpz", tp)] = [now(act)]
                            last_evac = now(act)
                        war[("ybuf", r)] = [now(pe)]
                        war[("xr", r)] = [toks["y"], now(vec)]
                        vw = war.pop("vp", [])
                        for k in range(KC):
                            P.emit("tensor", lambda t, k=k, zr=zr, j=j:
                                   t.matmul(Vp, zT[zr][:, k * CHUNK + j * 128: k * CHUNK + (j + 1) * 128],
                                            wqkv_sb[:, k * 192 + 128: k * 192 + 192],
                                            start=(k == 0), stop=(k == KC - 1), skip_group_check=True),
                                   waits=([wqkv_tok, last_evac] + vw) if k == 0 else [],
                                   incs=[(pe, 1)] if k == KC - 1 else [])
                        vp_tok = now(pe)
                        P.emit("scalar", lambda s, i=i: s.copy(v_sb[:, i * 64:(i + 1) * 64], Vp),
                               waits=[vp_tok], incs=[(act, 1)])
                        war["vp"] = [now(act)]
                    qkw = war.pop("qk", [])
                    for k in range(KC):
                        P.emit("tensor", lambda t, k=k, zr=zr:
                               t.matmul(QK[:, :], wqkv_sb[:, k * 192: k * 192 + 128],
                                        zT[zr][:, k * CHUNK:(k + 1) * CHUNK],
                                        start=(k == 0), stop=(k == KC - 1), skip_group_check=True),
                               waits=([wqkv_tok, last_evac] + qkw) if k == 0 else [],
                               incs=[(pe, 1)] if k == KC - 1 else [])
                    qk_tok = now(pe)
                    b = ch8 // 4
                    pos = (ch8 % 4) * CHUNK
                    P.emit("scalar", lambda s, b=b, pos=pos: s.activation(
                        qkT[0:64, b * SEQ + pos: b * SEQ + pos + CHUNK], QK[0:64, :], AF.Identity,
                        scale=f32(0.125)),
                           waits=[qk_tok], incs=[(act, 1)])
                    P.emit("scalar", lambda s, b=b, pos=pos: s.copy(
                        qkT[64:128, b * SEQ + pos: b * SEQ + pos + CHUNK], QK[64:128, :]),
                           incs=[(act, 1)])
                    war["qk"] = [now(act)]
                    war[("zslot", zr)] = [now(pe)]
                qkT_tok = now(act)
                vsb_tok = now(act)

                for b in range(NB):
                    P.emit("gpsimd", lambda g: g.memset(vI[:, :], 1.0),
                           waits=[vsb_tok] + war.pop("vI", []), incs=[(gp, 1)])
                    vi0 = now(gp)
                    for jb in range(16):
                        P.emit("vector", lambda v, b=b, jb=jb: v.tensor_copy(
                            vI[:, jb * 128:(jb + 1) * 128].rearrange("p (h two) -> p h two", two=2)[:, :, 0:1],
                            v_sb[:, (b * 16 + jb) * 64:(b * 16 + jb + 1) * 64].rearrange("p (h o) -> p h o", o=1)),
                               waits=[vi0] if jb == 0 else [], incs=[(vec, 1)])
                    vI_tok = now(vec)

                    def load_rows(h, slot, waits_, b=b):
                        P.emit("scalar", lambda s, h=h, b=b, slot=slot: s.dma_start(
                            out=kc_r[0:1, slot * SEQ:(slot + 1) * SEQ],
                            in_=qkT[64 + h:65 + h, b * SEQ:(b + 1) * SEQ]),
                               waits=waits_ + [qkT_tok], incs=[(dR[slot], 16)])
                        P.emit("scalar", lambda s, h=h, b=b, slot=slot: s.dma_start(
                            out=qc_r[0:1, slot * SEQ:(slot + 1) * SEQ],
                            in_=qkT[h:h + 1, b * SEQ:(b + 1) * SEQ]),
                               waits=[], incs=[(dR[slot], 16)])
                        return now(dR[slot])

                    rowtok = {0: load_rows(0, 0, war.pop(("rows", 0), []))}
                    rowtok[1] = load_rows(1, 1, war.pop(("rows", 1), []))
                    pending = None

                    def flush_pending(p):
                        g, e_tok, ndb_, njb_, h_, xw_, ic_ev = p
                        eb = g % 2
                        for j2 in range(2):
                            jb = 2 * g + j2
                            P.emit("tensor", lambda t, h_=h_, jb=jb, eb=eb, ndb_=ndb_,
                                   njb_=njb_, j2=j2:
                                   t.matmul(ND[ndb_],
                                            vI[:, jb * 128 + 2 * h_: jb * 128 + 2 * h_ + 2],
                                            Esb[eb][:, 512 * j2:512 * (j2 + 1)],
                                            start=(jb == 0), stop=(jb == njb_ - 1),
                                            skip_group_check=True),
                                   waits=([vI_tok, e_tok] + xw_) if j2 == 0 else [],
                                   incs=[(pe, 1)] if j2 == 1 else [])
                        war[("Ebuf", eb)] = [now(pe)]
                        if ic_ev is not None:
                            nd_tok = now(pe)
                            ic = ic_ev
                            sg = (h_ * 4 + ic) % 4
                            P.emit("vector", lambda v, ndb_=ndb_, sg=sg: v.tensor_copy(
                                stg[sg][:, :], ND[ndb_]),
                                   waits=[nd_tok] + war.pop(("stg", sg), []), incs=[(vec, 1)])
                            st_tok = now(vec)
                            war[("nd", ndb_)] = [st_tok]
                            P.emit("scalar", lambda s, h_=h_, ic=ic, sg=sg: s.dma_start(
                                out=ndbuf[h_:h_ + 1, ic * 512:(ic + 1) * 512], in_=stg[sg][0:1, :]),
                                   waits=[st_tok] + war.pop("hscat", []) +
                                         (war.pop("ndbuf", []) if (h_ == 0 and ic == 0) else []),
                                   incs=[(dG[sg], 16)])
                            P.emit("scalar", lambda s, h_=h_, ic=ic, sg=sg: s.dma_start(
                                out=ndbuf[64 + h_:65 + h_, ic * 512:(ic + 1) * 512],
                                in_=stg[sg][1:2, :]),
                                   waits=[], incs=[(dG[sg], 16)])
                            war[("stg", sg)] = [now(dG[sg])]
                            war["hscat"] = [now(dG[sg])]

                    for h in range(64):
                        slot = h % 2
                        rtok = rowtok.pop(h)
                        for ic in range(4):
                            ndb = ic % 2
                            ndw = war.pop(("nd", ndb), [])
                            njb = 4 * (ic + 1)
                            ngroups = njb // 2
                            for g in range(ngroups):
                                pb = g % 2
                                eb = g % 2
                                ssw = war.pop(("SSg", pb), [])
                                for j2 in range(2):
                                    jb = 2 * g + j2
                                    P.emit("tensor", lambda t, ic=ic, jb=jb, slot=slot, pb=pb, j2=j2:
                                           t.matmul(SS[2 * pb + j2][:, :],
                                                    kc_r[0:1, slot * SEQ + jb * 128:
                                                         slot * SEQ + (jb + 1) * 128],
                                                    qc_r[0:1, slot * SEQ + ic * 512:
                                                         slot * SEQ + (ic + 1) * 512],
                                                    start=True, stop=True, skip_group_check=True),
                                           waits=([rtok] + ssw) if j2 == 0 else [],
                                           incs=[(pe, 1)] if j2 == 1 else [])
                                s_tok = now(pe)
                                ebw = war.pop(("Ebuf", eb), [])
                                P.emit("scalar", lambda s, eb=eb, pb=pb: s.activation(
                                    Esb[eb][:, 0:1024], PBbig[:, 1024 * pb:1024 * (pb + 1)],
                                    AF.Exp, bias=f32(-2.0)),
                                       waits=[s_tok] + ebw, incs=[(act, 1)])
                                e_tok = now(act)
                                war[("SSg", pb)] = [e_tok]
                                if g >= ngroups - 2:
                                    mof = (g - (ngroups - 2)) * 1024
                                    P.emit("vector", lambda v, eb=eb, mof=mof: v.tensor_tensor(
                                        Esb[eb][:, 0:1024], Esb[eb][:, 0:1024],
                                        masks[:, mof:mof + 1024], ALU.mult),
                                           waits=[e_tok, mask_tok], incs=[(vec, 1)])
                                    e_tok = now(vec)
                                if pending is not None:
                                    flush_pending(pending)
                                pending = (g, e_tok, ndb, njb, h,
                                           ndw if g == 0 else [],
                                           ic if g == ngroups - 1 else None)
                        if h + 2 < 64:
                            rowtok[h + 2] = load_rows(h + 2, slot, [now(pe)])
                    flush_pending(pending)
                    pending = None
                    scat_toks = [now(q) for q in dG]
                    P.emit("vector", lambda v: v.reciprocal(rden[:, :], ndbuf[64:128, :]),
                           waits=scat_toks + war.pop("rden", []), incs=[(vec, 1)])
                    P.emit("vector", lambda v: v.tensor_mul(oT[:, :], ndbuf[0:64, :], rden[:, :]),
                           waits=[now(vec)] + war.pop("oT", []), incs=[(vec, 1)])
                    oT_tok = now(vec)
                    war["ndbuf"] = [oT_tok]
                    war["rden"] = [oT_tok]
                    for j in range(16):
                        i = b * 16 + j
                        r = i % 2
                        optoks = []
                        for cc in range(2):
                            opw = war.pop(("OP", cc), [])
                            P.emit("tensor", lambda t, j=j, cc=cc:
                                   t.matmul(OP[cc][:, :], oT[:, j * 128:(j + 1) * 128],
                                            wout_sb[:, cc * 512:(cc + 1) * 512],
                                            start=True, stop=True, skip_group_check=True),
                                   waits=[oT_tok, wout_tok] + opw, incs=[(pe, 1)])
                            optoks.append(now(pe))
                        P.emit("sync", lambda sy, i=i, r=r: sy.dma_start(
                            out=xr[r][:, :], in_=src_dram[i * 128:(i + 1) * 128, :]),
                               waits=war.pop(("xr", r), []), incs=[(dXs[r], 16)])
                        ld = now(dXs[r])
                        ybw = war.pop(("ybuf", r), [])
                        for cc in range(2):
                            P.emit("vector", lambda v, r=r, cc=cc: v.tensor_tensor(
                                ybuf[r][:, cc * 512:(cc + 1) * 512], xr[r][:, cc * 512:(cc + 1) * 512],
                                OP[cc][:, :], ALU.add),
                                   waits=[ld, optoks[cc]] + (ybw if cc == 0 else []),
                                   incs=[(vec, 1)])
                            war[("OP", cc)] = [now(vec)]
                        stt_tok = now(vec)
                        war[("xr", r)] = [stt_tok]
                        P.emit("sync", lambda sy, i=i, r=r: sy.dma_start(
                            out=dst_dram[i * 128:(i + 1) * 128, :], in_=ybuf[r][:, :]),
                               waits=[stt_tok], incs=[(dYs[r], 16)])
                        war[("ybuf", r)] = [now(dYs[r])]
                    war["oT"] = [now(pe)]
                    war["vI"] = [now(pe)]
                    war[("rows", 0)] = [now(pe)]
                    war[("rows", 1)] = [now(pe)]
            engine_barrier()

        if "F1" in phases:
            ffn_phase(x_d, xs_d, "f1")
            a_src = xs_d
        else:
            a_src = x_d
        if "A" in phases:
            attn_phase(a_src, xs_d)
            f2_src = xs_d
        else:
            f2_src = a_src
        if "F2" in phases:
            ffn_phase(f2_src, out_d, "f2")
        else:
            prev = None
            for i in range(NTILES):
                P.emit("sync", lambda sy, i=i: sy.dma_start(out=xr[0][:, :],
                                                            in_=f2_src[i * 128:(i + 1) * 128, :]),
                       waits=[prev] if prev else [], incs=[(dXs[0], 16)])
                ld = now(dXs[0])
                P.emit("sync", lambda sy, i=i: sy.dma_start(out=out_d[i * 128:(i + 1) * 128, :],
                                                            in_=xr[0][:, :]),
                       waits=[ld], incs=[(dYs[0], 16)])
                prev = now(dYs[0])

        P.emit("sync", lambda sy: sy.nop(), waits=[now(dYs[0]), now(dYs[1])])

        with ExitStack() as sctx:
            for s in all_sems:
                s.h = sctx.enter_context(nc.semaphore(s.name))
            block = sctx.enter_context(nc.Block())
            block.tensor(lambda t: P.run("tensor", t))
            block.vector(lambda v: P.run("vector", v))
            block.scalar(lambda s: P.run("scalar", s))
            block.gpsimd(lambda g: P.run("gpsimd", g))
            block.sync(lambda sy: P.run("sync", sy))
    return nc


def kernel(L0_attn_x, L0_attn_w_qkv, L0_attn_w_out, L0_ffn_gate_w, L0_ffn_expert_bias,
           L0_ffn_w_up, L0_ffn_w_down, L0_ffn_shared_w_up, L0_ffn_shared_w_down):
    x = np.ascontiguousarray(np.asarray(L0_attn_x, dtype=np.float32))
    wqkvT = np.ascontiguousarray(np.asarray(L0_attn_w_qkv, np.float32).T)
    woutT = np.ascontiguousarray(np.asarray(L0_attn_w_out, np.float32).T)
    gwT = np.ascontiguousarray(np.asarray(L0_ffn_gate_w, np.float32).T)
    biasbc = np.ascontiguousarray(
        np.tile(np.asarray(L0_ffn_expert_bias, np.float32)[None, :], (128, 1)))
    up = np.asarray(L0_ffn_w_up, np.float32)
    dn = np.asarray(L0_ffn_w_down, np.float32)
    sup = np.asarray(L0_ffn_shared_w_up, np.float32)
    sdn = np.asarray(L0_ffn_shared_w_down, np.float32)
    upT = np.ascontiguousarray(np.concatenate([up.transpose(0, 2, 1), sup.T[None]], axis=0))
    dnT = np.ascontiguousarray(np.concatenate([dn.transpose(0, 2, 1), sdn.T[None]], axis=0))

    nc = build_nc()
    in_maps = []
    for c in range(NCORES):
        xs = np.ascontiguousarray(x[2 * c:2 * c + 2].reshape(T, D))
        in_maps.append(dict(x=xs, wqkvT=wqkvT, woutT=woutT, gwT=gwT, biasbc=biasbc,
                            upT=upT, dnT=dnT))
    res = run_bass_kernel_spmd(nc, in_maps, core_ids=list(range(NCORES)))
    out = np.concatenate([np.asarray(res.results[c]["out"]).reshape(2, SEQ, D)
                          for c in range(NCORES)], axis=0)
    return out.astype(np.float32)


# version 06
# speedup vs baseline: 1.6182x; 1.0086x over previous
import numpy as np
from contextlib import ExitStack

import concourse.bass as bass
import concourse.mybir as mybir
from concourse.bass_utils import run_bass_kernel_spmd

F32 = mybir.dt.float32
BF16 = mybir.dt.bfloat16
AF = mybir.ActivationFunctionType
ALU = mybir.AluOpType
AX = mybir.AxisListType
f32 = np.float32

NCORES = 8
D = 1024
KC = 8
HID = 1408
NH_A, NT_A = 640, 5
NH_B, NT_B = 768, 6
NE = 9
SEQ = 2048
NB = 2
T = NB * SEQ
NTILES = T // 128
HALF = 2048
HTILES = HALF // 128
CHUNK = 512
HCHUNKS = HALF // CHUNK


class Sem:
    __slots__ = ("name", "v", "h")

    def __init__(self, name):
        self.name, self.v, self.h = name, 0, None


class Plan:
    def __init__(self):
        self.ops = []

    def emit(self, engine, fn, waits=(), incs=()):
        ws = [(s, v) for (s, v) in waits if v > 0]
        ri = []
        for (s, k) in incs:
            s.v += k
            ri.append((s, k))
        self.ops.append((engine, ws, fn, ri))

    def run(self, engine_name, eng):
        for e, ws, fn, ri in self.ops:
            if e != engine_name:
                continue
            merged = {}
            for (s, v) in ws:
                if s.name in merged:
                    merged[s.name] = (s, max(v, merged[s.name][1]))
                else:
                    merged[s.name] = (s, v)
            for (s, v) in merged.values():
                eng.wait_ge(s.h, v)
            inst = fn(eng)
            for (s, k) in ri:
                inst.then_inc(s.h, k)


def build_nc(phases=("F1", "A", "F2"), dbg=False):
    nc = bass.Bass("TRN2", target_bir_lowering=False, debug=False)
    dbg_d = {}
    if dbg:
        dbg_d["yTh"] = nc.dram_tensor("dbg_yTh", [128, KC * HALF], F32, kind="ExternalOutput").ap()
        dbg_d["mTc"] = nc.dram_tensor("dbg_mTc", [128, 128], F32, kind="ExternalOutput").ap()
        dbg_d["dacc"] = nc.dram_tensor("dbg_dacc", [128, HTILES * D], F32, kind="ExternalOutput").ap()
        dbg_d["ye"] = nc.dram_tensor("dbg_ye", [128, KC * CHUNK], F32, kind="ExternalOutput").ap()
        dbg_d["xs0"] = nc.dram_tensor("dbg_xs0", [HALF, D], F32, kind="ExternalOutput").ap()
        dbg_d["dacc2"] = nc.dram_tensor("dbg_dacc2", [128, HTILES * D], F32, kind="ExternalOutput").ap()

    x_d = nc.dram_tensor("x", [T, D], F32, kind="ExternalInput").ap()
    wqkvT_d = nc.dram_tensor("wqkvT", [D, 192], F32, kind="ExternalInput").ap()
    woutT_d = nc.dram_tensor("woutT", [64, D], F32, kind="ExternalInput").ap()
    gwT_d = nc.dram_tensor("gwT", [D, 8], F32, kind="ExternalInput").ap()
    biasbc_d = nc.dram_tensor("biasbc", [128, 8], F32, kind="ExternalInput").ap()
    upT_d = nc.dram_tensor("upT", [NE, D, HID], F32, kind="ExternalInput").ap()
    dnT_d = nc.dram_tensor("dnT", [NE, HID, D], F32, kind="ExternalInput").ap()
    out_d = nc.dram_tensor("out", [T, D], F32, kind="ExternalOutput").ap()
    xs_d = nc.dram_tensor("xs", [T, D], F32, kind="Internal").ap()

    P = Plan()
    dInit = Sem("dInit")
    dWA = [Sem("dWA0"), Sem("dWA1")]
    dWB = Sem("dWB")
    dXs = [Sem("dX0"), Sem("dX1")]
    dYs = [Sem("dY0"), Sem("dY1")]
    dM = [Sem("dM0"), Sem("dM1")]
    dR = [Sem("dR0"), Sem("dR1")]
    dG = [Sem(f"dG{i}") for i in range(4)]
    dDbg = Sem("dDbg")
    pe, act, vec, gp, bar = Sem("pe"), Sem("act"), Sem("vec"), Sem("gp"), Sem("bar")
    all_sems = [dInit, dWA[0], dWA[1], dWB, dXs[0], dXs[1], dYs[0], dYs[1],
                dM[0], dM[1], dR[0], dR[1]] + dG + [dDbg, pe, act, vec, gp, bar]

    def now(s):
        return (s, s.v)

    with ExitStack() as top:
        sb = lambda n_, sh, dt: top.enter_context(nc.sbuf_tensor(n_, sh, dt))
        ident = sb("ident", [128, 128], F32)
        ones1 = sb("ones1", [1, 128], BF16)
        iota8 = sb("iota8", [128, 8], F32)
        biasb = sb("biasb", [128, 8], F32)
        gw_sb = sb("gw_sb", [128, KC * 8], F32)
        xr = [sb(f"xr{i}", [128, D], F32) for i in range(2)]
        ybuf = [sb(f"ybuf{i}", [128, D], F32) for i in range(2)]
        stt_ = [sb(f"stt{i}", [128, 16], F32) for i in range(2)]
        PBbig = top.enter_context(nc.psum_tensor("PBbig", [128, 2048], F32))
        PB = [PBbig[:, 512 * i:512 * (i + 1)] for i in range(4)] + \
             [top.enter_context(nc.psum_tensor(f"PB{i}", [128, 512], F32)) for i in range(4, 8)]

        P.emit("gpsimd", lambda g: g.iota(ident[:, :], pattern=[[1, 128]], base=0,
                                          channel_multiplier=-1, allow_small_or_imprecise_dtypes=True),
               incs=[(gp, 1)])
        P.emit("gpsimd", lambda g: g.iota(iota8[:, :], pattern=[[1, 8]], base=0,
                                          channel_multiplier=0, allow_small_or_imprecise_dtypes=True),
               incs=[(gp, 1)])
        P.emit("gpsimd", lambda g: g.memset(ones1[:, :], 1.0), incs=[(gp, 1)])
        P.emit("gpsimd", lambda g: g.dma_start(out=biasb[:, :], in_=biasbc_d[:, :]), incs=[(dInit, 16)])
        P.emit("gpsimd", lambda g: g.dma_start(
            out=gw_sb[:, :].rearrange("p (k e) -> p k e", k=KC),
            in_=gwT_d[:, :].rearrange("(k p) e -> p k e", p=128)), incs=[(dInit, 16)])
        init_gp = now(gp)
        init_dW = now(dInit)
        P.emit("vector", lambda v: v.tensor_single_scalar(ident[:, :], ident[:, :], 0.0, ALU.is_equal),
               waits=[init_gp], incs=[(vec, 1)])
        ident_tok = now(vec)

        def engine_barrier():
            target = bar.v + 5
            dma_fence = {"sync": [now(q) for q in dXs + dYs],
                         "gpsimd": [now(dInit), now(dWA[0]), now(dWA[1]), now(dWB), now(dDbg)],
                         "scalar": [now(q) for q in dM + dR + dG]}
            for e in ("tensor", "vector", "scalar", "gpsimd", "sync"):
                P.emit(e, lambda eng: eng.nop(), waits=dma_fence.get(e, []), incs=[(bar, 1)])
            for e in ("tensor", "vector", "scalar", "gpsimd", "sync"):
                P.emit(e, lambda eng: eng.nop(), waits=[(bar, target)])

        def emit_ln_tile(war, src_dram, ti, r):
            st = stt_[r]
            mu, var = st[:, 12:13], st[:, 13:14]
            rstd, nm = st[:, 14:15], st[:, 15:16]
            P.emit("sync", lambda sy: sy.dma_start(out=xr[r][:, :],
                                                   in_=src_dram[ti * 128:(ti + 1) * 128, :]),
                   waits=war.pop(("xr", r), []), incs=[(dXs[r], 16)])
            ld = now(dXs[r])
            stw = war.pop(("st", r), [])
            ybw = war.pop(("ybuf", r), [])
            P.emit("vector", lambda v: v.bn_stats(st[:, 0:6], xr[r][:, 0:512]),
                   waits=[ld] + stw, incs=[(vec, 1)])
            P.emit("vector", lambda v: v.bn_stats(st[:, 6:12], xr[r][:, 512:1024]),
                   waits=[now(vec)], incs=[(vec, 1)])
            P.emit("vector", lambda v: v.bn_aggr(st[:, 12:14], st[:, 0:12]),
                   waits=[now(vec)], incs=[(vec, 1)])
            agg_tok = now(vec)
            P.emit("scalar", lambda s: s.activation(rstd, var, AF.Sqrt, bias=f32(1e-5)),
                   waits=[agg_tok], incs=[(act, 1)])
            sqr_tok = now(act)
            P.emit("vector", lambda v: v.reciprocal(rstd, rstd), waits=[sqr_tok], incs=[(vec, 1)])
            rstd_tok = now(vec)
            P.emit("vector", lambda v: v.scalar_tensor_tensor(nm, mu, -1.0, rstd, ALU.mult, ALU.mult),
                   waits=[rstd_tok], incs=[(vec, 1)])
            nm_tok = now(vec)
            P.emit("scalar", lambda s: s.activation(ybuf[r][:, :], xr[r][:, :], AF.Identity,
                                                    bias=nm, scale=rstd),
                   waits=[rstd_tok, nm_tok] + ybw, incs=[(act, 1)])
            y_tok = now(act)
            war[("st", r)] = [y_tok]
            return {"ld": ld, "y": y_tok}

        def ffn_phase(src_dram, dst_dram, tag):
            with ExitStack() as ph:
                psb = lambda n_, sh, dt: ph.enter_context(nc.sbuf_tensor(n_ + tag, sh, dt))
                yTh = psb("yTh", [128, KC * HALF], BF16)
                mmF = psb("mmF", [128, 128], F32)
                mTcF = psb("mTcF", [128, 128], BF16)
                m1r = [psb(f"m1r{i}", [1, CHUNK], BF16) for i in range(2)]
                sc_t = psb("sc_t", [128, 8], F32)
                bb_t = psb("bb_t", [128, 8], F32)
                t1_t = psb("t1_t", [128, 8], F32)
                oh_t = psb("oh_t", [128, 8], F32)
                v1_t = psb("v1_t", [128, 1], F32)
                i1_t = psb("i1_t", [128, 1], F32)
                WupA = [psb(f"WupA{i}", [128, KC * NH_A], BF16) for i in range(2)]
                WdnA = [psb(f"WdnA{i}", [128, NT_A * D], BF16) for i in range(2)]
                WupB = psb("WupB", [128, KC * NH_B], BF16)
                WdnB = psb("WdnB", [128, NT_B * D], BF16)
                ye = [psb(f"ye{i}", [128, KC * CHUNK], BF16) for i in range(2)]
                hh = [psb("hhA", [128, NT_A * CHUNK], BF16), psb("hhB", [128, NT_B * CHUNK], BF16)]
                dacc = psb("dacc", [128, HTILES * D], F32)
                yTf32 = dacc

                U = [PB[0], PB[1]]
                DD = [PB[2], PB[3]]
                MB = PB[4]
                TPy = [PB[5][:, 0:128], PB[6][:, 0:128]]
                G8 = PB[7][:, 0:8]
                TPm = PB[4][0:128, 0:128]

                war = {}

                def load_piece(e, piece, waits_):
                    if piece == 0:
                        n0, nt, wu, wd = 0, NT_A, WupA[e % 2], WdnA[e % 2]
                        qsem = dWA[e % 2]
                    else:
                        n0, nt, wu, wd = NH_A, NT_B, WupB, WdnB
                        qsem = dWB
                    P.emit("gpsimd", lambda g, e=e, n0=n0, nt=nt, wu=wu: g.dma_start(
                        out=wu[:, 0:KC * nt * 128].rearrange("p (k n) -> p k n", k=KC),
                        in_=upT_d[e:e + 1, :, n0:n0 + nt * 128].rearrange("o (k p) n -> p (o k) n", p=128)),
                        waits=waits_, incs=[(qsem, 16)])
                    P.emit("gpsimd", lambda g, e=e, n0=n0, nt=nt, wd=wd: g.dma_start(
                        out=wd[:, 0:nt * D].rearrange("p (k c) -> p k c", k=nt),
                        in_=dnT_d[e:e + 1, n0:n0 + nt * 128, :].rearrange("o (k p) c -> p (o k) c", p=128)),
                        waits=[], incs=[(qsem, 16)])
                    return now(qsem)

                mTh_tok = [None]

                def load_m1(e, ch, slot, waits_):
                    P.emit("scalar", lambda s, e=e, ch=ch, slot=slot: s.dma_start(
                        out=m1r[slot][0:1, :],
                        in_=mTcF[e * 16 + 4 * ch: e * 16 + 4 * ch + 4, 0:128]),
                           waits=waits_ + [mTh_tok[0]], incs=[(dM[slot], 16)])
                    return now(dM[slot])

                for half in range(2):
                    t0 = half * HTILES
                    for i in range(HTILES):
                        r = i % 2
                        ci, jj = i // 4, i % 4
                        toks = emit_ln_tile(war, src_dram, t0 + i, r)
                        evac_toks = []
                        ytfw = war.pop("yTf32", []) + (war.pop("dacc", []) if i == 0 else [])
                        for k in range(KC):
                            tp = (i * KC + k) % 2
                            P.emit("tensor", lambda t, k=k, r=r, tp=tp:
                                   t.transpose(TPy[tp], ybuf[r][:, k * 128:(k + 1) * 128], ident[:, :]),
                                   waits=[toks["y"], ident_tok] + war.pop(("tpy", tp), []),
                                   incs=[(pe, 1)])
                            tp_tok = now(pe)
                            if k % 2 == 0:
                                P.emit("scalar", lambda s, k=k, tp=tp:
                                       s.copy(yTf32[:, k * 128:(k + 1) * 128], TPy[tp]),
                                       waits=[tp_tok] + (ytfw if k == 0 else []),
                                       incs=[(act, 1)])
                                ev = now(act)
                            else:
                                P.emit("vector", lambda v, k=k, tp=tp:
                                       v.tensor_copy(yTf32[:, k * 128:(k + 1) * 128], TPy[tp]),
                                       waits=[tp_tok] + (ytfw if k == 1 else []),
                                       incs=[(vec, 1)])
                                ev = now(vec)
                            evac_toks.append(ev)
                            war[("tpy", tp)] = [ev]
                            P.emit("gpsimd", lambda g, k=k, i=i:
                                   g.tensor_copy(yTh[:, k * HALF + i * 128: k * HALF + (i + 1) * 128],
                                                 yTf32[:, k * 128:(k + 1) * 128]),
                                   waits=[ev] + (war.pop("yTh", []) if (i == 0 and k == 0) else []),
                                   incs=[(gp, 1)])
                        war[("ybuf", r)] = [now(pe)]
                        war[("xr", r)] = [toks["y"], now(vec)]
                        g8w = war.pop("g8", [])
                        for k in range(KC):
                            P.emit("tensor", lambda t, k=k:
                                   t.matmul(G8, yTf32[:, k * 128:(k + 1) * 128],
                                            gw_sb[:, k * 8:(k + 1) * 8],
                                            start=(k == 0), stop=(k == KC - 1), skip_group_check=True),
                                   waits=([evac_toks[KC - 2], evac_toks[KC - 1], init_dW] + g8w)
                                         if k == 0 else [],
                                   incs=[(pe, 1)] if k == KC - 1 else [])
                        gate_tok = now(pe)
                        war["yTf32"] = [gate_tok, now(gp)]
                        P.emit("scalar", lambda s: s.activation(sc_t[:, :], G8, AF.Sigmoid),
                               waits=[gate_tok], incs=[(act, 1)])
                        sig_tok = now(act)
                        war["g8"] = [sig_tok]
                        rw = war.pop("route", [])
                        P.emit("vector", lambda v: v.tensor_add(bb_t[:, :], sc_t[:, :], biasb[:, :]),
                               waits=[sig_tok, init_dW] + rw, incs=[(vec, 1)])
                        P.emit("vector", lambda v: v.reduce_max(v1_t[:, :], bb_t[:, :], axis=AX.X),
                               waits=[now(vec)], incs=[(vec, 1)])
                        P.emit("vector", lambda v: v.tensor_scalar(t1_t[:, :], bb_t[:, :], v1_t[:, 0:1],
                                                                   -100.0, ALU.is_ge, ALU.mult),
                               waits=[now(vec)], incs=[(vec, 1)])
                        P.emit("vector", lambda v: v.tensor_tensor(t1_t[:, :], t1_t[:, :], iota8[:, :],
                                                                   ALU.add),
                               waits=[init_gp, now(vec)], incs=[(vec, 1)])
                        P.emit("vector", lambda v: v.tensor_reduce(i1_t[:, :], t1_t[:, :], axis=AX.X,
                                                                   op=ALU.min),
                               waits=[now(vec)], incs=[(vec, 1)])
                        P.emit("vector", lambda v: v.tensor_single_scalar(oh_t[:, :], t1_t[:, :],
                                                                          i1_t[:, 0:1], ALU.is_equal),
                               waits=[now(vec)], incs=[(vec, 1)])
                        P.emit("vector", lambda v: v.scalar_tensor_tensor(bb_t[:, :], oh_t[:, :], -1e30,
                                                                          bb_t[:, :], ALU.mult, ALU.add),
                               waits=[now(vec)], incs=[(vec, 1)])
                        P.emit("vector", lambda v: v.reduce_max(v1_t[:, :], bb_t[:, :], axis=AX.X),
                               waits=[now(vec)], incs=[(vec, 1)])
                        P.emit("vector", lambda v: v.tensor_scalar(t1_t[:, :], bb_t[:, :], v1_t[:, 0:1],
                                                                   -100.0, ALU.is_ge, ALU.mult),
                               waits=[now(vec)], incs=[(vec, 1)])
                        P.emit("vector", lambda v: v.tensor_tensor(t1_t[:, :], t1_t[:, :], iota8[:, :],
                                                                   ALU.add),
                               waits=[now(vec)], incs=[(vec, 1)])
                        P.emit("vector", lambda v: v.tensor_reduce(i1_t[:, :], t1_t[:, :], axis=AX.X,
                                                                   op=ALU.min),
                               waits=[now(vec)], incs=[(vec, 1)])
                        P.emit("vector", lambda v: v.tensor_single_scalar(t1_t[:, :], t1_t[:, :],
                                                                          i1_t[:, 0:1], ALU.is_equal),
                               waits=[now(vec)], incs=[(vec, 1)])
                        P.emit("vector", lambda v: v.tensor_add(oh_t[:, :], oh_t[:, :], t1_t[:, :]),
                               waits=[now(vec)], incs=[(vec, 1)])
                        P.emit("vector", lambda v, i=i: v.tensor_mul(mmF[:, i:i + 113:16],
                                                                     sc_t[:, :], oh_t[:, :]),
                               waits=[now(vec)] + (war.pop("mmF", []) if i == 0 else []),
                               incs=[(vec, 1)])
                        m_tok = now(vec)
                        war["route"] = [m_tok]
                        if i == HTILES - 1:
                            P.emit("tensor", lambda t: t.transpose(TPm, mmF[:, :], ident[:, :]),
                                   waits=[m_tok] + war.pop("tpm", []) + war.pop("mb", []),
                                   incs=[(pe, 1)])
                            tpm_tok = now(pe)
                            P.emit("scalar", lambda s: s.copy(mTcF[:, :], TPm),
                                   waits=[tpm_tok] + war.pop("mTc", []),
                                   incs=[(act, 1)])
                            mTh_tok[0] = now(act)
                            war["tpm"] = [mTh_tok[0]]
                            war["mmF"] = [tpm_tok]
                    ycast_all = now(gp)

                    ptokA = {0: load_piece(0, 0, war.pop(("WA", 0), []))}
                    ptokB = load_piece(0, 1, war.pop("WB", []))
                    ptokA[1] = load_piece(1, 0, war.pop(("WA", 1), []))
                    m1toks = {}
                    gctr = 0

                    ye_toks = {}

                    def emit_mbye(e_, ch_):
                        nonlocal gctr
                        g_ = gctr
                        gctr += 1
                        slot = g_ % 2
                        yp_ = ch_ % 2
                        m1toks[g_] = load_m1(e_, ch_, slot, war.pop(("m1", slot), []))
                        mbw = war.pop("mb", [])
                        if g_ == 0:
                            mbw = mbw + [mTh_tok[0]]
                        P.emit("tensor", lambda t, slot=slot:
                               t.matmul(MB[:, :], ones1[0:1, :], m1r[slot][0:1, :],
                                        start=True, stop=True, skip_group_check=True),
                               waits=[m1toks[g_], init_gp] + mbw, incs=[(pe, 1)])
                        mb_tok = now(pe)
                        war[("m1", slot)] = [mb_tok]
                        yew = war.pop(("ye", yp_), [])
                        for k in range(KC):
                            P.emit("vector", lambda v, k=k, ch_=ch_, yp_=yp_:
                                   v.tensor_tensor(ye[yp_][:, k * CHUNK:(k + 1) * CHUNK],
                                                   yTh[:, k * HALF + ch_ * CHUNK:
                                                       k * HALF + (ch_ + 1) * CHUNK],
                                                   MB[:, :], ALU.mult),
                                   waits=([mb_tok, ycast_all] + yew) if k == 0 else [],
                                   incs=[(vec, 1)] if k == KC - 1 else [])
                        ye_toks[(e_, ch_)] = now(vec)
                        war["mb"] = [now(vec)]

                    emit_mbye(0, 0)
                    for e in range(NE):
                        routed = e < 8
                        lastAuse = []
                        lastBuse = []
                        for ch in range(HCHUNKS):
                            yp = ch % 2
                            if routed:
                                ye_tok = ye_toks.pop((e, ch))
                            for piece in range(2):
                                if piece == 0:
                                    nt, wu, wd, wtok = NT_A, WupA[e % 2], WdnA[e % 2], ptokA[e]
                                else:
                                    nt, wu, wd, wtok = NT_B, WupB, WdnB, ptokB
                                hb = piece
                                sq_last = None
                                for ntile in range(nt):
                                    ub = ntile % 2
                                    uw = war.pop(("U", ub), [])
                                    for k in range(KC):
                                        if routed:
                                            movap = (lambda k=k, yp=yp:
                                                     ye[yp][:, k * CHUNK:(k + 1) * CHUNK])
                                            mov_tok = ye_tok
                                        else:
                                            movap = (lambda k=k, ch=ch:
                                                     yTh[:, k * HALF + ch * CHUNK:
                                                         k * HALF + (ch + 1) * CHUNK])
                                            mov_tok = ycast_all
                                        P.emit("tensor", lambda t, ntile=ntile, k=k, nt=nt, wu=wu,
                                               ub=ub, movap=movap:
                                               t.matmul(U[ub][:, :],
                                                        wu[:, (k * nt + ntile) * 128:
                                                           (k * nt + ntile + 1) * 128],
                                                        movap(), start=(k == 0), stop=(k == KC - 1),
                                                        skip_group_check=True),
                                               waits=([wtok, mov_tok] + uw) if k == 0 else [],
                                               incs=[(pe, 1)] if k == KC - 1 else [])
                                    u_tok = now(pe)
                                    P.emit("scalar", lambda s, ntile=ntile, ub=ub, hb=hb:
                                           s.activation(hh[hb][:, ntile * CHUNK:(ntile + 1) * CHUNK],
                                                        U[ub][:, :], AF.Relu),
                                           waits=[u_tok] + (war.pop(("hhh", hb), [])
                                                            if ntile == 0 else []),
                                           incs=[(act, 1)])
                                    relu_tok = now(act)
                                    war[("U", ub)] = [relu_tok]
                                    P.emit("vector", lambda v, ntile=ntile, hb=hb:
                                           v.tensor_mul(hh[hb][:, ntile * CHUNK:(ntile + 1) * CHUNK],
                                                        hh[hb][:, ntile * CHUNK:(ntile + 1) * CHUNK],
                                                        hh[hb][:, ntile * CHUNK:(ntile + 1) * CHUNK]),
                                           waits=[relu_tok], incs=[(vec, 1)])
                                    sq_last = now(vec)
                                for tt in range(4):
                                    for cc in range(2):
                                        db = (tt * 2 + cc) % 2
                                        dw_ = war.pop(("D", db), [])
                                        for ntile in range(nt):
                                            P.emit("tensor", lambda t, tt=tt, cc=cc, ntile=ntile,
                                                   nt=nt, wd=wd, hb=hb, db=db:
                                                   t.matmul(DD[db][:, :],
                                                            hh[hb][:, ntile * CHUNK + tt * 128:
                                                                   ntile * CHUNK + (tt + 1) * 128],
                                                            wd[:, ntile * D + cc * 512:
                                                               ntile * D + (cc + 1) * 512],
                                                            start=(ntile == 0), stop=(ntile == nt - 1),
                                                            skip_group_check=True),
                                                   waits=([sq_last] + dw_) if ntile == 0 else [],
                                                   incs=[(pe, 1)] if ntile == nt - 1 else [])
                                        d_tok = now(pe)
                                        dslc = (lambda ch=ch, tt=tt, cc=cc:
                                                dacc[:, (ch * 4 + tt) * D + cc * 512:
                                                     (ch * 4 + tt) * D + (cc + 1) * 512])
                                        dchain = war.pop(("dch", ch, tt, cc), [])
                                        if e == 0 and piece == 0:
                                            P.emit("vector", lambda v, dslc=dslc, db=db:
                                                   v.tensor_copy(dslc(), DD[db][:, :]),
                                                   waits=[d_tok] + ([ycast_all]
                                                                    if (ch == 0 and tt == 0 and cc == 0)
                                                                    else []),
                                                   incs=[(vec, 1)])
                                        else:
                                            P.emit("vector", lambda v, dslc=dslc, db=db:
                                                   v.tensor_tensor(dslc(), dslc(), DD[db][:, :],
                                                                   ALU.add),
                                                   waits=[d_tok] + dchain, incs=[(vec, 1)])
                                        war[("D", db)] = [now(vec)]
                                        war[("dch", ch, tt, cc)] = [now(vec)]
                                if piece == 0:
                                    lastAuse = [now(pe)]
                                    if ch < HCHUNKS - 1 and routed:
                                        emit_mbye(e, ch + 1)
                                    elif ch == HCHUNKS - 1 and routed and e + 1 < 8:
                                        emit_mbye(e + 1, 0)
                                else:
                                    lastBuse = [now(pe)]
                                war[("hhh", hb)] = [now(pe)]
                            if routed:
                                war[("ye", yp)] = [now(pe)]
                        war[("WA", e % 2)] = lastAuse
                        war["WB"] = lastBuse
                        if e + 2 < NE:
                            ptokA[e + 2] = load_piece(e + 2, 0, war.pop(("WA", e % 2)))
                        if e + 1 < NE:
                            ptokB = load_piece(e + 1, 1, war.pop("WB"))
                    dacc_done = now(vec)

                    if dbg and half == 0 and tag == "f1":
                        P.emit("gpsimd", lambda g: g.dma_start(out=dbg_d["yTh"][:, :], in_=yTh[:, :]),
                               waits=[dacc_done], incs=[(dDbg, 16)])
                        P.emit("gpsimd", lambda g: g.dma_start(out=dbg_d["mTc"][:, :], in_=mTcF[:, :]),
                               waits=[], incs=[(dDbg, 16)])
                        P.emit("gpsimd", lambda g: g.dma_start(out=dbg_d["dacc"][:, :], in_=dacc[:, :]),
                               waits=[], incs=[(dDbg, 16)])
                        P.emit("gpsimd", lambda g: g.dma_start(out=dbg_d["ye"][:, :], in_=ye[1][:, :]),
                               waits=[], incs=[(dDbg, 16)])

                    for i in range(HTILES):
                        r = i % 2
                        P.emit("sync", lambda sy, i=i, r=r, t0=t0: sy.dma_start(
                            out=xr[r][:, :], in_=src_dram[(t0 + i) * 128:(t0 + i + 1) * 128, :]),
                               waits=[dacc_done] + war.pop(("xr", r), []), incs=[(dXs[r], 16)])
                        ld = now(dXs[r])
                        tap_guards = []
                        if dbg and half == 0 and tag == "f1" and i == 0:
                            P.emit("gpsimd", lambda g: g.dma_start(out=dbg_d["xs0"][0:128, :],
                                                                   in_=xr[r][:, :]),
                                   waits=[ld], incs=[(dDbg, 16)])
                            tap_guards.append(now(dDbg))
                        P.emit("vector", lambda v, i=i, r=r: v.scalar_tensor_tensor(
                            ybuf[r][:, :], dacc[:, i * D:(i + 1) * D], 0.5, xr[r][:, :],
                            ALU.mult, ALU.add),
                               waits=[ld] + war.pop(("ybuf", r), []), incs=[(vec, 1)])
                        stt_tok = now(vec)
                        yb_guards = []
                        if dbg and half == 0 and tag == "f1" and i == 0:
                            P.emit("gpsimd", lambda g: g.dma_start(out=dbg_d["xs0"][128:256, :],
                                                                   in_=ybuf[0][:, :]),
                                   waits=[stt_tok], incs=[(dDbg, 16)])
                            yb_guards.append(now(dDbg))
                        war[("xr", r)] = [stt_tok] + tap_guards
                        P.emit("sync", lambda sy, i=i, r=r, t0=t0: sy.dma_start(
                            out=dst_dram[(t0 + i) * 128:(t0 + i + 1) * 128, :], in_=ybuf[r][:, :]),
                               waits=[stt_tok], incs=[(dYs[r], 16)])
                        war[("ybuf", r)] = [now(dYs[r])] + yb_guards
                    if dbg and half == 0 and tag == "f1":
                        P.emit("gpsimd", lambda g: g.dma_start(out=dbg_d["xs0"][:, :],
                                                               in_=dst_dram[0:HALF, :]),
                               waits=[now(dYs[0]), now(dYs[1])], incs=[(dDbg, 16)])
                        P.emit("gpsimd", lambda g: g.dma_start(out=dbg_d["dacc2"][:, :], in_=dacc[:, :]),
                               waits=[now(vec)], incs=[(dDbg, 16)])
                    war["dacc"] = [now(vec)]
                    war["yTh"] = [now(pe), now(vec)]
                    war["mTc"] = [now(dM[0]), now(dM[1])]
            engine_barrier()

        def attn_phase(src_dram, dst_dram):
            with ExitStack() as ph:
                psb = lambda n_, sh, dt: ph.enter_context(nc.sbuf_tensor(n_ + "_at", sh, dt))
                iota512 = psb("iota512", [128, 512], F32)
                wqkv_sb = psb("wqkv", [128, KC * 192], BF16)
                wout_sb = psb("wout", [64, D], BF16)
                zT = [psb(f"zT{i}", [128, KC * CHUNK], BF16) for i in range(2)]
                qkT = psb("qkT", [128, NB * SEQ], BF16)
                v_sb = psb("v_sb", [128, NTILES * 64], BF16)
                vI = psb("vI", [128, 16 * 128], BF16)
                kc_r = psb("kc_r", [1, 2 * SEQ], BF16)
                qc_r = psb("qc_r", [1, 2 * SEQ], BF16)
                Esb = [psb(f"Esb{i}", [128, 4 * 512], BF16) for i in range(2)]
                masks = psb("masks", [128, 4 * 512], BF16)
                ndbuf = psb("ndbuf", [128, SEQ], F32)
                rden = psb("rden", [64, SEQ], F32)
                oT = psb("oT", [64, SEQ], BF16)
                stg = [psb(f"stg{i}", [2, 512], F32) for i in range(4)]

                TPz = [PB[5][:, 0:128], PB[6][:, 0:128]]
                QK = PB[7]
                Vp = PB[4][:, 0:64]
                SS = [PB[0], PB[1], PB[2], PB[3]]
                ND = [PB[5][0:2, :], PB[6][0:2, :]]
                OP = [PB[4], PB[7]]

                war = {}
                P.emit("gpsimd", lambda g: g.dma_start(
                    out=wqkv_sb[:, :].rearrange("p (k f) -> p k f", k=KC),
                    in_=wqkvT_d[:, :].rearrange("(k p) f -> p k f", p=128)), incs=[(dInit, 16)])
                P.emit("gpsimd", lambda g: g.dma_start(out=wout_sb[:, :], in_=woutT_d[:, :]),
                       incs=[(dInit, 16)])
                wqkv_tok = now(dInit)
                wout_tok = now(dInit)
                P.emit("gpsimd", lambda g: g.iota(iota512[:, :], pattern=[[1, 512]], base=0,
                                                  channel_multiplier=-1,
                                                  allow_small_or_imprecise_dtypes=True),
                       incs=[(gp, 1)])
                io5_tok = now(gp)
                for m in range(4):
                    P.emit("vector", lambda v, m=m: v.tensor_single_scalar(
                        masks[:, m * 512:(m + 1) * 512], iota512[:, :], float(128.0 * m), ALU.is_ge),
                           waits=[io5_tok], incs=[(vec, 1)])
                mask_tok = now(vec)

                for ch8 in range(8):
                    zr = ch8 % 2
                    last_evac = None
                    for j in range(4):
                        i = ch8 * 4 + j
                        r = i % 2
                        toks = emit_ln_tile(war, src_dram, i, r)
                        for k in range(KC):
                            tp = (i * KC + k) % 2
                            P.emit("tensor", lambda t, k=k, r=r, tp=tp:
                                   t.transpose(TPz[tp], ybuf[r][:, k * 128:(k + 1) * 128], ident[:, :]),
                                   waits=[toks["y"], ident_tok] + war.pop(("tpz", tp), []),
                                   incs=[(pe, 1)])
                            tp_tok = now(pe)
                            P.emit("scalar", lambda s, k=k, j=j, zr=zr, tp=tp:
                                   s.copy(zT[zr][:, k * CHUNK + j * 128: k * CHUNK + (j + 1) * 128],
                                          TPz[tp]),
                                   waits=[tp_tok] + (war.pop(("zslot", zr), [])
                                                     if (k == 0 and j == 0) else []),
                                   incs=[(act, 1)])
                            war[("tpz", tp)] = [now(act)]
                            last_evac = now(act)
                        war[("ybuf", r)] = [now(pe)]
                        war[("xr", r)] = [toks["y"], now(vec)]
                        vw = war.pop("vp", [])
                        for k in range(KC):
                            P.emit("tensor", lambda t, k=k, zr=zr, j=j:
                                   t.matmul(Vp, zT[zr][:, k * CHUNK + j * 128: k * CHUNK + (j + 1) * 128],
                                            wqkv_sb[:, k * 192 + 128: k * 192 + 192],
                                            start=(k == 0), stop=(k == KC - 1), skip_group_check=True),
                                   waits=([wqkv_tok, last_evac] + vw) if k == 0 else [],
                                   incs=[(pe, 1)] if k == KC - 1 else [])
                        vp_tok = now(pe)
                        P.emit("scalar", lambda s, i=i: s.copy(v_sb[:, i * 64:(i + 1) * 64], Vp),
                               waits=[vp_tok], incs=[(act, 1)])
                        war["vp"] = [now(act)]
                    qkw = war.pop("qk", [])
                    for k in range(KC):
                        P.emit("tensor", lambda t, k=k, zr=zr:
                               t.matmul(QK[:, :], wqkv_sb[:, k * 192: k * 192 + 128],
                                        zT[zr][:, k * CHUNK:(k + 1) * CHUNK],
                                        start=(k == 0), stop=(k == KC - 1), skip_group_check=True),
                               waits=([wqkv_tok, last_evac] + qkw) if k == 0 else [],
                               incs=[(pe, 1)] if k == KC - 1 else [])
                    qk_tok = now(pe)
                    b = ch8 // 4
                    pos = (ch8 % 4) * CHUNK
                    P.emit("scalar", lambda s, b=b, pos=pos: s.activation(
                        qkT[0:64, b * SEQ + pos: b * SEQ + pos + CHUNK], QK[0:64, :], AF.Identity,
                        scale=f32(0.125)),
                           waits=[qk_tok], incs=[(act, 1)])
                    P.emit("scalar", lambda s, b=b, pos=pos: s.copy(
                        qkT[64:128, b * SEQ + pos: b * SEQ + pos + CHUNK], QK[64:128, :]),
                           incs=[(act, 1)])
                    war["qk"] = [now(act)]
                    war[("zslot", zr)] = [now(pe)]
                qkT_tok = now(act)
                vsb_tok = now(act)

                for b in range(NB):
                    P.emit("gpsimd", lambda g: g.memset(vI[:, :], 1.0),
                           waits=[vsb_tok] + war.pop("vI", []), incs=[(gp, 1)])
                    vi0 = now(gp)
                    for jb in range(16):
                        P.emit("vector", lambda v, b=b, jb=jb: v.tensor_copy(
                            vI[:, jb * 128:(jb + 1) * 128].rearrange("p (h two) -> p h two", two=2)[:, :, 0:1],
                            v_sb[:, (b * 16 + jb) * 64:(b * 16 + jb + 1) * 64].rearrange("p (h o) -> p h o", o=1)),
                               waits=[vi0] if jb == 0 else [], incs=[(vec, 1)])
                    vI_tok = now(vec)

                    def load_rows(h, slot, waits_, b=b):
                        P.emit("scalar", lambda s, h=h, b=b, slot=slot: s.dma_start(
                            out=kc_r[0:1, slot * SEQ:(slot + 1) * SEQ],
                            in_=qkT[64 + h:65 + h, b * SEQ:(b + 1) * SEQ]),
                               waits=waits_ + [qkT_tok], incs=[(dR[slot], 16)])
                        P.emit("scalar", lambda s, h=h, b=b, slot=slot: s.dma_start(
                            out=qc_r[0:1, slot * SEQ:(slot + 1) * SEQ],
                            in_=qkT[h:h + 1, b * SEQ:(b + 1) * SEQ]),
                               waits=[], incs=[(dR[slot], 16)])
                        return now(dR[slot])

                    rowtok = {0: load_rows(0, 0, war.pop(("rows", 0), []))}
                    rowtok[1] = load_rows(1, 1, war.pop(("rows", 1), []))
                    pending = None

                    def flush_pending(p):
                        g, e_tok, ndb_, njb_, h_, xw_, ic_ev = p
                        eb = g % 2
                        for j2 in range(2):
                            jb = 2 * g + j2
                            P.emit("tensor", lambda t, h_=h_, jb=jb, eb=eb, ndb_=ndb_,
                                   njb_=njb_, j2=j2:
                                   t.matmul(ND[ndb_],
                                            vI[:, jb * 128 + 2 * h_: jb * 128 + 2 * h_ + 2],
                                            Esb[eb][:, 512 * j2:512 * (j2 + 1)],
                                            start=(jb == 0), stop=(jb == njb_ - 1),
                                            skip_group_check=True),
                                   waits=([vI_tok, e_tok] + xw_) if j2 == 0 else [],
                                   incs=[(pe, 1)] if j2 == 1 else [])
                        war[("Ebuf", eb)] = [now(pe)]
                        if ic_ev is not None:
                            nd_tok = now(pe)
                            ic = ic_ev
                            sg = (h_ * 4 + ic) % 4
                            P.emit("vector", lambda v, ndb_=ndb_, sg=sg: v.tensor_copy(
                                stg[sg][:, :], ND[ndb_]),
                                   waits=[nd_tok] + war.pop(("stg", sg), []), incs=[(vec, 1)])
                            st_tok = now(vec)
                            war[("nd", ndb_)] = [st_tok]
                            P.emit("scalar", lambda s, h_=h_, ic=ic, sg=sg: s.dma_start(
                                out=ndbuf[h_:h_ + 1, ic * 512:(ic + 1) * 512], in_=stg[sg][0:1, :]),
                                   waits=[st_tok] + war.pop("hscat", []) +
                                         (war.pop("ndbuf", []) if (h_ == 0 and ic == 0) else []),
                                   incs=[(dG[sg], 16)])
                            P.emit("scalar", lambda s, h_=h_, ic=ic, sg=sg: s.dma_start(
                                out=ndbuf[64 + h_:65 + h_, ic * 512:(ic + 1) * 512],
                                in_=stg[sg][1:2, :]),
                                   waits=[], incs=[(dG[sg], 16)])
                            war[("stg", sg)] = [now(dG[sg])]
                            war["hscat"] = [now(dG[sg])]

                    for h in range(64):
                        slot = h % 2
                        rtok = rowtok.pop(h)
                        for ic in range(4):
                            ndb = ic % 2
                            ndw = war.pop(("nd", ndb), [])
                            njb = 4 * (ic + 1)
                            ngroups = njb // 2
                            for g in range(ngroups):
                                pb = g % 2
                                eb = g % 2
                                ssw = war.pop(("SSg", pb), [])
                                for j2 in range(2):
                                    jb = 2 * g + j2
                                    P.emit("tensor", lambda t, ic=ic, jb=jb, slot=slot, pb=pb, j2=j2:
                                           t.matmul(SS[2 * pb + j2][:, :],
                                                    kc_r[0:1, slot * SEQ + jb * 128:
                                                         slot * SEQ + (jb + 1) * 128],
                                                    qc_r[0:1, slot * SEQ + ic * 512:
                                                         slot * SEQ + (ic + 1) * 512],
                                                    start=True, stop=True, skip_group_check=True),
                                           waits=([rtok] + ssw) if j2 == 0 else [],
                                           incs=[(pe, 1)] if j2 == 1 else [])
                                s_tok = now(pe)
                                ebw = war.pop(("Ebuf", eb), [])
                                P.emit("scalar", lambda s, eb=eb, pb=pb: s.activation(
                                    Esb[eb][:, 0:1024], PBbig[:, 1024 * pb:1024 * (pb + 1)],
                                    AF.Exp, bias=f32(-2.0)),
                                       waits=[s_tok] + ebw, incs=[(act, 1)])
                                e_tok = now(act)
                                war[("SSg", pb)] = [e_tok]
                                if g >= ngroups - 2:
                                    mof = (g - (ngroups - 2)) * 1024
                                    P.emit("vector", lambda v, eb=eb, mof=mof: v.tensor_tensor(
                                        Esb[eb][:, 0:1024], Esb[eb][:, 0:1024],
                                        masks[:, mof:mof + 1024], ALU.mult),
                                           waits=[e_tok, mask_tok], incs=[(vec, 1)])
                                    e_tok = now(vec)
                                if pending is not None:
                                    flush_pending(pending)
                                pending = (g, e_tok, ndb, njb, h,
                                           ndw if g == 0 else [],
                                           ic if g == ngroups - 1 else None)
                        if h + 2 < 64:
                            rowtok[h + 2] = load_rows(h + 2, slot, [now(pe)])
                    flush_pending(pending)
                    pending = None
                    scat_toks = [now(q) for q in dG]
                    P.emit("vector", lambda v: v.reciprocal(rden[:, :], ndbuf[64:128, :]),
                           waits=scat_toks + war.pop("rden", []), incs=[(vec, 1)])
                    P.emit("vector", lambda v: v.tensor_mul(oT[:, :], ndbuf[0:64, :], rden[:, :]),
                           waits=[now(vec)] + war.pop("oT", []), incs=[(vec, 1)])
                    oT_tok = now(vec)
                    war["ndbuf"] = [oT_tok]
                    war["rden"] = [oT_tok]
                    for j in range(16):
                        i = b * 16 + j
                        r = i % 2
                        optoks = []
                        for cc in range(2):
                            opw = war.pop(("OP", cc), [])
                            P.emit("tensor", lambda t, j=j, cc=cc:
                                   t.matmul(OP[cc][:, :], oT[:, j * 128:(j + 1) * 128],
                                            wout_sb[:, cc * 512:(cc + 1) * 512],
                                            start=True, stop=True, skip_group_check=True),
                                   waits=[oT_tok, wout_tok] + opw, incs=[(pe, 1)])
                            optoks.append(now(pe))
                        P.emit("sync", lambda sy, i=i, r=r: sy.dma_start(
                            out=xr[r][:, :], in_=src_dram[i * 128:(i + 1) * 128, :]),
                               waits=war.pop(("xr", r), []), incs=[(dXs[r], 16)])
                        ld = now(dXs[r])
                        ybw = war.pop(("ybuf", r), [])
                        for cc in range(2):
                            P.emit("vector", lambda v, r=r, cc=cc: v.tensor_tensor(
                                ybuf[r][:, cc * 512:(cc + 1) * 512], xr[r][:, cc * 512:(cc + 1) * 512],
                                OP[cc][:, :], ALU.add),
                                   waits=[ld, optoks[cc]] + (ybw if cc == 0 else []),
                                   incs=[(vec, 1)])
                            war[("OP", cc)] = [now(vec)]
                        stt_tok = now(vec)
                        war[("xr", r)] = [stt_tok]
                        P.emit("sync", lambda sy, i=i, r=r: sy.dma_start(
                            out=dst_dram[i * 128:(i + 1) * 128, :], in_=ybuf[r][:, :]),
                               waits=[stt_tok], incs=[(dYs[r], 16)])
                        war[("ybuf", r)] = [now(dYs[r])]
                    war["oT"] = [now(pe)]
                    war["vI"] = [now(pe)]
                    war[("rows", 0)] = [now(pe)]
                    war[("rows", 1)] = [now(pe)]
            engine_barrier()

        if "F1" in phases:
            ffn_phase(x_d, xs_d, "f1")
            a_src = xs_d
        else:
            a_src = x_d
        if "A" in phases:
            attn_phase(a_src, xs_d)
            f2_src = xs_d
        else:
            f2_src = a_src
        if "F2" in phases:
            ffn_phase(f2_src, out_d, "f2")
        else:
            prev = None
            for i in range(NTILES):
                P.emit("sync", lambda sy, i=i: sy.dma_start(out=xr[0][:, :],
                                                            in_=f2_src[i * 128:(i + 1) * 128, :]),
                       waits=[prev] if prev else [], incs=[(dXs[0], 16)])
                ld = now(dXs[0])
                P.emit("sync", lambda sy, i=i: sy.dma_start(out=out_d[i * 128:(i + 1) * 128, :],
                                                            in_=xr[0][:, :]),
                       waits=[ld], incs=[(dYs[0], 16)])
                prev = now(dYs[0])

        P.emit("sync", lambda sy: sy.nop(), waits=[now(dYs[0]), now(dYs[1])])

        with ExitStack() as sctx:
            for s in all_sems:
                s.h = sctx.enter_context(nc.semaphore(s.name))
            block = sctx.enter_context(nc.Block())
            block.tensor(lambda t: P.run("tensor", t))
            block.vector(lambda v: P.run("vector", v))
            block.scalar(lambda s: P.run("scalar", s))
            block.gpsimd(lambda g: P.run("gpsimd", g))
            block.sync(lambda sy: P.run("sync", sy))
    return nc


def kernel(L0_attn_x, L0_attn_w_qkv, L0_attn_w_out, L0_ffn_gate_w, L0_ffn_expert_bias,
           L0_ffn_w_up, L0_ffn_w_down, L0_ffn_shared_w_up, L0_ffn_shared_w_down):
    x = np.ascontiguousarray(np.asarray(L0_attn_x, dtype=np.float32))
    wqkvT = np.ascontiguousarray(np.asarray(L0_attn_w_qkv, np.float32).T)
    woutT = np.ascontiguousarray(np.asarray(L0_attn_w_out, np.float32).T)
    gwT = np.ascontiguousarray(np.asarray(L0_ffn_gate_w, np.float32).T)
    biasbc = np.ascontiguousarray(
        np.tile(np.asarray(L0_ffn_expert_bias, np.float32)[None, :], (128, 1)))
    up = np.asarray(L0_ffn_w_up, np.float32)
    dn = np.asarray(L0_ffn_w_down, np.float32)
    sup = np.asarray(L0_ffn_shared_w_up, np.float32)
    sdn = np.asarray(L0_ffn_shared_w_down, np.float32)
    upT = np.ascontiguousarray(np.concatenate([up.transpose(0, 2, 1), sup.T[None]], axis=0))
    dnT = np.ascontiguousarray(np.concatenate([dn.transpose(0, 2, 1), sdn.T[None]], axis=0))

    nc = build_nc()
    in_maps = []
    for c in range(NCORES):
        xs = np.ascontiguousarray(x[2 * c:2 * c + 2].reshape(T, D))
        in_maps.append(dict(x=xs, wqkvT=wqkvT, woutT=woutT, gwT=gwT, biasbc=biasbc,
                            upT=upT, dnT=dnT))
    res = run_bass_kernel_spmd(nc, in_maps, core_ids=list(range(NCORES)))
    out = np.concatenate([np.asarray(res.results[c]["out"]).reshape(2, SEQ, D)
                          for c in range(NCORES)], axis=0)
    return out.astype(np.float32)


# version 07
# speedup vs baseline: 1.6766x; 1.0361x over previous
import numpy as np
from contextlib import ExitStack

import concourse.bass as bass
import concourse.mybir as mybir
from concourse.bass_utils import run_bass_kernel_spmd

F32 = mybir.dt.float32
BF16 = mybir.dt.bfloat16
AF = mybir.ActivationFunctionType
ALU = mybir.AluOpType
AX = mybir.AxisListType
f32 = np.float32

NCORES = 8
D = 1024
KC = 8
HID = 1408
NH_A, NT_A = 640, 5
NH_B, NT_B = 768, 6
NE = 9
SEQ = 2048
NB = 2
T = NB * SEQ
NTILES = T // 128
HALF = 2048
HTILES = HALF // 128
CHUNK = 512
HCHUNKS = HALF // CHUNK


class Sem:
    __slots__ = ("name", "v", "h")

    def __init__(self, name):
        self.name, self.v, self.h = name, 0, None


class Plan:
    def __init__(self):
        self.ops = []

    def emit(self, engine, fn, waits=(), incs=()):
        ws = [(s, v) for (s, v) in waits if v > 0]
        ri = []
        for (s, k) in incs:
            s.v += k
            ri.append((s, k))
        self.ops.append((engine, ws, fn, ri))

    def run(self, engine_name, eng):
        for e, ws, fn, ri in self.ops:
            if e != engine_name:
                continue
            merged = {}
            for (s, v) in ws:
                if s.name in merged:
                    merged[s.name] = (s, max(v, merged[s.name][1]))
                else:
                    merged[s.name] = (s, v)
            for (s, v) in merged.values():
                eng.wait_ge(s.h, v)
            inst = fn(eng)
            for (s, k) in ri:
                inst.then_inc(s.h, k)


def build_nc(phases=("F1", "A", "F2"), dbg=False):
    nc = bass.Bass("TRN2", target_bir_lowering=False, debug=False)
    dbg_d = {}
    if dbg:
        dbg_d["yTh"] = nc.dram_tensor("dbg_yTh", [128, KC * HALF], F32, kind="ExternalOutput").ap()
        dbg_d["mTc"] = nc.dram_tensor("dbg_mTc", [128, 128], F32, kind="ExternalOutput").ap()
        dbg_d["dacc"] = nc.dram_tensor("dbg_dacc", [128, HTILES * D], F32, kind="ExternalOutput").ap()
        dbg_d["ye"] = nc.dram_tensor("dbg_ye", [128, KC * CHUNK], F32, kind="ExternalOutput").ap()
        dbg_d["xs0"] = nc.dram_tensor("dbg_xs0", [HALF, D], F32, kind="ExternalOutput").ap()
        dbg_d["dacc2"] = nc.dram_tensor("dbg_dacc2", [128, HTILES * D], F32, kind="ExternalOutput").ap()

    x_d = nc.dram_tensor("x", [T, D], F32, kind="ExternalInput").ap()
    wqkvT_d = nc.dram_tensor("wqkvT", [D, 192], F32, kind="ExternalInput").ap()
    woutT_d = nc.dram_tensor("woutT", [64, D], F32, kind="ExternalInput").ap()
    gwT_d = nc.dram_tensor("gwT", [D, 8], F32, kind="ExternalInput").ap()
    biasbc_d = nc.dram_tensor("biasbc", [128, 8], F32, kind="ExternalInput").ap()
    upT_d = nc.dram_tensor("upT", [NE, D, HID], F32, kind="ExternalInput").ap()
    dnT_d = nc.dram_tensor("dnT", [NE, HID, D], F32, kind="ExternalInput").ap()
    out_d = nc.dram_tensor("out", [T, D], F32, kind="ExternalOutput").ap()
    xs_d = nc.dram_tensor("xs", [T, D], F32, kind="Internal").ap()

    P = Plan()
    dInit = Sem("dInit")
    dWA = [Sem("dWA0"), Sem("dWA1")]
    dWB = Sem("dWB")
    dXs = [Sem("dX0"), Sem("dX1")]
    dYs = [Sem("dY0"), Sem("dY1")]
    dM = [Sem("dM0"), Sem("dM1")]
    dR = [Sem("dR0"), Sem("dR1")]
    dG = [Sem(f"dG{i}") for i in range(4)]
    dDbg = Sem("dDbg")
    pe, act, vec, gp, bar = Sem("pe"), Sem("act"), Sem("vec"), Sem("gp"), Sem("bar")
    all_sems = [dInit, dWA[0], dWA[1], dWB, dXs[0], dXs[1], dYs[0], dYs[1],
                dM[0], dM[1], dR[0], dR[1]] + dG + [dDbg, pe, act, vec, gp, bar]

    def now(s):
        return (s, s.v)

    with ExitStack() as top:
        sb = lambda n_, sh, dt: top.enter_context(nc.sbuf_tensor(n_, sh, dt))
        ident = sb("ident", [128, 128], F32)
        ones1 = sb("ones1", [1, 128], BF16)
        iota8 = sb("iota8", [128, 8], F32)
        biasb = sb("biasb", [128, 8], F32)
        gw_sb = sb("gw_sb", [128, KC * 8], F32)
        xr = [sb(f"xr{i}", [128, D], F32) for i in range(2)]
        ybuf = [sb(f"ybuf{i}", [128, D], F32) for i in range(2)]
        stt_ = [sb(f"stt{i}", [128, 16], F32) for i in range(2)]
        PBbig = top.enter_context(nc.psum_tensor("PBbig", [128, 2048], F32))
        PB = [PBbig[:, 512 * i:512 * (i + 1)] for i in range(4)] + \
             [top.enter_context(nc.psum_tensor(f"PB{i}", [128, 512], F32)) for i in range(4, 8)]

        P.emit("gpsimd", lambda g: g.iota(ident[:, :], pattern=[[1, 128]], base=0,
                                          channel_multiplier=-1, allow_small_or_imprecise_dtypes=True),
               incs=[(gp, 1)])
        P.emit("gpsimd", lambda g: g.iota(iota8[:, :], pattern=[[1, 8]], base=0,
                                          channel_multiplier=0, allow_small_or_imprecise_dtypes=True),
               incs=[(gp, 1)])
        P.emit("gpsimd", lambda g: g.memset(ones1[:, :], 1.0), incs=[(gp, 1)])
        P.emit("gpsimd", lambda g: g.dma_start(out=biasb[:, :], in_=biasbc_d[:, :]), incs=[(dInit, 16)])
        P.emit("gpsimd", lambda g: g.dma_start(
            out=gw_sb[:, :].rearrange("p (k e) -> p k e", k=KC),
            in_=gwT_d[:, :].rearrange("(k p) e -> p k e", p=128)), incs=[(dInit, 16)])
        init_gp = now(gp)
        init_dW = now(dInit)
        P.emit("vector", lambda v: v.tensor_single_scalar(ident[:, :], ident[:, :], 0.0, ALU.is_equal),
               waits=[init_gp], incs=[(vec, 1)])
        ident_tok = now(vec)

        def engine_barrier():
            target = bar.v + 5
            dma_fence = {"sync": [now(q) for q in dXs + dYs],
                         "gpsimd": [now(dInit), now(dWA[0]), now(dWA[1]), now(dWB), now(dDbg)],
                         "scalar": [now(q) for q in dM + dR + dG]}
            for e in ("tensor", "vector", "scalar", "gpsimd", "sync"):
                P.emit(e, lambda eng: eng.nop(), waits=dma_fence.get(e, []), incs=[(bar, 1)])
            for e in ("tensor", "vector", "scalar", "gpsimd", "sync"):
                P.emit(e, lambda eng: eng.nop(), waits=[(bar, target)])

        def emit_ln_tile(war, src_dram, ti, r):
            st = stt_[r]
            mu, var = st[:, 12:13], st[:, 13:14]
            rstd, nm = st[:, 14:15], st[:, 15:16]
            P.emit("sync", lambda sy: sy.dma_start(out=xr[r][:, :],
                                                   in_=src_dram[ti * 128:(ti + 1) * 128, :]),
                   waits=war.pop(("xr", r), []), incs=[(dXs[r], 16)])
            ld = now(dXs[r])
            stw = war.pop(("st", r), [])
            ybw = war.pop(("ybuf", r), [])
            P.emit("vector", lambda v: v.bn_stats(st[:, 0:6], xr[r][:, 0:512]),
                   waits=[ld] + stw, incs=[(vec, 1)])
            P.emit("vector", lambda v: v.bn_stats(st[:, 6:12], xr[r][:, 512:1024]),
                   waits=[now(vec)], incs=[(vec, 1)])
            P.emit("vector", lambda v: v.bn_aggr(st[:, 12:14], st[:, 0:12]),
                   waits=[now(vec)], incs=[(vec, 1)])
            agg_tok = now(vec)
            P.emit("scalar", lambda s: s.activation(rstd, var, AF.Sqrt, bias=f32(1e-5)),
                   waits=[agg_tok], incs=[(act, 1)])
            sqr_tok = now(act)
            P.emit("vector", lambda v: v.reciprocal(rstd, rstd), waits=[sqr_tok], incs=[(vec, 1)])
            rstd_tok = now(vec)
            P.emit("vector", lambda v: v.scalar_tensor_tensor(nm, mu, -1.0, rstd, ALU.mult, ALU.mult),
                   waits=[rstd_tok], incs=[(vec, 1)])
            nm_tok = now(vec)
            P.emit("scalar", lambda s: s.activation(ybuf[r][:, :], xr[r][:, :], AF.Identity,
                                                    bias=nm, scale=rstd),
                   waits=[rstd_tok, nm_tok] + ybw, incs=[(act, 1)])
            y_tok = now(act)
            war[("st", r)] = [y_tok]
            return {"ld": ld, "y": y_tok}

        def ffn_phase(src_dram, dst_dram, tag):
            with ExitStack() as ph:
                psb = lambda n_, sh, dt: ph.enter_context(nc.sbuf_tensor(n_ + tag, sh, dt))
                yTh = psb("yTh", [128, KC * HALF], BF16)
                mmF = psb("mmF", [128, 128], F32)
                mTcF = psb("mTcF", [128, 128], BF16)
                m1r = [psb(f"m1r{i}", [1, CHUNK], BF16) for i in range(2)]
                sc_t = psb("sc_t", [128, 8], F32)
                bb_t = psb("bb_t", [128, 8], F32)
                t1_t = psb("t1_t", [128, 8], F32)
                oh_t = psb("oh_t", [128, 8], F32)
                v1_t = psb("v1_t", [128, 1], F32)
                i1_t = psb("i1_t", [128, 1], F32)
                WupA = [psb(f"WupA{i}", [128, KC * NH_A], BF16) for i in range(2)]
                WdnA = [psb(f"WdnA{i}", [128, NT_A * D], BF16) for i in range(2)]
                WupB = psb("WupB", [128, KC * NH_B], BF16)
                WdnB = psb("WdnB", [128, NT_B * D], BF16)
                ye = [psb(f"ye{i}", [128, KC * CHUNK], BF16) for i in range(2)]
                hh = [psb("hhA", [128, NT_A * CHUNK], BF16), psb("hhB", [128, NT_B * CHUNK], BF16)]
                dacc = psb("dacc", [128, HTILES * D], F32)
                yTf32 = dacc

                U = [PB[0], PB[1]]
                DD = [PB[2], PB[3]]
                MB = PB[4]
                TPy = [PB[5][:, 0:128], PB[6][:, 0:128]]
                G8 = PB[7][:, 0:8]
                TPm = PB[4][0:128, 0:128]

                war = {}

                def load_piece(e, piece, waits_):
                    if piece == 0:
                        n0, nt, wu, wd = 0, NT_A, WupA[e % 2], WdnA[e % 2]
                        qsem = dWA[e % 2]
                    else:
                        n0, nt, wu, wd = NH_A, NT_B, WupB, WdnB
                        qsem = dWB
                    P.emit("gpsimd", lambda g, e=e, n0=n0, nt=nt, wu=wu: g.dma_start(
                        out=wu[:, 0:KC * nt * 128].rearrange("p (k n) -> p k n", k=KC),
                        in_=upT_d[e:e + 1, :, n0:n0 + nt * 128].rearrange("o (k p) n -> p (o k) n", p=128)),
                        waits=waits_, incs=[(qsem, 16)])
                    P.emit("gpsimd", lambda g, e=e, n0=n0, nt=nt, wd=wd: g.dma_start(
                        out=wd[:, 0:nt * D].rearrange("p (k c) -> p k c", k=nt),
                        in_=dnT_d[e:e + 1, n0:n0 + nt * 128, :].rearrange("o (k p) c -> p (o k) c", p=128)),
                        waits=[], incs=[(qsem, 16)])
                    return now(qsem)

                mTh_tok = [None]

                def load_m1(e, ch, slot, waits_):
                    P.emit("scalar", lambda s, e=e, ch=ch, slot=slot: s.dma_start(
                        out=m1r[slot][0:1, :],
                        in_=mTcF[e * 16 + 4 * ch: e * 16 + 4 * ch + 4, 0:128]),
                           waits=waits_ + [mTh_tok[0]], incs=[(dM[slot], 16)])
                    return now(dM[slot])

                for half in range(2):
                    t0 = half * HTILES
                    for i in range(HTILES):
                        r = i % 2
                        ci, jj = i // 4, i % 4
                        toks = emit_ln_tile(war, src_dram, t0 + i, r)
                        evac_toks = []
                        ytfw = war.pop("yTf32", []) + (war.pop("dacc", []) if i == 0 else [])
                        for k in range(KC):
                            tp = (i * KC + k) % 2
                            P.emit("tensor", lambda t, k=k, r=r, tp=tp:
                                   t.transpose(TPy[tp], ybuf[r][:, k * 128:(k + 1) * 128], ident[:, :]),
                                   waits=[toks["y"], ident_tok] + war.pop(("tpy", tp), []),
                                   incs=[(pe, 1)])
                            tp_tok = now(pe)
                            if k % 2 == 0:
                                P.emit("scalar", lambda s, k=k, tp=tp:
                                       s.copy(yTf32[:, k * 128:(k + 1) * 128], TPy[tp]),
                                       waits=[tp_tok] + (ytfw if k == 0 else []),
                                       incs=[(act, 1)])
                                ev = now(act)
                            else:
                                P.emit("vector", lambda v, k=k, tp=tp:
                                       v.tensor_copy(yTf32[:, k * 128:(k + 1) * 128], TPy[tp]),
                                       waits=[tp_tok] + (ytfw if k == 1 else []),
                                       incs=[(vec, 1)])
                                ev = now(vec)
                            evac_toks.append(ev)
                            war[("tpy", tp)] = [ev]
                            P.emit("gpsimd", lambda g, k=k, i=i:
                                   g.tensor_copy(yTh[:, k * HALF + i * 128: k * HALF + (i + 1) * 128],
                                                 yTf32[:, k * 128:(k + 1) * 128]),
                                   waits=[ev] + (war.pop("yTh", []) if (i == 0 and k == 0) else []),
                                   incs=[(gp, 1)])
                        war[("ybuf", r)] = [now(pe)]
                        war[("xr", r)] = [toks["y"], now(vec)]
                        g8w = war.pop("g8", [])
                        for k in range(KC):
                            P.emit("tensor", lambda t, k=k:
                                   t.matmul(G8, yTf32[:, k * 128:(k + 1) * 128],
                                            gw_sb[:, k * 8:(k + 1) * 8],
                                            start=(k == 0), stop=(k == KC - 1), skip_group_check=True),
                                   waits=([evac_toks[KC - 2], evac_toks[KC - 1], init_dW] + g8w)
                                         if k == 0 else [],
                                   incs=[(pe, 1)] if k == KC - 1 else [])
                        gate_tok = now(pe)
                        war["yTf32"] = [gate_tok, now(gp)]
                        P.emit("scalar", lambda s: s.activation(sc_t[:, :], G8, AF.Sigmoid),
                               waits=[gate_tok], incs=[(act, 1)])
                        sig_tok = now(act)
                        war["g8"] = [sig_tok]
                        rw = war.pop("route", [])
                        P.emit("vector", lambda v: v.tensor_add(bb_t[:, :], sc_t[:, :], biasb[:, :]),
                               waits=[sig_tok, init_dW] + rw, incs=[(vec, 1)])
                        P.emit("vector", lambda v: v.reduce_max(v1_t[:, :], bb_t[:, :], axis=AX.X),
                               waits=[now(vec)], incs=[(vec, 1)])
                        P.emit("vector", lambda v: v.tensor_scalar(t1_t[:, :], bb_t[:, :], v1_t[:, 0:1],
                                                                   -100.0, ALU.is_ge, ALU.mult),
                               waits=[now(vec)], incs=[(vec, 1)])
                        P.emit("vector", lambda v: v.tensor_tensor(t1_t[:, :], t1_t[:, :], iota8[:, :],
                                                                   ALU.add),
                               waits=[init_gp, now(vec)], incs=[(vec, 1)])
                        P.emit("vector", lambda v: v.tensor_reduce(i1_t[:, :], t1_t[:, :], axis=AX.X,
                                                                   op=ALU.min),
                               waits=[now(vec)], incs=[(vec, 1)])
                        P.emit("vector", lambda v: v.tensor_single_scalar(oh_t[:, :], t1_t[:, :],
                                                                          i1_t[:, 0:1], ALU.is_equal),
                               waits=[now(vec)], incs=[(vec, 1)])
                        P.emit("vector", lambda v: v.scalar_tensor_tensor(bb_t[:, :], oh_t[:, :], -1e30,
                                                                          bb_t[:, :], ALU.mult, ALU.add),
                               waits=[now(vec)], incs=[(vec, 1)])
                        P.emit("vector", lambda v: v.reduce_max(v1_t[:, :], bb_t[:, :], axis=AX.X),
                               waits=[now(vec)], incs=[(vec, 1)])
                        P.emit("vector", lambda v: v.tensor_scalar(t1_t[:, :], bb_t[:, :], v1_t[:, 0:1],
                                                                   -100.0, ALU.is_ge, ALU.mult),
                               waits=[now(vec)], incs=[(vec, 1)])
                        P.emit("vector", lambda v: v.tensor_tensor(t1_t[:, :], t1_t[:, :], iota8[:, :],
                                                                   ALU.add),
                               waits=[now(vec)], incs=[(vec, 1)])
                        P.emit("vector", lambda v: v.tensor_reduce(i1_t[:, :], t1_t[:, :], axis=AX.X,
                                                                   op=ALU.min),
                               waits=[now(vec)], incs=[(vec, 1)])
                        P.emit("vector", lambda v: v.tensor_single_scalar(t1_t[:, :], t1_t[:, :],
                                                                          i1_t[:, 0:1], ALU.is_equal),
                               waits=[now(vec)], incs=[(vec, 1)])
                        P.emit("vector", lambda v: v.tensor_add(oh_t[:, :], oh_t[:, :], t1_t[:, :]),
                               waits=[now(vec)], incs=[(vec, 1)])
                        P.emit("vector", lambda v, i=i: v.tensor_mul(mmF[:, i:i + 113:16],
                                                                     sc_t[:, :], oh_t[:, :]),
                               waits=[now(vec)] + (war.pop("mmF", []) if i == 0 else []),
                               incs=[(vec, 1)])
                        m_tok = now(vec)
                        war["route"] = [m_tok]
                        if i == HTILES - 1:
                            P.emit("tensor", lambda t: t.transpose(TPm, mmF[:, :], ident[:, :]),
                                   waits=[m_tok] + war.pop("tpm", []) + war.pop("mb", []),
                                   incs=[(pe, 1)])
                            tpm_tok = now(pe)
                            P.emit("scalar", lambda s: s.copy(mTcF[:, :], TPm),
                                   waits=[tpm_tok] + war.pop("mTc", []),
                                   incs=[(act, 1)])
                            mTh_tok[0] = now(act)
                            war["tpm"] = [mTh_tok[0]]
                            war["mmF"] = [tpm_tok]
                    ycast_all = now(gp)

                    ptokA = {0: load_piece(0, 0, war.pop(("WA", 0), []))}
                    ptokB = load_piece(0, 1, war.pop("WB", []))
                    ptokA[1] = load_piece(1, 0, war.pop(("WA", 1), []))
                    m1toks = {}
                    gctr = 0

                    ye_toks = {}

                    def emit_mbye(e_, ch_):
                        nonlocal gctr
                        g_ = gctr
                        gctr += 1
                        slot = g_ % 2
                        yp_ = ch_ % 2
                        m1toks[g_] = load_m1(e_, ch_, slot, war.pop(("m1", slot), []))
                        mbw = war.pop("mb", [])
                        if g_ == 0:
                            mbw = mbw + [mTh_tok[0]]
                        P.emit("tensor", lambda t, slot=slot:
                               t.matmul(MB[:, :], ones1[0:1, :], m1r[slot][0:1, :],
                                        start=True, stop=True, skip_group_check=True),
                               waits=[m1toks[g_], init_gp] + mbw, incs=[(pe, 1)])
                        mb_tok = now(pe)
                        war[("m1", slot)] = [mb_tok]
                        yew = war.pop(("ye", yp_), [])
                        for k in range(KC):
                            P.emit("vector", lambda v, k=k, ch_=ch_, yp_=yp_:
                                   v.tensor_tensor(ye[yp_][:, k * CHUNK:(k + 1) * CHUNK],
                                                   yTh[:, k * HALF + ch_ * CHUNK:
                                                       k * HALF + (ch_ + 1) * CHUNK],
                                                   MB[:, :], ALU.mult),
                                   waits=([mb_tok, ycast_all] + yew) if k == 0 else [],
                                   incs=[(vec, 1)] if k == KC - 1 else [])
                        ye_toks[(e_, ch_)] = now(vec)
                        war["mb"] = [now(vec)]

                    emit_mbye(0, 0)
                    for e in range(NE):
                        routed = e < 8
                        lastAuse = []
                        lastBuse = []
                        for ch in range(HCHUNKS):
                            yp = ch % 2
                            if routed:
                                ye_tok = ye_toks.pop((e, ch))
                            for piece in range(2):
                                if piece == 0:
                                    nt, wu, wd, wtok = NT_A, WupA[e % 2], WdnA[e % 2], ptokA[e]
                                else:
                                    nt, wu, wd, wtok = NT_B, WupB, WdnB, ptokB
                                hb = piece
                                sq_last = None
                                for ntile in range(nt):
                                    ub = ntile % 2
                                    uw = war.pop(("U", ub), [])
                                    for k in range(KC):
                                        if routed:
                                            movap = (lambda k=k, yp=yp:
                                                     ye[yp][:, k * CHUNK:(k + 1) * CHUNK])
                                            mov_tok = ye_tok
                                        else:
                                            movap = (lambda k=k, ch=ch:
                                                     yTh[:, k * HALF + ch * CHUNK:
                                                         k * HALF + (ch + 1) * CHUNK])
                                            mov_tok = ycast_all
                                        P.emit("tensor", lambda t, ntile=ntile, k=k, nt=nt, wu=wu,
                                               ub=ub, movap=movap:
                                               t.matmul(U[ub][:, :],
                                                        wu[:, (k * nt + ntile) * 128:
                                                           (k * nt + ntile + 1) * 128],
                                                        movap(), start=(k == 0), stop=(k == KC - 1),
                                                        skip_group_check=True),
                                               waits=([wtok, mov_tok] + uw) if k == 0 else [],
                                               incs=[(pe, 1)] if k == KC - 1 else [])
                                    u_tok = now(pe)
                                    P.emit("scalar", lambda s, ntile=ntile, ub=ub, hb=hb:
                                           s.activation(hh[hb][:, ntile * CHUNK:(ntile + 1) * CHUNK],
                                                        U[ub][:, :], AF.Relu),
                                           waits=[u_tok] + (war.pop(("hhh", hb), [])
                                                            if ntile == 0 else []),
                                           incs=[(act, 1)])
                                    relu_tok = now(act)
                                    war[("U", ub)] = [relu_tok]
                                    P.emit("vector", lambda v, ntile=ntile, hb=hb:
                                           v.tensor_mul(hh[hb][:, ntile * CHUNK:(ntile + 1) * CHUNK],
                                                        hh[hb][:, ntile * CHUNK:(ntile + 1) * CHUNK],
                                                        hh[hb][:, ntile * CHUNK:(ntile + 1) * CHUNK]),
                                           waits=[relu_tok], incs=[(vec, 1)])
                                    sq_last = now(vec)
                                for tt in range(4):
                                    for cc in range(2):
                                        db = (tt * 2 + cc) % 2
                                        dw_ = war.pop(("D", db), [])
                                        for ntile in range(nt):
                                            P.emit("tensor", lambda t, tt=tt, cc=cc, ntile=ntile,
                                                   nt=nt, wd=wd, hb=hb, db=db:
                                                   t.matmul(DD[db][:, :],
                                                            hh[hb][:, ntile * CHUNK + tt * 128:
                                                                   ntile * CHUNK + (tt + 1) * 128],
                                                            wd[:, ntile * D + cc * 512:
                                                               ntile * D + (cc + 1) * 512],
                                                            start=(ntile == 0), stop=(ntile == nt - 1),
                                                            skip_group_check=True),
                                                   waits=([sq_last] + dw_) if ntile == 0 else [],
                                                   incs=[(pe, 1)] if ntile == nt - 1 else [])
                                        d_tok = now(pe)
                                        dslc = (lambda ch=ch, tt=tt, cc=cc:
                                                dacc[:, (ch * 4 + tt) * D + cc * 512:
                                                     (ch * 4 + tt) * D + (cc + 1) * 512])
                                        dchain = war.pop(("dch", ch, tt, cc), [])
                                        if e == 0 and piece == 0:
                                            P.emit("vector", lambda v, dslc=dslc, db=db:
                                                   v.tensor_copy(dslc(), DD[db][:, :]),
                                                   waits=[d_tok] + ([ycast_all]
                                                                    if (ch == 0 and tt == 0 and cc == 0)
                                                                    else []),
                                                   incs=[(vec, 1)])
                                        else:
                                            P.emit("vector", lambda v, dslc=dslc, db=db:
                                                   v.tensor_tensor(dslc(), dslc(), DD[db][:, :],
                                                                   ALU.add),
                                                   waits=[d_tok] + dchain, incs=[(vec, 1)])
                                        war[("D", db)] = [now(vec)]
                                        war[("dch", ch, tt, cc)] = [now(vec)]
                                if piece == 0:
                                    lastAuse = [now(pe)]
                                    if ch < HCHUNKS - 1 and routed:
                                        emit_mbye(e, ch + 1)
                                    elif ch == HCHUNKS - 1 and routed and e + 1 < 8:
                                        emit_mbye(e + 1, 0)
                                else:
                                    lastBuse = [now(pe)]
                                war[("hhh", hb)] = [now(pe)]
                            if routed:
                                war[("ye", yp)] = [now(pe)]
                        war[("WA", e % 2)] = lastAuse
                        war["WB"] = lastBuse
                        if e + 2 < NE:
                            ptokA[e + 2] = load_piece(e + 2, 0, war.pop(("WA", e % 2)))
                        if e + 1 < NE:
                            ptokB = load_piece(e + 1, 1, war.pop("WB"))
                    dacc_done = now(vec)

                    if dbg and half == 0 and tag == "f1":
                        P.emit("gpsimd", lambda g: g.dma_start(out=dbg_d["yTh"][:, :], in_=yTh[:, :]),
                               waits=[dacc_done], incs=[(dDbg, 16)])
                        P.emit("gpsimd", lambda g: g.dma_start(out=dbg_d["mTc"][:, :], in_=mTcF[:, :]),
                               waits=[], incs=[(dDbg, 16)])
                        P.emit("gpsimd", lambda g: g.dma_start(out=dbg_d["dacc"][:, :], in_=dacc[:, :]),
                               waits=[], incs=[(dDbg, 16)])
                        P.emit("gpsimd", lambda g: g.dma_start(out=dbg_d["ye"][:, :], in_=ye[1][:, :]),
                               waits=[], incs=[(dDbg, 16)])

                    for i in range(HTILES):
                        r = i % 2
                        P.emit("sync", lambda sy, i=i, r=r, t0=t0: sy.dma_start(
                            out=xr[r][:, :], in_=src_dram[(t0 + i) * 128:(t0 + i + 1) * 128, :]),
                               waits=[dacc_done] + war.pop(("xr", r), []), incs=[(dXs[r], 16)])
                        ld = now(dXs[r])
                        tap_guards = []
                        if dbg and half == 0 and tag == "f1" and i == 0:
                            P.emit("gpsimd", lambda g: g.dma_start(out=dbg_d["xs0"][0:128, :],
                                                                   in_=xr[r][:, :]),
                                   waits=[ld], incs=[(dDbg, 16)])
                            tap_guards.append(now(dDbg))
                        P.emit("vector", lambda v, i=i, r=r: v.scalar_tensor_tensor(
                            ybuf[r][:, :], dacc[:, i * D:(i + 1) * D], 0.5, xr[r][:, :],
                            ALU.mult, ALU.add),
                               waits=[ld] + war.pop(("ybuf", r), []), incs=[(vec, 1)])
                        stt_tok = now(vec)
                        yb_guards = []
                        if dbg and half == 0 and tag == "f1" and i == 0:
                            P.emit("gpsimd", lambda g: g.dma_start(out=dbg_d["xs0"][128:256, :],
                                                                   in_=ybuf[0][:, :]),
                                   waits=[stt_tok], incs=[(dDbg, 16)])
                            yb_guards.append(now(dDbg))
                        war[("xr", r)] = [stt_tok] + tap_guards
                        P.emit("sync", lambda sy, i=i, r=r, t0=t0: sy.dma_start(
                            out=dst_dram[(t0 + i) * 128:(t0 + i + 1) * 128, :], in_=ybuf[r][:, :]),
                               waits=[stt_tok], incs=[(dYs[r], 16)])
                        war[("ybuf", r)] = [now(dYs[r])] + yb_guards
                    if dbg and half == 0 and tag == "f1":
                        P.emit("gpsimd", lambda g: g.dma_start(out=dbg_d["xs0"][:, :],
                                                               in_=dst_dram[0:HALF, :]),
                               waits=[now(dYs[0]), now(dYs[1])], incs=[(dDbg, 16)])
                        P.emit("gpsimd", lambda g: g.dma_start(out=dbg_d["dacc2"][:, :], in_=dacc[:, :]),
                               waits=[now(vec)], incs=[(dDbg, 16)])
                    war["dacc"] = [now(vec)]
                    war["yTh"] = [now(pe), now(vec)]
                    war["mTc"] = [now(dM[0]), now(dM[1])]
            engine_barrier()

        def attn_phase(src_dram, dst_dram):
            with ExitStack() as ph:
                psb = lambda n_, sh, dt: ph.enter_context(nc.sbuf_tensor(n_ + "_at", sh, dt))
                iota512 = psb("iota512", [128, 512], F32)
                wqkv_sb = psb("wqkv", [128, KC * 192], BF16)
                wout_sb = psb("wout", [64, D], BF16)
                zT = [psb(f"zT{i}", [128, KC * CHUNK], BF16) for i in range(2)]
                qkT = psb("qkT", [128, NB * SEQ], BF16)
                v_sb = psb("v_sb", [128, NTILES * 64], BF16)
                vI = psb("vI", [128, 16 * 128], BF16)
                qc_r = psb("qc_r", [1, 2 * SEQ], BF16)
                k_sb = psb("k_sb", [128, NTILES * 64], F32)
                Esb = [psb(f"Esb{i}", [128, 512], BF16) for i in range(4)]
                masks = psb("masks", [128, 4 * 512], BF16)
                ndbuf = psb("ndbuf", [128, SEQ], F32)
                rden = psb("rden", [64, SEQ], F32)
                oT = psb("oT", [64, SEQ], BF16)
                stg = [psb(f"stg{i}", [2, 512], F32) for i in range(4)]

                TPz = [PB[5][:, 0:128], PB[6][:, 0:128]]
                QK = PB[7]
                Vp = PB[4][:, 0:64]
                BQ = [PB[0], PB[1]]
                Kp = PB[4][:, 64:128]
                ND = [PB[5][0:2, :], PB[6][0:2, :]]
                OP = [PB[4], PB[7]]

                war = {}
                P.emit("gpsimd", lambda g: g.dma_start(
                    out=wqkv_sb[:, :].rearrange("p (k f) -> p k f", k=KC),
                    in_=wqkvT_d[:, :].rearrange("(k p) f -> p k f", p=128)), incs=[(dInit, 16)])
                P.emit("gpsimd", lambda g: g.dma_start(out=wout_sb[:, :], in_=woutT_d[:, :]),
                       incs=[(dInit, 16)])
                wqkv_tok = now(dInit)
                wout_tok = now(dInit)
                P.emit("gpsimd", lambda g: g.iota(iota512[:, :], pattern=[[1, 512]], base=0,
                                                  channel_multiplier=-1,
                                                  allow_small_or_imprecise_dtypes=True),
                       incs=[(gp, 1)])
                io5_tok = now(gp)
                for m in range(4):
                    P.emit("vector", lambda v, m=m: v.tensor_single_scalar(
                        masks[:, m * 512:(m + 1) * 512], iota512[:, :], float(128.0 * m), ALU.is_ge),
                           waits=[io5_tok], incs=[(vec, 1)])
                mask_tok = now(vec)

                for ch8 in range(8):
                    zr = ch8 % 2
                    last_evac = None
                    for j in range(4):
                        i = ch8 * 4 + j
                        r = i % 2
                        toks = emit_ln_tile(war, src_dram, i, r)
                        for k in range(KC):
                            tp = (i * KC + k) % 2
                            P.emit("tensor", lambda t, k=k, r=r, tp=tp:
                                   t.transpose(TPz[tp], ybuf[r][:, k * 128:(k + 1) * 128], ident[:, :]),
                                   waits=[toks["y"], ident_tok] + war.pop(("tpz", tp), []),
                                   incs=[(pe, 1)])
                            tp_tok = now(pe)
                            P.emit("scalar", lambda s, k=k, j=j, zr=zr, tp=tp:
                                   s.copy(zT[zr][:, k * CHUNK + j * 128: k * CHUNK + (j + 1) * 128],
                                          TPz[tp]),
                                   waits=[tp_tok] + (war.pop(("zslot", zr), [])
                                                     if (k == 0 and j == 0) else []),
                                   incs=[(act, 1)])
                            war[("tpz", tp)] = [now(act)]
                            last_evac = now(act)
                        war[("ybuf", r)] = [now(pe)]
                        war[("xr", r)] = [toks["y"], now(vec)]
                        vw = war.pop("vp", [])
                        for k in range(KC):
                            P.emit("tensor", lambda t, k=k, zr=zr, j=j:
                                   t.matmul(Vp, zT[zr][:, k * CHUNK + j * 128: k * CHUNK + (j + 1) * 128],
                                            wqkv_sb[:, k * 192 + 128: k * 192 + 192],
                                            start=(k == 0), stop=(k == KC - 1), skip_group_check=True),
                                   waits=([wqkv_tok, last_evac] + vw) if k == 0 else [],
                                   incs=[(pe, 1)] if k == KC - 1 else [])
                        vp_tok = now(pe)
                        P.emit("scalar", lambda s, i=i: s.copy(v_sb[:, i * 64:(i + 1) * 64], Vp),
                               waits=[vp_tok], incs=[(act, 1)])
                        vev = now(act)
                        kw = war.pop("kp", [])
                        for k in range(KC):
                            P.emit("tensor", lambda t, k=k, zr=zr, j=j:
                                   t.matmul(Kp, zT[zr][:, k * CHUNK + j * 128: k * CHUNK + (j + 1) * 128],
                                            wqkv_sb[:, k * 192 + 64: k * 192 + 128],
                                            start=(k == 0), stop=(k == KC - 1), skip_group_check=True),
                                   waits=([vev] + kw) if k == 0 else [],
                                   incs=[(pe, 1)] if k == KC - 1 else [])
                        kp_tok = now(pe)
                        P.emit("vector", lambda v, i=i: v.tensor_copy(k_sb[:, i * 64:(i + 1) * 64], Kp),
                               waits=[kp_tok], incs=[(vec, 1)])
                        war["kp"] = [now(vec)]
                        war["vp"] = [now(vec)]
                    qkw = war.pop("qk", [])
                    for k in range(KC):
                        P.emit("tensor", lambda t, k=k, zr=zr:
                               t.matmul(QK[:, :], wqkv_sb[:, k * 192: k * 192 + 128],
                                        zT[zr][:, k * CHUNK:(k + 1) * CHUNK],
                                        start=(k == 0), stop=(k == KC - 1), skip_group_check=True),
                               waits=([wqkv_tok, last_evac] + qkw) if k == 0 else [],
                               incs=[(pe, 1)] if k == KC - 1 else [])
                    qk_tok = now(pe)
                    b = ch8 // 4
                    pos = (ch8 % 4) * CHUNK
                    P.emit("scalar", lambda s, b=b, pos=pos: s.activation(
                        qkT[0:64, b * SEQ + pos: b * SEQ + pos + CHUNK], QK[0:64, :], AF.Identity,
                        scale=f32(0.125)),
                           waits=[qk_tok], incs=[(act, 1)])
                    war["qk"] = [now(act)]
                    war[("zslot", zr)] = [now(pe)]
                qkT_tok = now(act)
                vsb_tok = now(act)
                ksb_tok = now(vec)

                for b in range(NB):
                    P.emit("gpsimd", lambda g: g.memset(vI[:, :], 1.0),
                           waits=[vsb_tok] + war.pop("vI", []), incs=[(gp, 1)])
                    vi0 = now(gp)
                    for jb in range(16):
                        P.emit("vector", lambda v, b=b, jb=jb: v.tensor_copy(
                            vI[:, jb * 128:(jb + 1) * 128].rearrange("p (h two) -> p h two", two=2)[:, :, 0:1],
                            v_sb[:, (b * 16 + jb) * 64:(b * 16 + jb + 1) * 64].rearrange("p (h o) -> p h o", o=1)),
                               waits=[vi0] if jb == 0 else [], incs=[(vec, 1)])
                    vI_tok = now(vec)

                    def load_rows(h, slot, waits_, b=b):
                        P.emit("scalar", lambda s, h=h, b=b, slot=slot: s.dma_start(
                            out=qc_r[0:1, slot * SEQ:(slot + 1) * SEQ],
                            in_=qkT[h:h + 1, b * SEQ:(b + 1) * SEQ]),
                               waits=waits_ + [qkT_tok], incs=[(dR[slot], 16)])
                        return now(dR[slot])

                    rowtok = {0: load_rows(0, 0, war.pop(("rows", 0), []))}
                    rowtok[1] = load_rows(1, 1, war.pop(("rows", 1), []))
                    pending = None

                    def flush_pending(p):
                        jb, e_tok, ndb_, njb_, h_, xw_, ic_ev = p
                        es = jb % 4
                        P.emit("tensor", lambda t, h_=h_, jb=jb, es=es, ndb_=ndb_, njb_=njb_:
                               t.matmul(ND[ndb_],
                                        vI[:, jb * 128 + 2 * h_: jb * 128 + 2 * h_ + 2],
                                        Esb[es][:, :], start=(jb == 0), stop=(jb == njb_ - 1),
                                        skip_group_check=True),
                               waits=[vI_tok, e_tok] + xw_, incs=[(pe, 1)])
                        war[("E", es)] = [now(pe)]
                        if ic_ev is not None:
                            nd_tok = now(pe)
                            ic = ic_ev
                            sg = (h_ * 4 + ic) % 4
                            P.emit("vector", lambda v, ndb_=ndb_, sg=sg: v.tensor_copy(
                                stg[sg][:, :], ND[ndb_]),
                                   waits=[nd_tok] + war.pop(("stg", sg), []), incs=[(vec, 1)])
                            st_tok = now(vec)
                            war[("nd", ndb_)] = [st_tok]
                            P.emit("scalar", lambda s, h_=h_, ic=ic, sg=sg: s.dma_start(
                                out=ndbuf[h_:h_ + 1, ic * 512:(ic + 1) * 512], in_=stg[sg][0:1, :]),
                                   waits=[st_tok] + war.pop("hscat", []) +
                                         (war.pop("ndbuf", []) if (h_ == 0 and ic == 0) else []),
                                   incs=[(dG[sg], 16)])
                            P.emit("scalar", lambda s, h_=h_, ic=ic, sg=sg: s.dma_start(
                                out=ndbuf[64 + h_:65 + h_, ic * 512:(ic + 1) * 512],
                                in_=stg[sg][1:2, :]),
                                   waits=[], incs=[(dG[sg], 16)])
                            war[("stg", sg)] = [now(dG[sg])]
                            war["hscat"] = [now(dG[sg])]

                    for h in range(64):
                        slot = h % 2
                        rtok = rowtok.pop(h)
                        for ic in range(4):
                            ndb = ic % 2
                            ndw = war.pop(("nd", ndb), [])
                            njb = 4 * (ic + 1)
                            bqb = ic % 2
                            P.emit("tensor", lambda t, ic=ic, slot=slot, bqb=bqb:
                                   t.matmul(BQ[bqb][:, :], ones1[0:1, :],
                                            qc_r[0:1, slot * SEQ + ic * 512: slot * SEQ + (ic + 1) * 512],
                                            start=True, stop=True, skip_group_check=True),
                                   waits=[rtok, init_gp] + war.pop(("BQ", bqb), []), incs=[(pe, 1)])
                            bq_tok = now(pe)
                            for jb in range(njb):
                                es = jb % 4
                                ew = war.pop(("E", es), [])
                                ti_g = b * 16 + jb
                                P.emit("scalar", lambda s, es=es, bqb=bqb, ti_g=ti_g, h=h:
                                       s.activation(Esb[es][:, :], BQ[bqb][:, :], AF.Exp,
                                                    bias=f32(-2.0),
                                                    scale=k_sb[:, ti_g * 64 + h: ti_g * 64 + h + 1]),
                                       waits=[bq_tok, ksb_tok] + ew, incs=[(act, 1)])
                                e_tok = now(act)
                                if jb >= 4 * ic:
                                    m = jb - 4 * ic
                                    P.emit("vector", lambda v, es=es, m=m: v.tensor_tensor(
                                        Esb[es][:, :], Esb[es][:, :],
                                        masks[:, m * 512:(m + 1) * 512], ALU.mult),
                                           waits=[e_tok, mask_tok], incs=[(vec, 1)])
                                    e_tok = now(vec)
                                if pending is not None:
                                    flush_pending(pending)
                                pending = (jb, e_tok, ndb, njb, h,
                                           ndw if jb == 0 else [],
                                           ic if jb == njb - 1 else None)
                            war[("BQ", bqb)] = [now(act)]
                        if h + 2 < 64:
                            rowtok[h + 2] = load_rows(h + 2, slot, [now(pe)])
                    flush_pending(pending)
                    pending = None
                    scat_toks = [now(q) for q in dG]
                    P.emit("vector", lambda v: v.reciprocal(rden[:, :], ndbuf[64:128, :]),
                           waits=scat_toks + war.pop("rden", []), incs=[(vec, 1)])
                    P.emit("vector", lambda v: v.tensor_mul(oT[:, :], ndbuf[0:64, :], rden[:, :]),
                           waits=[now(vec)] + war.pop("oT", []), incs=[(vec, 1)])
                    oT_tok = now(vec)
                    war["ndbuf"] = [oT_tok]
                    war["rden"] = [oT_tok]
                    for j in range(16):
                        i = b * 16 + j
                        r = i % 2
                        optoks = []
                        for cc in range(2):
                            opw = war.pop(("OP", cc), [])
                            P.emit("tensor", lambda t, j=j, cc=cc:
                                   t.matmul(OP[cc][:, :], oT[:, j * 128:(j + 1) * 128],
                                            wout_sb[:, cc * 512:(cc + 1) * 512],
                                            start=True, stop=True, skip_group_check=True),
                                   waits=[oT_tok, wout_tok] + opw, incs=[(pe, 1)])
                            optoks.append(now(pe))
                        P.emit("sync", lambda sy, i=i, r=r: sy.dma_start(
                            out=xr[r][:, :], in_=src_dram[i * 128:(i + 1) * 128, :]),
                               waits=war.pop(("xr", r), []), incs=[(dXs[r], 16)])
                        ld = now(dXs[r])
                        ybw = war.pop(("ybuf", r), [])
                        for cc in range(2):
                            P.emit("vector", lambda v, r=r, cc=cc: v.tensor_tensor(
                                ybuf[r][:, cc * 512:(cc + 1) * 512], xr[r][:, cc * 512:(cc + 1) * 512],
                                OP[cc][:, :], ALU.add),
                                   waits=[ld, optoks[cc]] + (ybw if cc == 0 else []),
                                   incs=[(vec, 1)])
                            war[("OP", cc)] = [now(vec)]
                        stt_tok = now(vec)
                        war[("xr", r)] = [stt_tok]
                        P.emit("sync", lambda sy, i=i, r=r: sy.dma_start(
                            out=dst_dram[i * 128:(i + 1) * 128, :], in_=ybuf[r][:, :]),
                               waits=[stt_tok], incs=[(dYs[r], 16)])
                        war[("ybuf", r)] = [now(dYs[r])]
                    war["oT"] = [now(pe)]
                    war["vI"] = [now(pe)]
                    war[("rows", 0)] = [now(pe)]
                    war[("rows", 1)] = [now(pe)]
            engine_barrier()

        if "F1" in phases:
            ffn_phase(x_d, xs_d, "f1")
            a_src = xs_d
        else:
            a_src = x_d
        if "A" in phases:
            attn_phase(a_src, xs_d)
            f2_src = xs_d
        else:
            f2_src = a_src
        if "F2" in phases:
            ffn_phase(f2_src, out_d, "f2")
        else:
            prev = None
            for i in range(NTILES):
                P.emit("sync", lambda sy, i=i: sy.dma_start(out=xr[0][:, :],
                                                            in_=f2_src[i * 128:(i + 1) * 128, :]),
                       waits=[prev] if prev else [], incs=[(dXs[0], 16)])
                ld = now(dXs[0])
                P.emit("sync", lambda sy, i=i: sy.dma_start(out=out_d[i * 128:(i + 1) * 128, :],
                                                            in_=xr[0][:, :]),
                       waits=[ld], incs=[(dYs[0], 16)])
                prev = now(dYs[0])

        P.emit("sync", lambda sy: sy.nop(), waits=[now(dYs[0]), now(dYs[1])])

        with ExitStack() as sctx:
            for s in all_sems:
                s.h = sctx.enter_context(nc.semaphore(s.name))
            block = sctx.enter_context(nc.Block())
            block.tensor(lambda t: P.run("tensor", t))
            block.vector(lambda v: P.run("vector", v))
            block.scalar(lambda s: P.run("scalar", s))
            block.gpsimd(lambda g: P.run("gpsimd", g))
            block.sync(lambda sy: P.run("sync", sy))
    return nc


def kernel(L0_attn_x, L0_attn_w_qkv, L0_attn_w_out, L0_ffn_gate_w, L0_ffn_expert_bias,
           L0_ffn_w_up, L0_ffn_w_down, L0_ffn_shared_w_up, L0_ffn_shared_w_down):
    x = np.ascontiguousarray(np.asarray(L0_attn_x, dtype=np.float32))
    wqkvT = np.ascontiguousarray(np.asarray(L0_attn_w_qkv, np.float32).T)
    woutT = np.ascontiguousarray(np.asarray(L0_attn_w_out, np.float32).T)
    gwT = np.ascontiguousarray(np.asarray(L0_ffn_gate_w, np.float32).T)
    biasbc = np.ascontiguousarray(
        np.tile(np.asarray(L0_ffn_expert_bias, np.float32)[None, :], (128, 1)))
    up = np.asarray(L0_ffn_w_up, np.float32)
    dn = np.asarray(L0_ffn_w_down, np.float32)
    sup = np.asarray(L0_ffn_shared_w_up, np.float32)
    sdn = np.asarray(L0_ffn_shared_w_down, np.float32)
    upT = np.ascontiguousarray(np.concatenate([up.transpose(0, 2, 1), sup.T[None]], axis=0))
    dnT = np.ascontiguousarray(np.concatenate([dn.transpose(0, 2, 1), sdn.T[None]], axis=0))

    nc = build_nc()
    in_maps = []
    for c in range(NCORES):
        xs = np.ascontiguousarray(x[2 * c:2 * c + 2].reshape(T, D))
        in_maps.append(dict(x=xs, wqkvT=wqkvT, woutT=woutT, gwT=gwT, biasbc=biasbc,
                            upT=upT, dnT=dnT))
    res = run_bass_kernel_spmd(nc, in_maps, core_ids=list(range(NCORES)))
    out = np.concatenate([np.asarray(res.results[c]["out"]).reshape(2, SEQ, D)
                          for c in range(NCORES)], axis=0)
    return out.astype(np.float32)


# version 08
# speedup vs baseline: 1.6773x; 1.0004x over previous
import numpy as np
from contextlib import ExitStack

import concourse.bass as bass
import concourse.mybir as mybir
from concourse.bass_utils import run_bass_kernel_spmd

F32 = mybir.dt.float32
BF16 = mybir.dt.bfloat16
AF = mybir.ActivationFunctionType
ALU = mybir.AluOpType
AX = mybir.AxisListType
f32 = np.float32

NCORES = 8
D = 1024
KC = 8
HID = 1408
NH_A, NT_A = 640, 5
NH_B, NT_B = 768, 6
NE = 9
SEQ = 2048
NB = 2
T = NB * SEQ
NTILES = T // 128
HALF = 2048
HTILES = HALF // 128
CHUNK = 512
HCHUNKS = HALF // CHUNK


class Sem:
    __slots__ = ("name", "v", "h")

    def __init__(self, name):
        self.name, self.v, self.h = name, 0, None


class Plan:
    def __init__(self):
        self.ops = []

    def emit(self, engine, fn, waits=(), incs=()):
        ws = [(s, v) for (s, v) in waits if v > 0]
        ri = []
        for (s, k) in incs:
            s.v += k
            ri.append((s, k))
        self.ops.append((engine, ws, fn, ri))

    def run(self, engine_name, eng):
        for e, ws, fn, ri in self.ops:
            if e != engine_name:
                continue
            merged = {}
            for (s, v) in ws:
                if s.name in merged:
                    merged[s.name] = (s, max(v, merged[s.name][1]))
                else:
                    merged[s.name] = (s, v)
            for (s, v) in merged.values():
                eng.wait_ge(s.h, v)
            inst = fn(eng)
            for (s, k) in ri:
                inst.then_inc(s.h, k)


def build_nc(phases=("F1", "A", "F2"), dbg=False):
    nc = bass.Bass("TRN2", target_bir_lowering=False, debug=False)
    dbg_d = {}
    if dbg:
        dbg_d["yTh"] = nc.dram_tensor("dbg_yTh", [128, KC * HALF], F32, kind="ExternalOutput").ap()
        dbg_d["mTc"] = nc.dram_tensor("dbg_mTc", [128, 128], F32, kind="ExternalOutput").ap()
        dbg_d["dacc"] = nc.dram_tensor("dbg_dacc", [128, HTILES * D], F32, kind="ExternalOutput").ap()
        dbg_d["ye"] = nc.dram_tensor("dbg_ye", [128, KC * CHUNK], F32, kind="ExternalOutput").ap()
        dbg_d["xs0"] = nc.dram_tensor("dbg_xs0", [HALF, D], F32, kind="ExternalOutput").ap()
        dbg_d["dacc2"] = nc.dram_tensor("dbg_dacc2", [128, HTILES * D], F32, kind="ExternalOutput").ap()

    x_d = nc.dram_tensor("x", [T, D], F32, kind="ExternalInput").ap()
    wqkvT_d = nc.dram_tensor("wqkvT", [D, 192], F32, kind="ExternalInput").ap()
    woutT_d = nc.dram_tensor("woutT", [64, D], F32, kind="ExternalInput").ap()
    gwT_d = nc.dram_tensor("gwT", [D, 8], F32, kind="ExternalInput").ap()
    biasbc_d = nc.dram_tensor("biasbc", [128, 8], F32, kind="ExternalInput").ap()
    upT_d = nc.dram_tensor("upT", [NE, D, HID], F32, kind="ExternalInput").ap()
    dnT_d = nc.dram_tensor("dnT", [NE, HID, D], F32, kind="ExternalInput").ap()
    out_d = nc.dram_tensor("out", [T, D], F32, kind="ExternalOutput").ap()
    xs_d = nc.dram_tensor("xs", [T, D], F32, kind="Internal").ap()

    P = Plan()
    dInit = Sem("dInit")
    dWA = [Sem("dWA0"), Sem("dWA1")]
    dWB = Sem("dWB")
    dXs = [Sem("dX0"), Sem("dX1")]
    dYs = [Sem("dY0"), Sem("dY1")]
    dM = [Sem("dM0"), Sem("dM1")]
    dR = [Sem("dR0"), Sem("dR1")]
    dG = [Sem(f"dG{i}") for i in range(4)]
    dDbg = Sem("dDbg")
    pe, act, vec, gp, bar = Sem("pe"), Sem("act"), Sem("vec"), Sem("gp"), Sem("bar")
    all_sems = [dInit, dWA[0], dWA[1], dWB, dXs[0], dXs[1], dYs[0], dYs[1],
                dM[0], dM[1], dR[0], dR[1]] + dG + [dDbg, pe, act, vec, gp, bar]

    def now(s):
        return (s, s.v)

    with ExitStack() as top:
        sb = lambda n_, sh, dt: top.enter_context(nc.sbuf_tensor(n_, sh, dt))
        ident = sb("ident", [128, 128], F32)
        ones1 = sb("ones1", [1, 128], BF16)
        iota8 = sb("iota8", [128, 8], F32)
        biasb = sb("biasb", [128, 8], F32)
        gw_sb = sb("gw_sb", [128, KC * 8], F32)
        xr = [sb(f"xr{i}", [128, D], F32) for i in range(2)]
        ybuf = [sb(f"ybuf{i}", [128, D], F32) for i in range(2)]
        stt_ = [sb(f"stt{i}", [128, 16], F32) for i in range(2)]
        PBbig = top.enter_context(nc.psum_tensor("PBbig", [128, 2048], F32))
        PB = [PBbig[:, 512 * i:512 * (i + 1)] for i in range(4)] + \
             [top.enter_context(nc.psum_tensor(f"PB{i}", [128, 512], F32)) for i in range(4, 8)]

        P.emit("gpsimd", lambda g: g.iota(ident[:, :], pattern=[[1, 128]], base=0,
                                          channel_multiplier=-1, allow_small_or_imprecise_dtypes=True),
               incs=[(gp, 1)])
        P.emit("gpsimd", lambda g: g.iota(iota8[:, :], pattern=[[1, 8]], base=0,
                                          channel_multiplier=0, allow_small_or_imprecise_dtypes=True),
               incs=[(gp, 1)])
        P.emit("gpsimd", lambda g: g.memset(ones1[:, :], 1.0), incs=[(gp, 1)])
        P.emit("gpsimd", lambda g: g.dma_start(out=biasb[:, :], in_=biasbc_d[:, :]), incs=[(dInit, 16)])
        P.emit("gpsimd", lambda g: g.dma_start(
            out=gw_sb[:, :].rearrange("p (k e) -> p k e", k=KC),
            in_=gwT_d[:, :].rearrange("(k p) e -> p k e", p=128)), incs=[(dInit, 16)])
        init_gp = now(gp)
        init_dW = now(dInit)
        P.emit("vector", lambda v: v.tensor_single_scalar(ident[:, :], ident[:, :], 0.0, ALU.is_equal),
               waits=[init_gp], incs=[(vec, 1)])
        ident_tok = now(vec)

        def engine_barrier():
            target = bar.v + 5
            dma_fence = {"sync": [now(q) for q in dXs + dYs],
                         "gpsimd": [now(dInit), now(dWA[0]), now(dWA[1]), now(dWB), now(dDbg)],
                         "scalar": [now(q) for q in dM + dR + dG]}
            for e in ("tensor", "vector", "scalar", "gpsimd", "sync"):
                P.emit(e, lambda eng: eng.nop(), waits=dma_fence.get(e, []), incs=[(bar, 1)])
            for e in ("tensor", "vector", "scalar", "gpsimd", "sync"):
                P.emit(e, lambda eng: eng.nop(), waits=[(bar, target)])

        def emit_ln_tile(war, src_dram, ti, r):
            st = stt_[r]
            mu, var = st[:, 12:13], st[:, 13:14]
            rstd, nm = st[:, 14:15], st[:, 15:16]
            P.emit("sync", lambda sy: sy.dma_start(out=xr[r][:, :],
                                                   in_=src_dram[ti * 128:(ti + 1) * 128, :]),
                   waits=war.pop(("xr", r), []), incs=[(dXs[r], 16)])
            ld = now(dXs[r])
            stw = war.pop(("st", r), [])
            ybw = war.pop(("ybuf", r), [])
            P.emit("vector", lambda v: v.bn_stats(st[:, 0:6], xr[r][:, 0:512]),
                   waits=[ld] + stw, incs=[(vec, 1)])
            P.emit("vector", lambda v: v.bn_stats(st[:, 6:12], xr[r][:, 512:1024]),
                   waits=[now(vec)], incs=[(vec, 1)])
            P.emit("vector", lambda v: v.bn_aggr(st[:, 12:14], st[:, 0:12]),
                   waits=[now(vec)], incs=[(vec, 1)])
            agg_tok = now(vec)
            P.emit("scalar", lambda s: s.activation(rstd, var, AF.Sqrt, bias=f32(1e-5)),
                   waits=[agg_tok], incs=[(act, 1)])
            sqr_tok = now(act)
            P.emit("vector", lambda v: v.reciprocal(rstd, rstd), waits=[sqr_tok], incs=[(vec, 1)])
            rstd_tok = now(vec)
            P.emit("vector", lambda v: v.scalar_tensor_tensor(nm, mu, -1.0, rstd, ALU.mult, ALU.mult),
                   waits=[rstd_tok], incs=[(vec, 1)])
            nm_tok = now(vec)
            P.emit("scalar", lambda s: s.activation(ybuf[r][:, :], xr[r][:, :], AF.Identity,
                                                    bias=nm, scale=rstd),
                   waits=[rstd_tok, nm_tok] + ybw, incs=[(act, 1)])
            y_tok = now(act)
            war[("st", r)] = [y_tok]
            return {"ld": ld, "y": y_tok}

        def ffn_phase(src_dram, dst_dram, tag):
            with ExitStack() as ph:
                psb = lambda n_, sh, dt: ph.enter_context(nc.sbuf_tensor(n_ + tag, sh, dt))
                yTh = psb("yTh", [128, KC * HALF], BF16)
                mmF = psb("mmF", [128, 128], F32)
                mTcF = psb("mTcF", [128, 128], BF16)
                m1r = [psb(f"m1r{i}", [1, CHUNK], BF16) for i in range(2)]
                sc_t = psb("sc_t", [128, 8], F32)
                bb_t = psb("bb_t", [128, 8], F32)
                t1_t = psb("t1_t", [128, 8], F32)
                oh_t = psb("oh_t", [128, 8], F32)
                v1_t = psb("v1_t", [128, 1], F32)
                i1_t = psb("i1_t", [128, 1], F32)
                WupA = [psb(f"WupA{i}", [128, KC * NH_A], BF16) for i in range(2)]
                WdnA = [psb(f"WdnA{i}", [128, NT_A * D], BF16) for i in range(2)]
                WupB = psb("WupB", [128, KC * NH_B], BF16)
                WdnB = psb("WdnB", [128, NT_B * D], BF16)
                ye = [psb(f"ye{i}", [128, KC * CHUNK], BF16) for i in range(2)]
                hh = [psb("hhA", [128, NT_A * CHUNK], BF16), psb("hhB", [128, NT_B * CHUNK], BF16)]
                dacc = psb("dacc", [128, HTILES * D], F32)
                yTf32 = dacc

                U = [PB[0], PB[1]]
                DD = [PB[2], PB[3]]
                MB = PB[4]
                TPy = [PB[5][:, 0:128], PB[6][:, 0:128]]
                G8 = PB[7][:, 0:8]
                TPm = PB[4][0:128, 0:128]

                war = {}

                def load_piece(e, piece, waits_):
                    if piece == 0:
                        n0, nt, wu, wd = 0, NT_A, WupA[e % 2], WdnA[e % 2]
                        qsem = dWA[e % 2]
                    else:
                        n0, nt, wu, wd = NH_A, NT_B, WupB, WdnB
                        qsem = dWB
                    P.emit("gpsimd", lambda g, e=e, n0=n0, nt=nt, wu=wu: g.dma_start(
                        out=wu[:, 0:KC * nt * 128].rearrange("p (k n) -> p k n", k=KC),
                        in_=upT_d[e:e + 1, :, n0:n0 + nt * 128].rearrange("o (k p) n -> p (o k) n", p=128)),
                        waits=waits_, incs=[(qsem, 16)])
                    P.emit("gpsimd", lambda g, e=e, n0=n0, nt=nt, wd=wd: g.dma_start(
                        out=wd[:, 0:nt * D].rearrange("p (k c) -> p k c", k=nt),
                        in_=dnT_d[e:e + 1, n0:n0 + nt * 128, :].rearrange("o (k p) c -> p (o k) c", p=128)),
                        waits=[], incs=[(qsem, 16)])
                    return now(qsem)

                mTh_tok = [None]

                def load_m1(e, ch, slot, waits_):
                    P.emit("sync", lambda s, e=e, ch=ch, slot=slot: s.dma_start(
                        out=m1r[slot][0:1, :],
                        in_=mTcF[e * 16 + 4 * ch: e * 16 + 4 * ch + 4, 0:128]),
                           waits=waits_ + [mTh_tok[0]], incs=[(dM[slot], 16)])
                    return now(dM[slot])

                for half in range(2):
                    t0 = half * HTILES
                    for i in range(HTILES):
                        r = i % 2
                        ci, jj = i // 4, i % 4
                        toks = emit_ln_tile(war, src_dram, t0 + i, r)
                        evac_toks = []
                        ytfw = war.pop("yTf32", []) + (war.pop("dacc", []) if i == 0 else [])
                        for k in range(KC):
                            tp = (i * KC + k) % 2
                            P.emit("tensor", lambda t, k=k, r=r, tp=tp:
                                   t.transpose(TPy[tp], ybuf[r][:, k * 128:(k + 1) * 128], ident[:, :]),
                                   waits=[toks["y"], ident_tok] + war.pop(("tpy", tp), []),
                                   incs=[(pe, 1)])
                            tp_tok = now(pe)
                            if k % 2 == 0:
                                P.emit("scalar", lambda s, k=k, tp=tp:
                                       s.copy(yTf32[:, k * 128:(k + 1) * 128], TPy[tp]),
                                       waits=[tp_tok] + (ytfw if k == 0 else []),
                                       incs=[(act, 1)])
                                ev = now(act)
                            else:
                                P.emit("vector", lambda v, k=k, tp=tp:
                                       v.tensor_copy(yTf32[:, k * 128:(k + 1) * 128], TPy[tp]),
                                       waits=[tp_tok] + (ytfw if k == 1 else []),
                                       incs=[(vec, 1)])
                                ev = now(vec)
                            evac_toks.append(ev)
                            war[("tpy", tp)] = [ev]
                            P.emit("gpsimd", lambda g, k=k, i=i:
                                   g.tensor_copy(yTh[:, k * HALF + i * 128: k * HALF + (i + 1) * 128],
                                                 yTf32[:, k * 128:(k + 1) * 128]),
                                   waits=[ev] + (war.pop("yTh", []) if (i == 0 and k == 0) else []),
                                   incs=[(gp, 1)])
                        war[("ybuf", r)] = [now(pe)]
                        war[("xr", r)] = [toks["y"], now(vec)]
                        g8w = war.pop("g8", [])
                        for k in range(KC):
                            P.emit("tensor", lambda t, k=k:
                                   t.matmul(G8, yTf32[:, k * 128:(k + 1) * 128],
                                            gw_sb[:, k * 8:(k + 1) * 8],
                                            start=(k == 0), stop=(k == KC - 1), skip_group_check=True),
                                   waits=([evac_toks[KC - 2], evac_toks[KC - 1], init_dW] + g8w)
                                         if k == 0 else [],
                                   incs=[(pe, 1)] if k == KC - 1 else [])
                        gate_tok = now(pe)
                        war["yTf32"] = [gate_tok, now(gp)]
                        P.emit("scalar", lambda s: s.activation(sc_t[:, :], G8, AF.Sigmoid),
                               waits=[gate_tok], incs=[(act, 1)])
                        sig_tok = now(act)
                        war["g8"] = [sig_tok]
                        rw = war.pop("route", [])
                        P.emit("vector", lambda v: v.tensor_add(bb_t[:, :], sc_t[:, :], biasb[:, :]),
                               waits=[sig_tok, init_dW] + rw, incs=[(vec, 1)])
                        P.emit("vector", lambda v: v.reduce_max(v1_t[:, :], bb_t[:, :], axis=AX.X),
                               waits=[now(vec)], incs=[(vec, 1)])
                        P.emit("vector", lambda v: v.tensor_scalar(t1_t[:, :], bb_t[:, :], v1_t[:, 0:1],
                                                                   -100.0, ALU.is_ge, ALU.mult),
                               waits=[now(vec)], incs=[(vec, 1)])
                        P.emit("vector", lambda v: v.tensor_tensor(t1_t[:, :], t1_t[:, :], iota8[:, :],
                                                                   ALU.add),
                               waits=[init_gp, now(vec)], incs=[(vec, 1)])
                        P.emit("vector", lambda v: v.tensor_reduce(i1_t[:, :], t1_t[:, :], axis=AX.X,
                                                                   op=ALU.min),
                               waits=[now(vec)], incs=[(vec, 1)])
                        P.emit("vector", lambda v: v.tensor_single_scalar(oh_t[:, :], t1_t[:, :],
                                                                          i1_t[:, 0:1], ALU.is_equal),
                               waits=[now(vec)], incs=[(vec, 1)])
                        P.emit("vector", lambda v: v.scalar_tensor_tensor(bb_t[:, :], oh_t[:, :], -1e30,
                                                                          bb_t[:, :], ALU.mult, ALU.add),
                               waits=[now(vec)], incs=[(vec, 1)])
                        P.emit("vector", lambda v: v.reduce_max(v1_t[:, :], bb_t[:, :], axis=AX.X),
                               waits=[now(vec)], incs=[(vec, 1)])
                        P.emit("vector", lambda v: v.tensor_scalar(t1_t[:, :], bb_t[:, :], v1_t[:, 0:1],
                                                                   -100.0, ALU.is_ge, ALU.mult),
                               waits=[now(vec)], incs=[(vec, 1)])
                        P.emit("vector", lambda v: v.tensor_tensor(t1_t[:, :], t1_t[:, :], iota8[:, :],
                                                                   ALU.add),
                               waits=[now(vec)], incs=[(vec, 1)])
                        P.emit("vector", lambda v: v.tensor_reduce(i1_t[:, :], t1_t[:, :], axis=AX.X,
                                                                   op=ALU.min),
                               waits=[now(vec)], incs=[(vec, 1)])
                        P.emit("vector", lambda v: v.tensor_single_scalar(t1_t[:, :], t1_t[:, :],
                                                                          i1_t[:, 0:1], ALU.is_equal),
                               waits=[now(vec)], incs=[(vec, 1)])
                        P.emit("vector", lambda v: v.tensor_add(oh_t[:, :], oh_t[:, :], t1_t[:, :]),
                               waits=[now(vec)], incs=[(vec, 1)])
                        P.emit("vector", lambda v, i=i: v.tensor_mul(mmF[:, i:i + 113:16],
                                                                     sc_t[:, :], oh_t[:, :]),
                               waits=[now(vec)] + (war.pop("mmF", []) if i == 0 else []),
                               incs=[(vec, 1)])
                        m_tok = now(vec)
                        war["route"] = [m_tok]
                        if i == HTILES - 1:
                            P.emit("tensor", lambda t: t.transpose(TPm, mmF[:, :], ident[:, :]),
                                   waits=[m_tok] + war.pop("tpm", []) + war.pop("mb", []),
                                   incs=[(pe, 1)])
                            tpm_tok = now(pe)
                            P.emit("scalar", lambda s: s.copy(mTcF[:, :], TPm),
                                   waits=[tpm_tok] + war.pop("mTc", []),
                                   incs=[(act, 1)])
                            mTh_tok[0] = now(act)
                            war["tpm"] = [mTh_tok[0]]
                            war["mmF"] = [tpm_tok]
                    ycast_all = now(gp)

                    ptokA = {0: load_piece(0, 0, war.pop(("WA", 0), []))}
                    ptokB = load_piece(0, 1, war.pop("WB", []))
                    ptokA[1] = load_piece(1, 0, war.pop(("WA", 1), []))
                    m1toks = {}
                    gctr = 0

                    ye_toks = {}

                    def emit_mbye(e_, ch_):
                        nonlocal gctr
                        g_ = gctr
                        gctr += 1
                        slot = g_ % 2
                        yp_ = ch_ % 2
                        m1toks[g_] = load_m1(e_, ch_, slot, war.pop(("m1", slot), []))
                        mbw = war.pop("mb", [])
                        if g_ == 0:
                            mbw = mbw + [mTh_tok[0]]
                        P.emit("tensor", lambda t, slot=slot:
                               t.matmul(MB[:, :], ones1[0:1, :], m1r[slot][0:1, :],
                                        start=True, stop=True, skip_group_check=True),
                               waits=[m1toks[g_], init_gp] + mbw, incs=[(pe, 1)])
                        mb_tok = now(pe)
                        war[("m1", slot)] = [mb_tok]
                        yew = war.pop(("ye", yp_), [])
                        for k in range(KC):
                            P.emit("vector", lambda v, k=k, ch_=ch_, yp_=yp_:
                                   v.tensor_tensor(ye[yp_][:, k * CHUNK:(k + 1) * CHUNK],
                                                   yTh[:, k * HALF + ch_ * CHUNK:
                                                       k * HALF + (ch_ + 1) * CHUNK],
                                                   MB[:, :], ALU.mult),
                                   waits=([mb_tok, ycast_all] + yew) if k == 0 else [],
                                   incs=[(vec, 1)] if k == KC - 1 else [])
                        ye_toks[(e_, ch_)] = now(vec)
                        war["mb"] = [now(vec)]

                    emit_mbye(0, 0)
                    for e in range(NE):
                        routed = e < 8
                        lastAuse = []
                        lastBuse = []
                        for ch in range(HCHUNKS):
                            yp = ch % 2
                            if routed:
                                ye_tok = ye_toks.pop((e, ch))
                            for piece in range(2):
                                if piece == 0:
                                    nt, wu, wd, wtok = NT_A, WupA[e % 2], WdnA[e % 2], ptokA[e]
                                else:
                                    nt, wu, wd, wtok = NT_B, WupB, WdnB, ptokB
                                hb = piece
                                sq_last = None
                                for ntile in range(nt):
                                    ub = ntile % 2
                                    uw = war.pop(("U", ub), [])
                                    for k in range(KC):
                                        if routed:
                                            movap = (lambda k=k, yp=yp:
                                                     ye[yp][:, k * CHUNK:(k + 1) * CHUNK])
                                            mov_tok = ye_tok
                                        else:
                                            movap = (lambda k=k, ch=ch:
                                                     yTh[:, k * HALF + ch * CHUNK:
                                                         k * HALF + (ch + 1) * CHUNK])
                                            mov_tok = ycast_all
                                        P.emit("tensor", lambda t, ntile=ntile, k=k, nt=nt, wu=wu,
                                               ub=ub, movap=movap:
                                               t.matmul(U[ub][:, :],
                                                        wu[:, (k * nt + ntile) * 128:
                                                           (k * nt + ntile + 1) * 128],
                                                        movap(), start=(k == 0), stop=(k == KC - 1),
                                                        skip_group_check=True),
                                               waits=([wtok, mov_tok] + uw) if k == 0 else [],
                                               incs=[(pe, 1)] if k == KC - 1 else [])
                                    u_tok = now(pe)
                                    P.emit("scalar", lambda s, ntile=ntile, ub=ub, hb=hb:
                                           s.activation(hh[hb][:, ntile * CHUNK:(ntile + 1) * CHUNK],
                                                        U[ub][:, :], AF.Relu),
                                           waits=[u_tok] + (war.pop(("hhh", hb), [])
                                                            if ntile == 0 else []),
                                           incs=[(act, 1)])
                                    relu_tok = now(act)
                                    war[("U", ub)] = [relu_tok]
                                    P.emit("vector", lambda v, ntile=ntile, hb=hb:
                                           v.tensor_mul(hh[hb][:, ntile * CHUNK:(ntile + 1) * CHUNK],
                                                        hh[hb][:, ntile * CHUNK:(ntile + 1) * CHUNK],
                                                        hh[hb][:, ntile * CHUNK:(ntile + 1) * CHUNK]),
                                           waits=[relu_tok], incs=[(vec, 1)])
                                    sq_last = now(vec)
                                for tt in range(4):
                                    for cc in range(2):
                                        db = (tt * 2 + cc) % 2
                                        dw_ = war.pop(("D", db), [])
                                        for ntile in range(nt):
                                            P.emit("tensor", lambda t, tt=tt, cc=cc, ntile=ntile,
                                                   nt=nt, wd=wd, hb=hb, db=db:
                                                   t.matmul(DD[db][:, :],
                                                            hh[hb][:, ntile * CHUNK + tt * 128:
                                                                   ntile * CHUNK + (tt + 1) * 128],
                                                            wd[:, ntile * D + cc * 512:
                                                               ntile * D + (cc + 1) * 512],
                                                            start=(ntile == 0), stop=(ntile == nt - 1),
                                                            skip_group_check=True),
                                                   waits=([sq_last] + dw_) if ntile == 0 else [],
                                                   incs=[(pe, 1)] if ntile == nt - 1 else [])
                                        d_tok = now(pe)
                                        dslc = (lambda ch=ch, tt=tt, cc=cc:
                                                dacc[:, (ch * 4 + tt) * D + cc * 512:
                                                     (ch * 4 + tt) * D + (cc + 1) * 512])
                                        dchain = war.pop(("dch", ch, tt, cc), [])
                                        if e == 0 and piece == 0:
                                            P.emit("vector", lambda v, dslc=dslc, db=db:
                                                   v.tensor_copy(dslc(), DD[db][:, :]),
                                                   waits=[d_tok] + ([ycast_all]
                                                                    if (ch == 0 and tt == 0 and cc == 0)
                                                                    else []),
                                                   incs=[(vec, 1)])
                                        else:
                                            P.emit("vector", lambda v, dslc=dslc, db=db:
                                                   v.tensor_tensor(dslc(), dslc(), DD[db][:, :],
                                                                   ALU.add),
                                                   waits=[d_tok] + dchain, incs=[(vec, 1)])
                                        war[("D", db)] = [now(vec)]
                                        war[("dch", ch, tt, cc)] = [now(vec)]
                                if piece == 0:
                                    lastAuse = [now(pe)]
                                    if ch < HCHUNKS - 1 and routed:
                                        emit_mbye(e, ch + 1)
                                    elif ch == HCHUNKS - 1 and routed and e + 1 < 8:
                                        emit_mbye(e + 1, 0)
                                else:
                                    lastBuse = [now(pe)]
                                war[("hhh", hb)] = [now(pe)]
                            if routed:
                                war[("ye", yp)] = [now(pe)]
                        war[("WA", e % 2)] = lastAuse
                        war["WB"] = lastBuse
                        if e + 2 < NE:
                            ptokA[e + 2] = load_piece(e + 2, 0, war.pop(("WA", e % 2)))
                        if e + 1 < NE:
                            ptokB = load_piece(e + 1, 1, war.pop("WB"))
                    dacc_done = now(vec)

                    if dbg and half == 0 and tag == "f1":
                        P.emit("gpsimd", lambda g: g.dma_start(out=dbg_d["yTh"][:, :], in_=yTh[:, :]),
                               waits=[dacc_done], incs=[(dDbg, 16)])
                        P.emit("gpsimd", lambda g: g.dma_start(out=dbg_d["mTc"][:, :], in_=mTcF[:, :]),
                               waits=[], incs=[(dDbg, 16)])
                        P.emit("gpsimd", lambda g: g.dma_start(out=dbg_d["dacc"][:, :], in_=dacc[:, :]),
                               waits=[], incs=[(dDbg, 16)])
                        P.emit("gpsimd", lambda g: g.dma_start(out=dbg_d["ye"][:, :], in_=ye[1][:, :]),
                               waits=[], incs=[(dDbg, 16)])

                    for i in range(HTILES):
                        r = i % 2
                        P.emit("sync", lambda sy, i=i, r=r, t0=t0: sy.dma_start(
                            out=xr[r][:, :], in_=src_dram[(t0 + i) * 128:(t0 + i + 1) * 128, :]),
                               waits=[dacc_done] + war.pop(("xr", r), []), incs=[(dXs[r], 16)])
                        ld = now(dXs[r])
                        tap_guards = []
                        if dbg and half == 0 and tag == "f1" and i == 0:
                            P.emit("gpsimd", lambda g: g.dma_start(out=dbg_d["xs0"][0:128, :],
                                                                   in_=xr[r][:, :]),
                                   waits=[ld], incs=[(dDbg, 16)])
                            tap_guards.append(now(dDbg))
                        P.emit("vector", lambda v, i=i, r=r: v.scalar_tensor_tensor(
                            ybuf[r][:, :], dacc[:, i * D:(i + 1) * D], 0.5, xr[r][:, :],
                            ALU.mult, ALU.add),
                               waits=[ld] + war.pop(("ybuf", r), []), incs=[(vec, 1)])
                        stt_tok = now(vec)
                        yb_guards = []
                        if dbg and half == 0 and tag == "f1" and i == 0:
                            P.emit("gpsimd", lambda g: g.dma_start(out=dbg_d["xs0"][128:256, :],
                                                                   in_=ybuf[0][:, :]),
                                   waits=[stt_tok], incs=[(dDbg, 16)])
                            yb_guards.append(now(dDbg))
                        war[("xr", r)] = [stt_tok] + tap_guards
                        P.emit("sync", lambda sy, i=i, r=r, t0=t0: sy.dma_start(
                            out=dst_dram[(t0 + i) * 128:(t0 + i + 1) * 128, :], in_=ybuf[r][:, :]),
                               waits=[stt_tok], incs=[(dYs[r], 16)])
                        war[("ybuf", r)] = [now(dYs[r])] + yb_guards
                    if dbg and half == 0 and tag == "f1":
                        P.emit("gpsimd", lambda g: g.dma_start(out=dbg_d["xs0"][:, :],
                                                               in_=dst_dram[0:HALF, :]),
                               waits=[now(dYs[0]), now(dYs[1])], incs=[(dDbg, 16)])
                        P.emit("gpsimd", lambda g: g.dma_start(out=dbg_d["dacc2"][:, :], in_=dacc[:, :]),
                               waits=[now(vec)], incs=[(dDbg, 16)])
                    war["dacc"] = [now(vec)]
                    war["yTh"] = [now(pe), now(vec)]
                    war["mTc"] = [now(dM[0]), now(dM[1])]
            engine_barrier()

        def attn_phase(src_dram, dst_dram):
            with ExitStack() as ph:
                psb = lambda n_, sh, dt: ph.enter_context(nc.sbuf_tensor(n_ + "_at", sh, dt))
                iota512 = psb("iota512", [128, 512], F32)
                wqkv_sb = psb("wqkv", [128, KC * 192], BF16)
                wout_sb = psb("wout", [64, D], BF16)
                zT = [psb(f"zT{i}", [128, KC * CHUNK], BF16) for i in range(2)]
                qkT = psb("qkT", [128, NB * SEQ], BF16)
                v_sb = psb("v_sb", [128, NTILES * 64], BF16)
                vI = psb("vI", [128, 16 * 128], BF16)
                qc_r = psb("qc_r", [1, 2 * SEQ], BF16)
                k_sb = psb("k_sb", [128, NTILES * 64], F32)
                Esb = [psb(f"Esb{i}", [128, 512], BF16) for i in range(4)]
                masks = psb("masks", [128, 4 * 512], BF16)
                ndbuf = psb("ndbuf", [128, SEQ], F32)
                rden = psb("rden", [64, SEQ], F32)
                oT = psb("oT", [64, SEQ], BF16)
                stg = [psb(f"stg{i}", [2, 512], F32) for i in range(4)]

                TPz = [PB[5][:, 0:128], PB[6][:, 0:128]]
                QK = PB[7]
                Vp = PB[4][:, 0:64]
                BQ = [PB[0], PB[1]]
                Kp = PB[4][:, 64:128]
                ND = [PB[5][0:2, :], PB[6][0:2, :]]
                OP = [PB[4], PB[7]]

                war = {}
                P.emit("gpsimd", lambda g: g.dma_start(
                    out=wqkv_sb[:, :].rearrange("p (k f) -> p k f", k=KC),
                    in_=wqkvT_d[:, :].rearrange("(k p) f -> p k f", p=128)), incs=[(dInit, 16)])
                P.emit("gpsimd", lambda g: g.dma_start(out=wout_sb[:, :], in_=woutT_d[:, :]),
                       incs=[(dInit, 16)])
                wqkv_tok = now(dInit)
                wout_tok = now(dInit)
                P.emit("gpsimd", lambda g: g.iota(iota512[:, :], pattern=[[1, 512]], base=0,
                                                  channel_multiplier=-1,
                                                  allow_small_or_imprecise_dtypes=True),
                       incs=[(gp, 1)])
                io5_tok = now(gp)
                for m in range(4):
                    P.emit("vector", lambda v, m=m: v.tensor_single_scalar(
                        masks[:, m * 512:(m + 1) * 512], iota512[:, :], float(128.0 * m), ALU.is_ge),
                           waits=[io5_tok], incs=[(vec, 1)])
                mask_tok = now(vec)

                for ch8 in range(8):
                    zr = ch8 % 2
                    last_evac = None
                    for j in range(4):
                        i = ch8 * 4 + j
                        r = i % 2
                        toks = emit_ln_tile(war, src_dram, i, r)
                        for k in range(KC):
                            tp = (i * KC + k) % 2
                            P.emit("tensor", lambda t, k=k, r=r, tp=tp:
                                   t.transpose(TPz[tp], ybuf[r][:, k * 128:(k + 1) * 128], ident[:, :]),
                                   waits=[toks["y"], ident_tok] + war.pop(("tpz", tp), []),
                                   incs=[(pe, 1)])
                            tp_tok = now(pe)
                            P.emit("scalar", lambda s, k=k, j=j, zr=zr, tp=tp:
                                   s.copy(zT[zr][:, k * CHUNK + j * 128: k * CHUNK + (j + 1) * 128],
                                          TPz[tp]),
                                   waits=[tp_tok] + (war.pop(("zslot", zr), [])
                                                     if (k == 0 and j == 0) else []),
                                   incs=[(act, 1)])
                            war[("tpz", tp)] = [now(act)]
                            last_evac = now(act)
                        war[("ybuf", r)] = [now(pe)]
                        war[("xr", r)] = [toks["y"], now(vec)]
                        vw = war.pop("vp", [])
                        for k in range(KC):
                            P.emit("tensor", lambda t, k=k, zr=zr, j=j:
                                   t.matmul(Vp, zT[zr][:, k * CHUNK + j * 128: k * CHUNK + (j + 1) * 128],
                                            wqkv_sb[:, k * 192 + 128: k * 192 + 192],
                                            start=(k == 0), stop=(k == KC - 1), skip_group_check=True),
                                   waits=([wqkv_tok, last_evac] + vw) if k == 0 else [],
                                   incs=[(pe, 1)] if k == KC - 1 else [])
                        vp_tok = now(pe)
                        P.emit("scalar", lambda s, i=i: s.copy(v_sb[:, i * 64:(i + 1) * 64], Vp),
                               waits=[vp_tok], incs=[(act, 1)])
                        vev = now(act)
                        kw = war.pop("kp", [])
                        for k in range(KC):
                            P.emit("tensor", lambda t, k=k, zr=zr, j=j:
                                   t.matmul(Kp, zT[zr][:, k * CHUNK + j * 128: k * CHUNK + (j + 1) * 128],
                                            wqkv_sb[:, k * 192 + 64: k * 192 + 128],
                                            start=(k == 0), stop=(k == KC - 1), skip_group_check=True),
                                   waits=([vev] + kw) if k == 0 else [],
                                   incs=[(pe, 1)] if k == KC - 1 else [])
                        kp_tok = now(pe)
                        P.emit("vector", lambda v, i=i: v.tensor_copy(k_sb[:, i * 64:(i + 1) * 64], Kp),
                               waits=[kp_tok], incs=[(vec, 1)])
                        war["kp"] = [now(vec)]
                        war["vp"] = [now(vec)]
                    qkw = war.pop("qk", [])
                    for k in range(KC):
                        P.emit("tensor", lambda t, k=k, zr=zr:
                               t.matmul(QK[:, :], wqkv_sb[:, k * 192: k * 192 + 128],
                                        zT[zr][:, k * CHUNK:(k + 1) * CHUNK],
                                        start=(k == 0), stop=(k == KC - 1), skip_group_check=True),
                               waits=([wqkv_tok, last_evac] + qkw) if k == 0 else [],
                               incs=[(pe, 1)] if k == KC - 1 else [])
                    qk_tok = now(pe)
                    b = ch8 // 4
                    pos = (ch8 % 4) * CHUNK
                    P.emit("scalar", lambda s, b=b, pos=pos: s.activation(
                        qkT[0:64, b * SEQ + pos: b * SEQ + pos + CHUNK], QK[0:64, :], AF.Identity,
                        scale=f32(0.125)),
                           waits=[qk_tok], incs=[(act, 1)])
                    war["qk"] = [now(act)]
                    war[("zslot", zr)] = [now(pe)]
                qkT_tok = now(act)
                vsb_tok = now(act)
                ksb_tok = now(vec)

                for b in range(NB):
                    P.emit("gpsimd", lambda g: g.memset(vI[:, :], 1.0),
                           waits=[vsb_tok] + war.pop("vI", []), incs=[(gp, 1)])
                    vi0 = now(gp)
                    for jb in range(16):
                        P.emit("vector", lambda v, b=b, jb=jb: v.tensor_copy(
                            vI[:, jb * 128:(jb + 1) * 128].rearrange("p (h two) -> p h two", two=2)[:, :, 0:1],
                            v_sb[:, (b * 16 + jb) * 64:(b * 16 + jb + 1) * 64].rearrange("p (h o) -> p h o", o=1)),
                               waits=[vi0] if jb == 0 else [], incs=[(vec, 1)])
                    vI_tok = now(vec)

                    def load_rows(h, slot, waits_, b=b):
                        P.emit("scalar", lambda s, h=h, b=b, slot=slot: s.dma_start(
                            out=qc_r[0:1, slot * SEQ:(slot + 1) * SEQ],
                            in_=qkT[h:h + 1, b * SEQ:(b + 1) * SEQ]),
                               waits=waits_ + [qkT_tok], incs=[(dR[slot], 16)])
                        return now(dR[slot])

                    rowtok = {0: load_rows(0, 0, war.pop(("rows", 0), []))}
                    rowtok[1] = load_rows(1, 1, war.pop(("rows", 1), []))
                    pending = None

                    def flush_pending(p):
                        jb, e_tok, ndb_, njb_, h_, xw_, ic_ev = p
                        es = jb % 4
                        P.emit("tensor", lambda t, h_=h_, jb=jb, es=es, ndb_=ndb_, njb_=njb_:
                               t.matmul(ND[ndb_],
                                        vI[:, jb * 128 + 2 * h_: jb * 128 + 2 * h_ + 2],
                                        Esb[es][:, :], start=(jb == 0), stop=(jb == njb_ - 1),
                                        skip_group_check=True),
                               waits=[vI_tok, e_tok] + xw_, incs=[(pe, 1)])
                        war[("E", es)] = [now(pe)]
                        if ic_ev is not None:
                            nd_tok = now(pe)
                            ic = ic_ev
                            sg = (h_ * 4 + ic) % 4
                            P.emit("vector", lambda v, ndb_=ndb_, sg=sg: v.tensor_copy(
                                stg[sg][:, :], ND[ndb_]),
                                   waits=[nd_tok] + war.pop(("stg", sg), []), incs=[(vec, 1)])
                            st_tok = now(vec)
                            war[("nd", ndb_)] = [st_tok]
                            P.emit("scalar", lambda s, h_=h_, ic=ic, sg=sg: s.dma_start(
                                out=ndbuf[h_:h_ + 1, ic * 512:(ic + 1) * 512], in_=stg[sg][0:1, :]),
                                   waits=[st_tok] + war.pop("hscat", []) +
                                         (war.pop("ndbuf", []) if (h_ == 0 and ic == 0) else []),
                                   incs=[(dG[sg], 16)])
                            P.emit("scalar", lambda s, h_=h_, ic=ic, sg=sg: s.dma_start(
                                out=ndbuf[64 + h_:65 + h_, ic * 512:(ic + 1) * 512],
                                in_=stg[sg][1:2, :]),
                                   waits=[], incs=[(dG[sg], 16)])
                            war[("stg", sg)] = [now(dG[sg])]
                            war["hscat"] = [now(dG[sg])]

                    for h in range(64):
                        slot = h % 2
                        rtok = rowtok.pop(h)
                        for ic in range(4):
                            ndb = ic % 2
                            ndw = war.pop(("nd", ndb), [])
                            njb = 4 * (ic + 1)
                            bqb = ic % 2
                            P.emit("tensor", lambda t, ic=ic, slot=slot, bqb=bqb:
                                   t.matmul(BQ[bqb][:, :], ones1[0:1, :],
                                            qc_r[0:1, slot * SEQ + ic * 512: slot * SEQ + (ic + 1) * 512],
                                            start=True, stop=True, skip_group_check=True),
                                   waits=[rtok, init_gp] + war.pop(("BQ", bqb), []), incs=[(pe, 1)])
                            bq_tok = now(pe)
                            for jb in range(njb):
                                es = jb % 4
                                ew = war.pop(("E", es), [])
                                ti_g = b * 16 + jb
                                P.emit("scalar", lambda s, es=es, bqb=bqb, ti_g=ti_g, h=h:
                                       s.activation(Esb[es][:, :], BQ[bqb][:, :], AF.Exp,
                                                    bias=f32(-2.0),
                                                    scale=k_sb[:, ti_g * 64 + h: ti_g * 64 + h + 1]),
                                       waits=[bq_tok, ksb_tok] + ew, incs=[(act, 1)])
                                e_tok = now(act)
                                if jb >= 4 * ic:
                                    m = jb - 4 * ic
                                    P.emit("vector", lambda v, es=es, m=m: v.tensor_tensor(
                                        Esb[es][:, :], Esb[es][:, :],
                                        masks[:, m * 512:(m + 1) * 512], ALU.mult),
                                           waits=[e_tok, mask_tok], incs=[(vec, 1)])
                                    e_tok = now(vec)
                                if pending is not None:
                                    flush_pending(pending)
                                pending = (jb, e_tok, ndb, njb, h,
                                           ndw if jb == 0 else [],
                                           ic if jb == njb - 1 else None)
                            war[("BQ", bqb)] = [now(act)]
                        if h + 2 < 64:
                            rowtok[h + 2] = load_rows(h + 2, slot, [now(pe)])
                    flush_pending(pending)
                    pending = None
                    scat_toks = [now(q) for q in dG]
                    P.emit("vector", lambda v: v.reciprocal(rden[:, :], ndbuf[64:128, :]),
                           waits=scat_toks + war.pop("rden", []), incs=[(vec, 1)])
                    P.emit("vector", lambda v: v.tensor_mul(oT[:, :], ndbuf[0:64, :], rden[:, :]),
                           waits=[now(vec)] + war.pop("oT", []), incs=[(vec, 1)])
                    oT_tok = now(vec)
                    war["ndbuf"] = [oT_tok]
                    war["rden"] = [oT_tok]
                    for j in range(16):
                        i = b * 16 + j
                        r = i % 2
                        optoks = []
                        for cc in range(2):
                            opw = war.pop(("OP", cc), [])
                            P.emit("tensor", lambda t, j=j, cc=cc:
                                   t.matmul(OP[cc][:, :], oT[:, j * 128:(j + 1) * 128],
                                            wout_sb[:, cc * 512:(cc + 1) * 512],
                                            start=True, stop=True, skip_group_check=True),
                                   waits=[oT_tok, wout_tok] + opw, incs=[(pe, 1)])
                            optoks.append(now(pe))
                        P.emit("sync", lambda sy, i=i, r=r: sy.dma_start(
                            out=xr[r][:, :], in_=src_dram[i * 128:(i + 1) * 128, :]),
                               waits=war.pop(("xr", r), []), incs=[(dXs[r], 16)])
                        ld = now(dXs[r])
                        ybw = war.pop(("ybuf", r), [])
                        for cc in range(2):
                            P.emit("vector", lambda v, r=r, cc=cc: v.tensor_tensor(
                                ybuf[r][:, cc * 512:(cc + 1) * 512], xr[r][:, cc * 512:(cc + 1) * 512],
                                OP[cc][:, :], ALU.add),
                                   waits=[ld, optoks[cc]] + (ybw if cc == 0 else []),
                                   incs=[(vec, 1)])
                            war[("OP", cc)] = [now(vec)]
                        stt_tok = now(vec)
                        war[("xr", r)] = [stt_tok]
                        P.emit("sync", lambda sy, i=i, r=r: sy.dma_start(
                            out=dst_dram[i * 128:(i + 1) * 128, :], in_=ybuf[r][:, :]),
                               waits=[stt_tok], incs=[(dYs[r], 16)])
                        war[("ybuf", r)] = [now(dYs[r])]
                    war["oT"] = [now(pe)]
                    war["vI"] = [now(pe)]
                    war[("rows", 0)] = [now(pe)]
                    war[("rows", 1)] = [now(pe)]
            engine_barrier()

        if "F1" in phases:
            ffn_phase(x_d, xs_d, "f1")
            a_src = xs_d
        else:
            a_src = x_d
        if "A" in phases:
            attn_phase(a_src, xs_d)
            f2_src = xs_d
        else:
            f2_src = a_src
        if "F2" in phases:
            ffn_phase(f2_src, out_d, "f2")
        else:
            prev = None
            for i in range(NTILES):
                P.emit("sync", lambda sy, i=i: sy.dma_start(out=xr[0][:, :],
                                                            in_=f2_src[i * 128:(i + 1) * 128, :]),
                       waits=[prev] if prev else [], incs=[(dXs[0], 16)])
                ld = now(dXs[0])
                P.emit("sync", lambda sy, i=i: sy.dma_start(out=out_d[i * 128:(i + 1) * 128, :],
                                                            in_=xr[0][:, :]),
                       waits=[ld], incs=[(dYs[0], 16)])
                prev = now(dYs[0])

        P.emit("sync", lambda sy: sy.nop(), waits=[now(dYs[0]), now(dYs[1])])

        with ExitStack() as sctx:
            for s in all_sems:
                s.h = sctx.enter_context(nc.semaphore(s.name))
            block = sctx.enter_context(nc.Block())
            block.tensor(lambda t: P.run("tensor", t))
            block.vector(lambda v: P.run("vector", v))
            block.scalar(lambda s: P.run("scalar", s))
            block.gpsimd(lambda g: P.run("gpsimd", g))
            block.sync(lambda sy: P.run("sync", sy))
    return nc


def kernel(L0_attn_x, L0_attn_w_qkv, L0_attn_w_out, L0_ffn_gate_w, L0_ffn_expert_bias,
           L0_ffn_w_up, L0_ffn_w_down, L0_ffn_shared_w_up, L0_ffn_shared_w_down):
    x = np.ascontiguousarray(np.asarray(L0_attn_x, dtype=np.float32))
    wqkvT = np.ascontiguousarray(np.asarray(L0_attn_w_qkv, np.float32).T)
    woutT = np.ascontiguousarray(np.asarray(L0_attn_w_out, np.float32).T)
    gwT = np.ascontiguousarray(np.asarray(L0_ffn_gate_w, np.float32).T)
    biasbc = np.ascontiguousarray(
        np.tile(np.asarray(L0_ffn_expert_bias, np.float32)[None, :], (128, 1)))
    up = np.asarray(L0_ffn_w_up, np.float32)
    dn = np.asarray(L0_ffn_w_down, np.float32)
    sup = np.asarray(L0_ffn_shared_w_up, np.float32)
    sdn = np.asarray(L0_ffn_shared_w_down, np.float32)
    upT = np.ascontiguousarray(np.concatenate([up.transpose(0, 2, 1), sup.T[None]], axis=0))
    dnT = np.ascontiguousarray(np.concatenate([dn.transpose(0, 2, 1), sdn.T[None]], axis=0))

    nc = build_nc()
    in_maps = []
    for c in range(NCORES):
        xs = np.ascontiguousarray(x[2 * c:2 * c + 2].reshape(T, D))
        in_maps.append(dict(x=xs, wqkvT=wqkvT, woutT=woutT, gwT=gwT, biasbc=biasbc,
                            upT=upT, dnT=dnT))
    res = run_bass_kernel_spmd(nc, in_maps, core_ids=list(range(NCORES)))
    out = np.concatenate([np.asarray(res.results[c]["out"]).reshape(2, SEQ, D)
                          for c in range(NCORES)], axis=0)
    return out.astype(np.float32)


# version 09
# speedup vs baseline: 1.6862x; 1.0053x over previous
import numpy as np
from contextlib import ExitStack

import concourse.bass as bass
import concourse.mybir as mybir
from concourse.bass_utils import run_bass_kernel_spmd

F32 = mybir.dt.float32
BF16 = mybir.dt.bfloat16
AF = mybir.ActivationFunctionType
ALU = mybir.AluOpType
AX = mybir.AxisListType
f32 = np.float32

NCORES = 8
D = 1024
KC = 8
HID = 1408
NH_A, NT_A = 640, 5
NH_B, NT_B = 768, 6
NE = 9
SEQ = 2048
NB = 2
T = NB * SEQ
NTILES = T // 128
HALF = 2048
HTILES = HALF // 128
CHUNK = 512
HCHUNKS = HALF // CHUNK


class Sem:
    __slots__ = ("name", "v", "h")

    def __init__(self, name):
        self.name, self.v, self.h = name, 0, None


class Plan:
    def __init__(self):
        self.ops = []

    def emit(self, engine, fn, waits=(), incs=()):
        ws = [(s, v) for (s, v) in waits if v > 0]
        ri = []
        for (s, k) in incs:
            s.v += k
            ri.append((s, k))
        self.ops.append((engine, ws, fn, ri))

    def run(self, engine_name, eng):
        for e, ws, fn, ri in self.ops:
            if e != engine_name:
                continue
            merged = {}
            for (s, v) in ws:
                if s.name in merged:
                    merged[s.name] = (s, max(v, merged[s.name][1]))
                else:
                    merged[s.name] = (s, v)
            for (s, v) in merged.values():
                eng.wait_ge(s.h, v)
            inst = fn(eng)
            for (s, k) in ri:
                inst.then_inc(s.h, k)


def build_nc(phases=("F1", "A", "F2"), dbg=False):
    nc = bass.Bass("TRN2", target_bir_lowering=False, debug=False)
    dbg_d = {}
    if dbg:
        dbg_d["yTh"] = nc.dram_tensor("dbg_yTh", [128, KC * HALF], F32, kind="ExternalOutput").ap()
        dbg_d["mTc"] = nc.dram_tensor("dbg_mTc", [128, 128], F32, kind="ExternalOutput").ap()
        dbg_d["dacc"] = nc.dram_tensor("dbg_dacc", [128, HTILES * D], F32, kind="ExternalOutput").ap()
        dbg_d["ye"] = nc.dram_tensor("dbg_ye", [128, KC * CHUNK], F32, kind="ExternalOutput").ap()
        dbg_d["xs0"] = nc.dram_tensor("dbg_xs0", [HALF, D], F32, kind="ExternalOutput").ap()
        dbg_d["dacc2"] = nc.dram_tensor("dbg_dacc2", [128, HTILES * D], F32, kind="ExternalOutput").ap()

    x_d = nc.dram_tensor("x", [T, D], F32, kind="ExternalInput").ap()
    wqkvT_d = nc.dram_tensor("wqkvT", [D, 192], F32, kind="ExternalInput").ap()
    woutT_d = nc.dram_tensor("woutT", [64, D], F32, kind="ExternalInput").ap()
    gwT_d = nc.dram_tensor("gwT", [D, 8], F32, kind="ExternalInput").ap()
    biasbc_d = nc.dram_tensor("biasbc", [128, 8], F32, kind="ExternalInput").ap()
    upT_d = nc.dram_tensor("upT", [NE, D, HID], F32, kind="ExternalInput").ap()
    dnT_d = nc.dram_tensor("dnT", [NE, HID, D], F32, kind="ExternalInput").ap()
    out_d = nc.dram_tensor("out", [T, D], F32, kind="ExternalOutput").ap()
    xs_d = nc.dram_tensor("xs", [T, D], F32, kind="Internal").ap()

    P = Plan()
    dInit = Sem("dInit")
    dWA = [Sem("dWA0"), Sem("dWA1")]
    dWB = Sem("dWB")
    dXs = [Sem("dX0"), Sem("dX1")]
    dYs = [Sem("dY0"), Sem("dY1")]
    dM = [Sem("dM0"), Sem("dM1")]
    dR = [Sem("dR0"), Sem("dR1")]
    dG = [Sem(f"dG{i}") for i in range(4)]
    dDbg = Sem("dDbg")
    pe, act, vec, gp, bar = Sem("pe"), Sem("act"), Sem("vec"), Sem("gp"), Sem("bar")
    all_sems = [dInit, dWA[0], dWA[1], dWB, dXs[0], dXs[1], dYs[0], dYs[1],
                dM[0], dM[1], dR[0], dR[1]] + dG + [dDbg, pe, act, vec, gp, bar]

    def now(s):
        return (s, s.v)

    with ExitStack() as top:
        sb = lambda n_, sh, dt: top.enter_context(nc.sbuf_tensor(n_, sh, dt))
        ident = sb("ident", [128, 128], F32)
        ones1 = sb("ones1", [1, 128], BF16)
        iota8 = sb("iota8", [128, 8], F32)
        biasb = sb("biasb", [128, 8], F32)
        gw_sb = sb("gw_sb", [128, KC * 8], F32)
        xr = [sb(f"xr{i}", [128, D], F32) for i in range(2)]
        ybuf = [sb(f"ybuf{i}", [128, D], F32) for i in range(2)]
        stt_ = [sb(f"stt{i}", [128, 16], F32) for i in range(2)]
        PBbig = top.enter_context(nc.psum_tensor("PBbig", [128, 2048], F32))
        PB = [PBbig[:, 512 * i:512 * (i + 1)] for i in range(4)] + \
             [top.enter_context(nc.psum_tensor(f"PB{i}", [128, 512], F32)) for i in range(4, 8)]

        P.emit("gpsimd", lambda g: g.iota(ident[:, :], pattern=[[1, 128]], base=0,
                                          channel_multiplier=-1, allow_small_or_imprecise_dtypes=True),
               incs=[(gp, 1)])
        P.emit("gpsimd", lambda g: g.iota(iota8[:, :], pattern=[[1, 8]], base=0,
                                          channel_multiplier=0, allow_small_or_imprecise_dtypes=True),
               incs=[(gp, 1)])
        P.emit("gpsimd", lambda g: g.memset(ones1[:, :], 1.0), incs=[(gp, 1)])
        P.emit("gpsimd", lambda g: g.dma_start(out=biasb[:, :], in_=biasbc_d[:, :]), incs=[(dInit, 16)])
        P.emit("gpsimd", lambda g: g.dma_start(
            out=gw_sb[:, :].rearrange("p (k e) -> p k e", k=KC),
            in_=gwT_d[:, :].rearrange("(k p) e -> p k e", p=128)), incs=[(dInit, 16)])
        init_gp = now(gp)
        init_dW = now(dInit)
        P.emit("vector", lambda v: v.tensor_single_scalar(ident[:, :], ident[:, :], 0.0, ALU.is_equal),
               waits=[init_gp], incs=[(vec, 1)])
        ident_tok = now(vec)

        def engine_barrier():
            target = bar.v + 5
            dma_fence = {"sync": [now(q) for q in dXs + dYs],
                         "gpsimd": [now(dInit), now(dWA[0]), now(dWA[1]), now(dWB), now(dDbg)],
                         "scalar": [now(q) for q in dM + dR + dG]}
            for e in ("tensor", "vector", "scalar", "gpsimd", "sync"):
                P.emit(e, lambda eng: eng.nop(), waits=dma_fence.get(e, []), incs=[(bar, 1)])
            for e in ("tensor", "vector", "scalar", "gpsimd", "sync"):
                P.emit(e, lambda eng: eng.nop(), waits=[(bar, target)])

        def emit_ln_tile(war, src_dram, ti, r):
            st = stt_[r]
            mu, var = st[:, 12:13], st[:, 13:14]
            rstd, nm = st[:, 14:15], st[:, 15:16]
            P.emit("sync", lambda sy: sy.dma_start(out=xr[r][:, :],
                                                   in_=src_dram[ti * 128:(ti + 1) * 128, :]),
                   waits=war.pop(("xr", r), []), incs=[(dXs[r], 16)])
            ld = now(dXs[r])
            stw = war.pop(("st", r), [])
            ybw = war.pop(("ybuf", r), [])
            P.emit("vector", lambda v: v.bn_stats(st[:, 0:6], xr[r][:, 0:512]),
                   waits=[ld] + stw, incs=[(vec, 1)])
            P.emit("vector", lambda v: v.bn_stats(st[:, 6:12], xr[r][:, 512:1024]),
                   waits=[now(vec)], incs=[(vec, 1)])
            P.emit("vector", lambda v: v.bn_aggr(st[:, 12:14], st[:, 0:12]),
                   waits=[now(vec)], incs=[(vec, 1)])
            agg_tok = now(vec)
            P.emit("scalar", lambda s: s.activation(rstd, var, AF.Sqrt, bias=f32(1e-5)),
                   waits=[agg_tok], incs=[(act, 1)])
            sqr_tok = now(act)
            P.emit("vector", lambda v: v.reciprocal(rstd, rstd), waits=[sqr_tok], incs=[(vec, 1)])
            rstd_tok = now(vec)
            P.emit("vector", lambda v: v.scalar_tensor_tensor(nm, mu, -1.0, rstd, ALU.mult, ALU.mult),
                   waits=[rstd_tok], incs=[(vec, 1)])
            nm_tok = now(vec)
            P.emit("scalar", lambda s: s.activation(ybuf[r][:, :], xr[r][:, :], AF.Identity,
                                                    bias=nm, scale=rstd),
                   waits=[rstd_tok, nm_tok] + ybw, incs=[(act, 1)])
            y_tok = now(act)
            war[("st", r)] = [y_tok]
            return {"ld": ld, "y": y_tok}

        def ffn_phase(src_dram, dst_dram, tag):
            with ExitStack() as ph:
                psb = lambda n_, sh, dt: ph.enter_context(nc.sbuf_tensor(n_ + tag, sh, dt))
                yTh = psb("yTh", [128, KC * HALF], BF16)
                mmF = psb("mmF", [128, 128], F32)
                mTcF = psb("mTcF", [128, 128], BF16)
                m1r = [psb(f"m1r{i}", [1, CHUNK], BF16) for i in range(2)]
                sc_t = psb("sc_t", [128, 8], F32)
                bb_t = psb("bb_t", [128, 8], F32)
                t1_t = psb("t1_t", [128, 8], F32)
                oh_t = psb("oh_t", [128, 8], F32)
                v1_t = psb("v1_t", [128, 1], F32)
                i1_t = psb("i1_t", [128, 1], F32)
                WupA = [psb(f"WupA{i}", [128, KC * NH_A], BF16) for i in range(2)]
                WdnA = [psb(f"WdnA{i}", [128, NT_A * D], BF16) for i in range(2)]
                WupB = psb("WupB", [128, KC * NH_B], BF16)
                WdnB = psb("WdnB", [128, NT_B * D], BF16)
                ye = [psb(f"ye{i}", [128, KC * CHUNK], BF16) for i in range(2)]
                hh = [psb("hhA", [128, NT_A * CHUNK], BF16), psb("hhB", [128, NT_B * CHUNK], BF16)]
                dacc = psb("dacc", [128, HTILES * D], F32)
                yTf32 = dacc

                U = [PB[0], PB[1]]
                DD = [PB[2], PB[3]]
                MB = PB[4]
                TPy = [PB[5][:, 0:128], PB[6][:, 0:128]]
                G8 = PB[7][:, 0:8]
                TPm = PB[4][0:128, 0:128]

                war = {}

                def load_piece(e, piece, waits_):
                    if piece == 0:
                        n0, nt, wu, wd = 0, NT_A, WupA[e % 2], WdnA[e % 2]
                        qsem = dWA[e % 2]
                    else:
                        n0, nt, wu, wd = NH_A, NT_B, WupB, WdnB
                        qsem = dWB
                    P.emit("gpsimd", lambda g, e=e, n0=n0, nt=nt, wu=wu: g.dma_start(
                        out=wu[:, 0:KC * nt * 128].rearrange("p (k n) -> p k n", k=KC),
                        in_=upT_d[e:e + 1, :, n0:n0 + nt * 128].rearrange("o (k p) n -> p (o k) n", p=128)),
                        waits=waits_, incs=[(qsem, 16)])
                    P.emit("gpsimd", lambda g, e=e, n0=n0, nt=nt, wd=wd: g.dma_start(
                        out=wd[:, 0:nt * D].rearrange("p (k c) -> p k c", k=nt),
                        in_=dnT_d[e:e + 1, n0:n0 + nt * 128, :].rearrange("o (k p) c -> p (o k) c", p=128)),
                        waits=[], incs=[(qsem, 16)])
                    return now(qsem)

                mTh_tok = [None]

                def load_m1(e, ch, slot, waits_):
                    P.emit("scalar", lambda s, e=e, ch=ch, slot=slot: s.dma_start(
                        out=m1r[slot][0:1, :],
                        in_=mTcF[e * 16 + 4 * ch: e * 16 + 4 * ch + 4, 0:128]),
                           waits=waits_ + [mTh_tok[0]], incs=[(dM[slot], 16)])
                    return now(dM[slot])

                for half in range(2):
                    t0 = half * HTILES
                    for i in range(HTILES):
                        r = i % 2
                        ci, jj = i // 4, i % 4
                        toks = emit_ln_tile(war, src_dram, t0 + i, r)
                        evac_toks = []
                        ytfw = war.pop("yTf32", []) + (war.pop("dacc", []) if i == 0 else [])
                        for k in range(KC):
                            tp = (i * KC + k) % 2
                            P.emit("tensor", lambda t, k=k, r=r, tp=tp:
                                   t.transpose(TPy[tp], ybuf[r][:, k * 128:(k + 1) * 128], ident[:, :]),
                                   waits=[toks["y"], ident_tok] + war.pop(("tpy", tp), []),
                                   incs=[(pe, 1)])
                            tp_tok = now(pe)
                            if k % 2 == 0:
                                P.emit("scalar", lambda s, k=k, tp=tp:
                                       s.copy(yTf32[:, k * 128:(k + 1) * 128], TPy[tp]),
                                       waits=[tp_tok] + (ytfw if k == 0 else []),
                                       incs=[(act, 1)])
                                ev = now(act)
                            else:
                                P.emit("vector", lambda v, k=k, tp=tp:
                                       v.tensor_copy(yTf32[:, k * 128:(k + 1) * 128], TPy[tp]),
                                       waits=[tp_tok] + (ytfw if k == 1 else []),
                                       incs=[(vec, 1)])
                                ev = now(vec)
                            evac_toks.append(ev)
                            war[("tpy", tp)] = [ev]
                            P.emit("gpsimd", lambda g, k=k, i=i:
                                   g.tensor_copy(yTh[:, k * HALF + i * 128: k * HALF + (i + 1) * 128],
                                                 yTf32[:, k * 128:(k + 1) * 128]),
                                   waits=[ev] + (war.pop("yTh", []) if (i == 0 and k == 0) else []),
                                   incs=[(gp, 1)])
                        war[("ybuf", r)] = [now(pe)]
                        war[("xr", r)] = [toks["y"], now(vec)]
                        g8w = war.pop("g8", [])
                        for k in range(KC):
                            P.emit("tensor", lambda t, k=k:
                                   t.matmul(G8, yTf32[:, k * 128:(k + 1) * 128],
                                            gw_sb[:, k * 8:(k + 1) * 8],
                                            start=(k == 0), stop=(k == KC - 1), skip_group_check=True),
                                   waits=([evac_toks[KC - 2], evac_toks[KC - 1], init_dW] + g8w)
                                         if k == 0 else [],
                                   incs=[(pe, 1)] if k == KC - 1 else [])
                        gate_tok = now(pe)
                        war["yTf32"] = [gate_tok, now(gp)]
                        P.emit("scalar", lambda s: s.activation(sc_t[:, :], G8, AF.Sigmoid),
                               waits=[gate_tok], incs=[(act, 1)])
                        sig_tok = now(act)
                        war["g8"] = [sig_tok]
                        rw = war.pop("route", [])
                        P.emit("vector", lambda v: v.tensor_add(bb_t[:, :], sc_t[:, :], biasb[:, :]),
                               waits=[sig_tok, init_dW] + rw, incs=[(vec, 1)])
                        P.emit("vector", lambda v: v.reduce_max(v1_t[:, :], bb_t[:, :], axis=AX.X),
                               waits=[now(vec)], incs=[(vec, 1)])
                        P.emit("vector", lambda v: v.tensor_scalar(t1_t[:, :], bb_t[:, :], v1_t[:, 0:1],
                                                                   -100.0, ALU.is_ge, ALU.mult),
                               waits=[now(vec)], incs=[(vec, 1)])
                        P.emit("vector", lambda v: v.tensor_tensor(t1_t[:, :], t1_t[:, :], iota8[:, :],
                                                                   ALU.add),
                               waits=[init_gp, now(vec)], incs=[(vec, 1)])
                        P.emit("vector", lambda v: v.tensor_reduce(i1_t[:, :], t1_t[:, :], axis=AX.X,
                                                                   op=ALU.min),
                               waits=[now(vec)], incs=[(vec, 1)])
                        P.emit("vector", lambda v: v.tensor_single_scalar(oh_t[:, :], t1_t[:, :],
                                                                          i1_t[:, 0:1], ALU.is_equal),
                               waits=[now(vec)], incs=[(vec, 1)])
                        P.emit("vector", lambda v: v.scalar_tensor_tensor(bb_t[:, :], oh_t[:, :], -1e30,
                                                                          bb_t[:, :], ALU.mult, ALU.add),
                               waits=[now(vec)], incs=[(vec, 1)])
                        P.emit("vector", lambda v: v.reduce_max(v1_t[:, :], bb_t[:, :], axis=AX.X),
                               waits=[now(vec)], incs=[(vec, 1)])
                        P.emit("vector", lambda v: v.tensor_scalar(t1_t[:, :], bb_t[:, :], v1_t[:, 0:1],
                                                                   -100.0, ALU.is_ge, ALU.mult),
                               waits=[now(vec)], incs=[(vec, 1)])
                        P.emit("vector", lambda v: v.tensor_tensor(t1_t[:, :], t1_t[:, :], iota8[:, :],
                                                                   ALU.add),
                               waits=[now(vec)], incs=[(vec, 1)])
                        P.emit("vector", lambda v: v.tensor_reduce(i1_t[:, :], t1_t[:, :], axis=AX.X,
                                                                   op=ALU.min),
                               waits=[now(vec)], incs=[(vec, 1)])
                        P.emit("vector", lambda v: v.tensor_single_scalar(t1_t[:, :], t1_t[:, :],
                                                                          i1_t[:, 0:1], ALU.is_equal),
                               waits=[now(vec)], incs=[(vec, 1)])
                        P.emit("vector", lambda v: v.tensor_add(oh_t[:, :], oh_t[:, :], t1_t[:, :]),
                               waits=[now(vec)], incs=[(vec, 1)])
                        P.emit("vector", lambda v, i=i: v.tensor_mul(mmF[:, i:i + 113:16],
                                                                     sc_t[:, :], oh_t[:, :]),
                               waits=[now(vec)] + (war.pop("mmF", []) if i == 0 else []),
                               incs=[(vec, 1)])
                        m_tok = now(vec)
                        war["route"] = [m_tok]
                        if i == HTILES - 1:
                            P.emit("tensor", lambda t: t.transpose(TPm, mmF[:, :], ident[:, :]),
                                   waits=[m_tok] + war.pop("tpm", []) + war.pop("mb", []),
                                   incs=[(pe, 1)])
                            tpm_tok = now(pe)
                            P.emit("scalar", lambda s: s.copy(mTcF[:, :], TPm),
                                   waits=[tpm_tok] + war.pop("mTc", []),
                                   incs=[(act, 1)])
                            mTh_tok[0] = now(act)
                            war["tpm"] = [mTh_tok[0]]
                            war["mmF"] = [tpm_tok]
                    ycast_all = now(gp)

                    ptokA = {0: load_piece(0, 0, war.pop(("WA", 0), []))}
                    ptokB = load_piece(0, 1, war.pop("WB", []))
                    ptokA[1] = load_piece(1, 0, war.pop(("WA", 1), []))
                    m1toks = {}
                    gctr = 0

                    ye_toks = {}

                    def emit_mbye(e_, ch_):
                        nonlocal gctr
                        g_ = gctr
                        gctr += 1
                        slot = g_ % 2
                        yp_ = ch_ % 2
                        m1toks[g_] = load_m1(e_, ch_, slot, war.pop(("m1", slot), []))
                        mbw = war.pop("mb", [])
                        if g_ == 0:
                            mbw = mbw + [mTh_tok[0]]
                        P.emit("tensor", lambda t, slot=slot:
                               t.matmul(MB[:, :], ones1[0:1, :], m1r[slot][0:1, :],
                                        start=True, stop=True, skip_group_check=True),
                               waits=[m1toks[g_], init_gp] + mbw, incs=[(pe, 1)])
                        mb_tok = now(pe)
                        war[("m1", slot)] = [mb_tok]
                        yew = war.pop(("ye", yp_), [])
                        for k in range(KC):
                            P.emit("vector", lambda v, k=k, ch_=ch_, yp_=yp_:
                                   v.tensor_tensor(ye[yp_][:, k * CHUNK:(k + 1) * CHUNK],
                                                   yTh[:, k * HALF + ch_ * CHUNK:
                                                       k * HALF + (ch_ + 1) * CHUNK],
                                                   MB[:, :], ALU.mult),
                                   waits=([mb_tok, ycast_all] + yew) if k == 0 else [],
                                   incs=[(vec, 1)] if k == KC - 1 else [])
                        ye_toks[(e_, ch_)] = now(vec)
                        war["mb"] = [now(vec)]

                    emit_mbye(0, 0)
                    for e in range(NE):
                        routed = e < 8
                        lastAuse = []
                        lastBuse = []
                        for ch in range(HCHUNKS):
                            yp = ch % 2
                            if routed:
                                ye_tok = ye_toks.pop((e, ch))
                            for piece in range(2):
                                if piece == 0:
                                    nt, wu, wd, wtok = NT_A, WupA[e % 2], WdnA[e % 2], ptokA[e]
                                else:
                                    nt, wu, wd, wtok = NT_B, WupB, WdnB, ptokB
                                hb = piece
                                sq_last = None
                                for ntile in range(nt):
                                    ub = ntile % 2
                                    uw = war.pop(("U", ub), [])
                                    for k in range(KC):
                                        if routed:
                                            movap = (lambda k=k, yp=yp:
                                                     ye[yp][:, k * CHUNK:(k + 1) * CHUNK])
                                            mov_tok = ye_tok
                                        else:
                                            movap = (lambda k=k, ch=ch:
                                                     yTh[:, k * HALF + ch * CHUNK:
                                                         k * HALF + (ch + 1) * CHUNK])
                                            mov_tok = ycast_all
                                        P.emit("tensor", lambda t, ntile=ntile, k=k, nt=nt, wu=wu,
                                               ub=ub, movap=movap:
                                               t.matmul(U[ub][:, :],
                                                        wu[:, (k * nt + ntile) * 128:
                                                           (k * nt + ntile + 1) * 128],
                                                        movap(), start=(k == 0), stop=(k == KC - 1),
                                                        skip_group_check=True),
                                               waits=([wtok, mov_tok] + uw) if k == 0 else [],
                                               incs=[(pe, 1)] if k == KC - 1 else [])
                                    u_tok = now(pe)
                                    P.emit("scalar", lambda s, ntile=ntile, ub=ub, hb=hb:
                                           s.activation(hh[hb][:, ntile * CHUNK:(ntile + 1) * CHUNK],
                                                        U[ub][:, :], AF.Relu),
                                           waits=[u_tok] + (war.pop(("hhh", hb), [])
                                                            if ntile == 0 else []),
                                           incs=[(act, 1)])
                                    relu_tok = now(act)
                                    war[("U", ub)] = [relu_tok]
                                    P.emit("vector", lambda v, ntile=ntile, hb=hb:
                                           v.tensor_mul(hh[hb][:, ntile * CHUNK:(ntile + 1) * CHUNK],
                                                        hh[hb][:, ntile * CHUNK:(ntile + 1) * CHUNK],
                                                        hh[hb][:, ntile * CHUNK:(ntile + 1) * CHUNK]),
                                           waits=[relu_tok], incs=[(vec, 1)])
                                    sq_last = now(vec)
                                for tt in range(4):
                                    for cc in range(2):
                                        db = (tt * 2 + cc) % 2
                                        dw_ = war.pop(("D", db), [])
                                        for ntile in range(nt):
                                            P.emit("tensor", lambda t, tt=tt, cc=cc, ntile=ntile,
                                                   nt=nt, wd=wd, hb=hb, db=db:
                                                   t.matmul(DD[db][:, :],
                                                            hh[hb][:, ntile * CHUNK + tt * 128:
                                                                   ntile * CHUNK + (tt + 1) * 128],
                                                            wd[:, ntile * D + cc * 512:
                                                               ntile * D + (cc + 1) * 512],
                                                            start=(ntile == 0), stop=(ntile == nt - 1),
                                                            skip_group_check=True),
                                                   waits=([sq_last] + dw_) if ntile == 0 else [],
                                                   incs=[(pe, 1)] if ntile == nt - 1 else [])
                                        d_tok = now(pe)
                                        dslc = (lambda ch=ch, tt=tt, cc=cc:
                                                dacc[:, (ch * 4 + tt) * D + cc * 512:
                                                     (ch * 4 + tt) * D + (cc + 1) * 512])
                                        dchain = war.pop(("dch", ch, tt, cc), [])
                                        if e == 0 and piece == 0:
                                            P.emit("vector", lambda v, dslc=dslc, db=db:
                                                   v.tensor_copy(dslc(), DD[db][:, :]),
                                                   waits=[d_tok] + ([ycast_all]
                                                                    if (ch == 0 and tt == 0 and cc == 0)
                                                                    else []),
                                                   incs=[(vec, 1)])
                                        else:
                                            P.emit("vector", lambda v, dslc=dslc, db=db:
                                                   v.tensor_tensor(dslc(), dslc(), DD[db][:, :],
                                                                   ALU.add),
                                                   waits=[d_tok] + dchain, incs=[(vec, 1)])
                                        war[("D", db)] = [now(vec)]
                                        war[("dch", ch, tt, cc)] = [now(vec)]
                                if piece == 0:
                                    lastAuse = [now(pe)]
                                    if ch < HCHUNKS - 1 and routed:
                                        emit_mbye(e, ch + 1)
                                    elif ch == HCHUNKS - 1 and routed and e + 1 < 8:
                                        emit_mbye(e + 1, 0)
                                else:
                                    lastBuse = [now(pe)]
                                war[("hhh", hb)] = [now(pe)]
                            if routed:
                                war[("ye", yp)] = [now(pe)]
                        war[("WA", e % 2)] = lastAuse
                        war["WB"] = lastBuse
                        if e + 2 < NE:
                            ptokA[e + 2] = load_piece(e + 2, 0, war.pop(("WA", e % 2)))
                        if e + 1 < NE:
                            ptokB = load_piece(e + 1, 1, war.pop("WB"))
                    dacc_done = now(vec)

                    if dbg and half == 0 and tag == "f1":
                        P.emit("gpsimd", lambda g: g.dma_start(out=dbg_d["yTh"][:, :], in_=yTh[:, :]),
                               waits=[dacc_done], incs=[(dDbg, 16)])
                        P.emit("gpsimd", lambda g: g.dma_start(out=dbg_d["mTc"][:, :], in_=mTcF[:, :]),
                               waits=[], incs=[(dDbg, 16)])
                        P.emit("gpsimd", lambda g: g.dma_start(out=dbg_d["dacc"][:, :], in_=dacc[:, :]),
                               waits=[], incs=[(dDbg, 16)])
                        P.emit("gpsimd", lambda g: g.dma_start(out=dbg_d["ye"][:, :], in_=ye[1][:, :]),
                               waits=[], incs=[(dDbg, 16)])

                    for i in range(HTILES):
                        r = i % 2
                        P.emit("sync", lambda sy, i=i, r=r, t0=t0: sy.dma_start(
                            out=xr[r][:, :], in_=src_dram[(t0 + i) * 128:(t0 + i + 1) * 128, :]),
                               waits=[dacc_done] + war.pop(("xr", r), []), incs=[(dXs[r], 16)])
                        ld = now(dXs[r])
                        tap_guards = []
                        if dbg and half == 0 and tag == "f1" and i == 0:
                            P.emit("gpsimd", lambda g: g.dma_start(out=dbg_d["xs0"][0:128, :],
                                                                   in_=xr[r][:, :]),
                                   waits=[ld], incs=[(dDbg, 16)])
                            tap_guards.append(now(dDbg))
                        P.emit("vector", lambda v, i=i, r=r: v.scalar_tensor_tensor(
                            ybuf[r][:, :], dacc[:, i * D:(i + 1) * D], 0.5, xr[r][:, :],
                            ALU.mult, ALU.add),
                               waits=[ld] + war.pop(("ybuf", r), []), incs=[(vec, 1)])
                        stt_tok = now(vec)
                        yb_guards = []
                        if dbg and half == 0 and tag == "f1" and i == 0:
                            P.emit("gpsimd", lambda g: g.dma_start(out=dbg_d["xs0"][128:256, :],
                                                                   in_=ybuf[0][:, :]),
                                   waits=[stt_tok], incs=[(dDbg, 16)])
                            yb_guards.append(now(dDbg))
                        war[("xr", r)] = [stt_tok] + tap_guards
                        P.emit("sync", lambda sy, i=i, r=r, t0=t0: sy.dma_start(
                            out=dst_dram[(t0 + i) * 128:(t0 + i + 1) * 128, :], in_=ybuf[r][:, :]),
                               waits=[stt_tok], incs=[(dYs[r], 16)])
                        war[("ybuf", r)] = [now(dYs[r])] + yb_guards
                    if dbg and half == 0 and tag == "f1":
                        P.emit("gpsimd", lambda g: g.dma_start(out=dbg_d["xs0"][:, :],
                                                               in_=dst_dram[0:HALF, :]),
                               waits=[now(dYs[0]), now(dYs[1])], incs=[(dDbg, 16)])
                        P.emit("gpsimd", lambda g: g.dma_start(out=dbg_d["dacc2"][:, :], in_=dacc[:, :]),
                               waits=[now(vec)], incs=[(dDbg, 16)])
                    war["dacc"] = [now(vec)]
                    war["yTh"] = [now(pe), now(vec)]
                    war["mTc"] = [now(dM[0]), now(dM[1])]
            engine_barrier()

        def attn_phase(src_dram, dst_dram):
            with ExitStack() as ph:
                psb = lambda n_, sh, dt: ph.enter_context(nc.sbuf_tensor(n_ + "_at", sh, dt))
                iota512 = psb("iota512", [128, 512], F32)
                wqkv_sb = psb("wqkv", [128, KC * 192], BF16)
                wout_sb = psb("wout", [64, D], BF16)
                zT = [psb(f"zT{i}", [128, KC * CHUNK], BF16) for i in range(2)]
                qkT = psb("qkT", [128, NB * SEQ], BF16)
                v_sb = psb("v_sb", [128, NTILES * 64], BF16)
                vI = psb("vI", [128, 16 * 128], BF16)
                qc_r = psb("qc_r", [1, 2 * SEQ], BF16)
                k_sb = psb("k_sb", [128, NTILES * 64], F32)
                Esb = [psb(f"Esb{i}", [128, 512], BF16) for i in range(8)]
                masks = psb("masks", [128, 4 * 512], BF16)
                ndbuf = psb("ndbuf", [128, SEQ], F32)
                rden = psb("rden", [64, SEQ], F32)
                oT = psb("oT", [64, SEQ], BF16)
                stg = [psb(f"stg{i}", [2, 512], F32) for i in range(4)]

                TPz = [PB[5][:, 0:128], PB[6][:, 0:128]]
                QK = PB[7]
                Vp = PB[4][:, 0:64]
                BQ = [PB[0], PB[1]]
                Kp = PB[2][:, 0:64]
                ND = [PB[5][0:2, :], PB[6][0:2, :]]
                OP = [PB[4], PB[7]]

                war = {}
                P.emit("gpsimd", lambda g: g.dma_start(
                    out=wqkv_sb[:, :].rearrange("p (k f) -> p k f", k=KC),
                    in_=wqkvT_d[:, :].rearrange("(k p) f -> p k f", p=128)), incs=[(dInit, 16)])
                P.emit("gpsimd", lambda g: g.dma_start(out=wout_sb[:, :], in_=woutT_d[:, :]),
                       incs=[(dInit, 16)])
                wqkv_tok = now(dInit)
                wout_tok = now(dInit)
                P.emit("gpsimd", lambda g: g.iota(iota512[:, :], pattern=[[1, 512]], base=0,
                                                  channel_multiplier=-1,
                                                  allow_small_or_imprecise_dtypes=True),
                       incs=[(gp, 1)])
                io5_tok = now(gp)
                for m in range(4):
                    P.emit("vector", lambda v, m=m: v.tensor_single_scalar(
                        masks[:, m * 512:(m + 1) * 512], iota512[:, :], float(128.0 * m), ALU.is_ge),
                           waits=[io5_tok], incs=[(vec, 1)])
                mask_tok = now(vec)

                for ch8 in range(8):
                    zr = ch8 % 2
                    last_evac = None
                    for j in range(4):
                        i = ch8 * 4 + j
                        r = i % 2
                        toks = emit_ln_tile(war, src_dram, i, r)
                        for k in range(KC):
                            tp = (i * KC + k) % 2
                            P.emit("tensor", lambda t, k=k, r=r, tp=tp:
                                   t.transpose(TPz[tp], ybuf[r][:, k * 128:(k + 1) * 128], ident[:, :]),
                                   waits=[toks["y"], ident_tok] + war.pop(("tpz", tp), []),
                                   incs=[(pe, 1)])
                            tp_tok = now(pe)
                            P.emit("scalar", lambda s, k=k, j=j, zr=zr, tp=tp:
                                   s.copy(zT[zr][:, k * CHUNK + j * 128: k * CHUNK + (j + 1) * 128],
                                          TPz[tp]),
                                   waits=[tp_tok] + (war.pop(("zslot", zr), [])
                                                     if (k == 0 and j == 0) else []),
                                   incs=[(act, 1)])
                            war[("tpz", tp)] = [now(act)]
                            last_evac = now(act)
                        war[("ybuf", r)] = [now(pe)]
                        war[("xr", r)] = [toks["y"], now(vec)]
                        vw = war.pop("vp", [])
                        for k in range(KC):
                            P.emit("tensor", lambda t, k=k, zr=zr, j=j:
                                   t.matmul(Vp, zT[zr][:, k * CHUNK + j * 128: k * CHUNK + (j + 1) * 128],
                                            wqkv_sb[:, k * 192 + 128: k * 192 + 192],
                                            start=(k == 0), stop=(k == KC - 1), skip_group_check=True),
                                   waits=([wqkv_tok, last_evac] + vw) if k == 0 else [],
                                   incs=[(pe, 1)] if k == KC - 1 else [])
                        vp_tok = now(pe)
                        P.emit("scalar", lambda s, i=i: s.copy(v_sb[:, i * 64:(i + 1) * 64], Vp),
                               waits=[vp_tok], incs=[(act, 1)])
                        vev = now(act)
                        kw = war.pop("kp", [])
                        for k in range(KC):
                            P.emit("tensor", lambda t, k=k, zr=zr, j=j:
                                   t.matmul(Kp, zT[zr][:, k * CHUNK + j * 128: k * CHUNK + (j + 1) * 128],
                                            wqkv_sb[:, k * 192 + 64: k * 192 + 128],
                                            start=(k == 0), stop=(k == KC - 1), skip_group_check=True),
                                   waits=kw if k == 0 else [],
                                   incs=[(pe, 1)] if k == KC - 1 else [])
                        kp_tok = now(pe)
                        P.emit("vector", lambda v, i=i: v.tensor_copy(k_sb[:, i * 64:(i + 1) * 64], Kp),
                               waits=[kp_tok], incs=[(vec, 1)])
                        war["kp"] = [now(vec)]
                        war["vp"] = [vev]
                    qkw = war.pop("qk", [])
                    for k in range(KC):
                        P.emit("tensor", lambda t, k=k, zr=zr:
                               t.matmul(QK[:, :], wqkv_sb[:, k * 192: k * 192 + 128],
                                        zT[zr][:, k * CHUNK:(k + 1) * CHUNK],
                                        start=(k == 0), stop=(k == KC - 1), skip_group_check=True),
                               waits=([wqkv_tok, last_evac] + qkw) if k == 0 else [],
                               incs=[(pe, 1)] if k == KC - 1 else [])
                    qk_tok = now(pe)
                    b = ch8 // 4
                    pos = (ch8 % 4) * CHUNK
                    P.emit("scalar", lambda s, b=b, pos=pos: s.activation(
                        qkT[0:64, b * SEQ + pos: b * SEQ + pos + CHUNK], QK[0:64, :], AF.Identity,
                        scale=f32(0.125)),
                           waits=[qk_tok], incs=[(act, 1)])
                    war["qk"] = [now(act)]
                    war[("zslot", zr)] = [now(pe)]
                qkT_tok = now(act)
                vsb_tok = now(act)
                ksb_tok = now(vec)

                for b in range(NB):
                    P.emit("gpsimd", lambda g: g.memset(vI[:, :], 1.0),
                           waits=[vsb_tok] + war.pop("vI", []), incs=[(gp, 1)])
                    vi0 = now(gp)
                    for jb in range(16):
                        P.emit("vector", lambda v, b=b, jb=jb: v.tensor_copy(
                            vI[:, jb * 128:(jb + 1) * 128].rearrange("p (h two) -> p h two", two=2)[:, :, 0:1],
                            v_sb[:, (b * 16 + jb) * 64:(b * 16 + jb + 1) * 64].rearrange("p (h o) -> p h o", o=1)),
                               waits=[vi0] if jb == 0 else [], incs=[(vec, 1)])
                    vI_tok = now(vec)

                    def load_rows(h, slot, waits_, b=b):
                        P.emit("scalar", lambda s, h=h, b=b, slot=slot: s.dma_start(
                            out=qc_r[0:1, slot * SEQ:(slot + 1) * SEQ],
                            in_=qkT[h:h + 1, b * SEQ:(b + 1) * SEQ]),
                               waits=waits_ + [qkT_tok], incs=[(dR[slot], 16)])
                        return now(dR[slot])

                    rowtok = {0: load_rows(0, 0, war.pop(("rows", 0), []))}
                    rowtok[1] = load_rows(1, 1, war.pop(("rows", 1), []))
                    war["vIfirst"] = [vI_tok]
                    pending = None

                    def flush_pending(p):
                        jb, e_tok, ndb_, njb_, h_, xw_, ic_ev = p
                        es = jb % 8
                        viw = war.pop("vIfirst", [])
                        P.emit("tensor", lambda t, h_=h_, jb=jb, es=es, ndb_=ndb_, njb_=njb_:
                               t.matmul(ND[ndb_],
                                        vI[:, jb * 128 + 2 * h_: jb * 128 + 2 * h_ + 2],
                                        Esb[es][:, :], start=(jb == 0), stop=(jb == njb_ - 1),
                                        skip_group_check=True),
                               waits=[e_tok] + viw + xw_, incs=[(pe, 1)])
                        war[("E", es)] = [now(pe)]
                        if ic_ev is not None:
                            nd_tok = now(pe)
                            ic = ic_ev
                            sg = (h_ * 4 + ic) % 4
                            P.emit("vector", lambda v, ndb_=ndb_, sg=sg: v.tensor_copy(
                                stg[sg][:, :], ND[ndb_]),
                                   waits=[nd_tok] + war.pop(("stg", sg), []), incs=[(vec, 1)])
                            st_tok = now(vec)
                            war[("nd", ndb_)] = [st_tok]
                            P.emit("scalar", lambda s, h_=h_, ic=ic, sg=sg: s.dma_start(
                                out=ndbuf[h_:h_ + 1, ic * 512:(ic + 1) * 512], in_=stg[sg][0:1, :]),
                                   waits=[st_tok] + war.pop("hscat", []) +
                                         (war.pop("ndbuf", []) if (h_ == 0 and ic == 0) else []),
                                   incs=[(dG[sg], 16)])
                            P.emit("scalar", lambda s, h_=h_, ic=ic, sg=sg: s.dma_start(
                                out=ndbuf[64 + h_:65 + h_, ic * 512:(ic + 1) * 512],
                                in_=stg[sg][1:2, :]),
                                   waits=[], incs=[(dG[sg], 16)])
                            war[("stg", sg)] = [now(dG[sg])]
                            war["hscat"] = [now(dG[sg])]

                    for h in range(64):
                        slot = h % 2
                        rtok = rowtok.pop(h)
                        for ic in range(4):
                            ndb = ic % 2
                            ndw = war.pop(("nd", ndb), [])
                            njb = 4 * (ic + 1)
                            bqb = ic % 2
                            P.emit("tensor", lambda t, ic=ic, slot=slot, bqb=bqb:
                                   t.matmul(BQ[bqb][:, :], ones1[0:1, :],
                                            qc_r[0:1, slot * SEQ + ic * 512: slot * SEQ + (ic + 1) * 512],
                                            start=True, stop=True, skip_group_check=True),
                                   waits=[rtok, init_gp] + war.pop(("BQ", bqb), []), incs=[(pe, 1)])
                            bq_tok = now(pe)
                            for jb in range(njb):
                                es = jb % 8
                                ew = war.pop(("E", es), [])
                                ti_g = b * 16 + jb
                                P.emit("scalar", lambda s, es=es, bqb=bqb, ti_g=ti_g, h=h:
                                       s.activation(Esb[es][:, :], BQ[bqb][:, :], AF.Exp,
                                                    bias=f32(-2.0),
                                                    scale=k_sb[:, ti_g * 64 + h: ti_g * 64 + h + 1]),
                                       waits=[bq_tok, ksb_tok] + ew, incs=[(act, 1)])
                                e_tok = now(act)
                                if jb >= 4 * ic:
                                    m = jb - 4 * ic
                                    P.emit("vector", lambda v, es=es, m=m: v.tensor_tensor(
                                        Esb[es][:, :], Esb[es][:, :],
                                        masks[:, m * 512:(m + 1) * 512], ALU.mult),
                                           waits=[e_tok, mask_tok], incs=[(vec, 1)])
                                    e_tok = now(vec)
                                if pending is not None:
                                    flush_pending(pending)
                                pending = (jb, e_tok, ndb, njb, h,
                                           ndw if jb == 0 else [],
                                           ic if jb == njb - 1 else None)
                            war[("BQ", bqb)] = [now(act)]
                        if h + 2 < 64:
                            rowtok[h + 2] = load_rows(h + 2, slot, [now(pe)])
                    flush_pending(pending)
                    pending = None
                    scat_toks = [now(q) for q in dG]
                    P.emit("vector", lambda v: v.reciprocal(rden[:, :], ndbuf[64:128, :]),
                           waits=scat_toks + war.pop("rden", []), incs=[(vec, 1)])
                    P.emit("vector", lambda v: v.tensor_mul(oT[:, :], ndbuf[0:64, :], rden[:, :]),
                           waits=[now(vec)] + war.pop("oT", []), incs=[(vec, 1)])
                    oT_tok = now(vec)
                    war["ndbuf"] = [oT_tok]
                    war["rden"] = [oT_tok]
                    for j in range(16):
                        i = b * 16 + j
                        r = i % 2
                        optoks = []
                        for cc in range(2):
                            opw = war.pop(("OP", cc), [])
                            P.emit("tensor", lambda t, j=j, cc=cc:
                                   t.matmul(OP[cc][:, :], oT[:, j * 128:(j + 1) * 128],
                                            wout_sb[:, cc * 512:(cc + 1) * 512],
                                            start=True, stop=True, skip_group_check=True),
                                   waits=[oT_tok, wout_tok] + opw, incs=[(pe, 1)])
                            optoks.append(now(pe))
                        P.emit("sync", lambda sy, i=i, r=r: sy.dma_start(
                            out=xr[r][:, :], in_=src_dram[i * 128:(i + 1) * 128, :]),
                               waits=war.pop(("xr", r), []), incs=[(dXs[r], 16)])
                        ld = now(dXs[r])
                        ybw = war.pop(("ybuf", r), [])
                        for cc in range(2):
                            P.emit("vector", lambda v, r=r, cc=cc: v.tensor_tensor(
                                ybuf[r][:, cc * 512:(cc + 1) * 512], xr[r][:, cc * 512:(cc + 1) * 512],
                                OP[cc][:, :], ALU.add),
                                   waits=[ld, optoks[cc]] + (ybw if cc == 0 else []),
                                   incs=[(vec, 1)])
                            war[("OP", cc)] = [now(vec)]
                        stt_tok = now(vec)
                        war[("xr", r)] = [stt_tok]
                        P.emit("sync", lambda sy, i=i, r=r: sy.dma_start(
                            out=dst_dram[i * 128:(i + 1) * 128, :], in_=ybuf[r][:, :]),
                               waits=[stt_tok], incs=[(dYs[r], 16)])
                        war[("ybuf", r)] = [now(dYs[r])]
                    war["oT"] = [now(pe)]
                    war["vI"] = [now(pe)]
                    war[("rows", 0)] = [now(pe)]
                    war[("rows", 1)] = [now(pe)]
            engine_barrier()

        if "F1" in phases:
            ffn_phase(x_d, xs_d, "f1")
            a_src = xs_d
        else:
            a_src = x_d
        if "A" in phases:
            attn_phase(a_src, xs_d)
            f2_src = xs_d
        else:
            f2_src = a_src
        if "F2" in phases:
            ffn_phase(f2_src, out_d, "f2")
        else:
            prev = None
            for i in range(NTILES):
                P.emit("sync", lambda sy, i=i: sy.dma_start(out=xr[0][:, :],
                                                            in_=f2_src[i * 128:(i + 1) * 128, :]),
                       waits=[prev] if prev else [], incs=[(dXs[0], 16)])
                ld = now(dXs[0])
                P.emit("sync", lambda sy, i=i: sy.dma_start(out=out_d[i * 128:(i + 1) * 128, :],
                                                            in_=xr[0][:, :]),
                       waits=[ld], incs=[(dYs[0], 16)])
                prev = now(dYs[0])

        P.emit("sync", lambda sy: sy.nop(), waits=[now(dYs[0]), now(dYs[1])])

        with ExitStack() as sctx:
            for s in all_sems:
                s.h = sctx.enter_context(nc.semaphore(s.name))
            block = sctx.enter_context(nc.Block())
            block.tensor(lambda t: P.run("tensor", t))
            block.vector(lambda v: P.run("vector", v))
            block.scalar(lambda s: P.run("scalar", s))
            block.gpsimd(lambda g: P.run("gpsimd", g))
            block.sync(lambda sy: P.run("sync", sy))
    return nc


def kernel(L0_attn_x, L0_attn_w_qkv, L0_attn_w_out, L0_ffn_gate_w, L0_ffn_expert_bias,
           L0_ffn_w_up, L0_ffn_w_down, L0_ffn_shared_w_up, L0_ffn_shared_w_down):
    x = np.ascontiguousarray(np.asarray(L0_attn_x, dtype=np.float32))
    wqkvT = np.ascontiguousarray(np.asarray(L0_attn_w_qkv, np.float32).T)
    woutT = np.ascontiguousarray(np.asarray(L0_attn_w_out, np.float32).T)
    gwT = np.ascontiguousarray(np.asarray(L0_ffn_gate_w, np.float32).T)
    biasbc = np.ascontiguousarray(
        np.tile(np.asarray(L0_ffn_expert_bias, np.float32)[None, :], (128, 1)))
    up = np.asarray(L0_ffn_w_up, np.float32)
    dn = np.asarray(L0_ffn_w_down, np.float32)
    sup = np.asarray(L0_ffn_shared_w_up, np.float32)
    sdn = np.asarray(L0_ffn_shared_w_down, np.float32)
    upT = np.ascontiguousarray(np.concatenate([up.transpose(0, 2, 1), sup.T[None]], axis=0))
    dnT = np.ascontiguousarray(np.concatenate([dn.transpose(0, 2, 1), sdn.T[None]], axis=0))

    nc = build_nc()
    in_maps = []
    for c in range(NCORES):
        xs = np.ascontiguousarray(x[2 * c:2 * c + 2].reshape(T, D))
        in_maps.append(dict(x=xs, wqkvT=wqkvT, woutT=woutT, gwT=gwT, biasbc=biasbc,
                            upT=upT, dnT=dnT))
    res = run_bass_kernel_spmd(nc, in_maps, core_ids=list(range(NCORES)))
    out = np.concatenate([np.asarray(res.results[c]["out"]).reshape(2, SEQ, D)
                          for c in range(NCORES)], axis=0)
    return out.astype(np.float32)
